# version 09
# baseline (speedup 1.0000x reference)
import os
os.environ.setdefault("GAUGE_NTFF_IGNORE_DMA", "1")
import numpy as np
import ml_dtypes
import concourse.bass as bass
import concourse.mybir as mybir
from concourse.bass_utils import run_bass_kernel_spmd
from contextlib import ExitStack

F32 = mybir.dt.float32
BF16 = mybir.dt.bfloat16
AF = mybir.ActivationFunctionType
ALU = mybir.AluOpType
AX = mybir.AxisListType
BF = ml_dtypes.bfloat16

B_LOC = 2
T = 2048
D = 2048
DI = 4096
DS = 16
DTR = 128
DCONV = 4
E0 = 6
R0 = 16
E1 = 8
I1 = 1408
LQ = 512
LKV = 512
NH = 16
HS = 128
NEG = -3.0e38
EPS_LN = 1e-5
RSQ_HS = 0.08838834764831845

WEIGHTS_F32 = {
    "L0_attn_in_proj_w": (2*DI, D), "L0_attn_conv_w": (DI, DCONV),
    "L0_attn_conv_b": (DI,), "L0_attn_x_proj_w": (DTR+2*DS, DI),
    "L0_attn_dt_proj_w": (DI, DTR), "L0_attn_dt_proj_b": (DI,),
    "L0_attn_A_log": (DI, DS), "L0_attn_D": (DI,),
    "L0_attn_out_proj_w": (D, DI),
    "L0_ffn_W_base": (D, D), "L0_ffn_b_base": (D,),
    "L0_ffn_A_s": (D, R0), "L0_ffn_B_s": (R0, D),
    "L0_ffn_A_r": (E0, D, R0), "L0_ffn_B_r": (E0, R0, D),
    "L0_ffn_W_router": (D, E0), "L0_ffn_b_router": (E0,),
    "L1_attn_Wdq": (LQ, D), "L1_attn_Wuq": (D, LQ), "L1_attn_Wdkv": (LKV, D),
    "L1_attn_Wuk": (D, LKV), "L1_attn_Wuv": (D, LKV), "L1_attn_Wo": (D, D),
    "L1_ffn_gate_w": (E1, D), "L1_ffn_w1": (E1, D, 2*I1),
    "L1_ffn_w2": (E1, I1, D), "L1_ffn_sw1": (I1, D), "L1_ffn_sw2": (I1, D),
    "L1_ffn_sw3": (D, I1),
}

SCRATCH = {
    "s_xn0":  ((B_LOC, T, D), BF16),
    "s_x1acc": ((B_LOC, T, D), F32),
    "s_gz":   ((B_LOC, DI, T), F32),
    "s_xcf":  ((B_LOC, DI, T), F32),
    "s_xcb":  ((B_LOC, DI, T), BF16),
    "s_bc":   ((B_LOC, 2*DS, T), BF16),
    "s_dtlo": ((B_LOC, DTR, T), BF16),
    "s_yg":   ((B_LOC, DI, T), BF16),
    "s_x1":   ((B_LOC, T, D), F32),
    "s_xn1":  ((B_LOC, T, D), BF16),
    "s_atn":  ((B_LOC, T, D), F32),
    "s_Aneg": ((DI, DS), F32),
    "wb_xproj_pad": ((256, DI), BF16),
    "wb_gate_wT": ((D, E1), BF16),
    "s_kfull": ((D, LKV), BF16),
    "s_vfull": ((LKV, D), BF16),
    "s_ckvf": ((B_LOC, LKV, T), BF16),
    "s_ckvt": ((B_LOC, T, LKV), BF16),
    "s_qd":   ((B_LOC, LQ, T), BF16),
    "s_p":    ((T, T), BF16),
    "wt_sw1": ((D, I1), BF16),
    "wt_sw2": ((D, I1), BF16),
    "wt_sw3": ((I1, D), BF16),
}

CAST_LISTS = {
    "ln0": [],
    "ffn0": ["L0_ffn_W_base", "L0_ffn_b_base", "L0_ffn_A_s", "L0_ffn_B_s",
             "L0_ffn_A_r", "L0_ffn_B_r", "L0_ffn_W_router", "L0_ffn_b_router"],
    "mamba_in": ["L0_attn_in_proj_w"],
    "mamba_scan": ["L0_attn_dt_proj_w"],
    "mamba_out": ["L0_attn_out_proj_w"],
    "ln1": [],
    "attn": ["L1_attn_Wdq", "L1_attn_Wuq", "L1_attn_Wdkv", "L1_attn_Wuk",
             "L1_attn_Wuv", "L1_attn_Wo"],
    "ffn1": ["L1_ffn_w1", "L1_ffn_w2", "L1_ffn_sw1", "L1_ffn_sw2",
             "L1_ffn_sw3"],
}


def cdiv(a, b):
    return (a + b - 1) // b


ENGINES = ("sync", "gpsimd", "scalar", "vector", "tensor")


class Sched:
    def __init__(self, nc, es, tag):
        self.nc = nc
        self.tag = tag
        self.ops = []
        self.per_eng = {e: [] for e in ENGINES}
        self.sems = {e: es.enter_context(nc.semaphore(f"{tag}_{e}")) for e in ENGINES}

    def add(self, eng, fn, deps=(), dma=False, self_edge=False):
        oid = len(self.ops)
        self.ops.append([eng, fn, tuple(deps), self_edge, 16 if dma else 1, 0])
        self.per_eng[eng].append(oid)
        return oid

    def emit(self, bar=None, bar_val=0):
        for e in ENGINES:
            c = 0
            for oid in self.per_eng[e]:
                c += self.ops[oid][4]
                self.ops[oid][5] = c
        finals = {e: (self.ops[self.per_eng[e][-1]][5] if self.per_eng[e] else 0)
                  for e in ENGINES}

        def engine_body(eng_name):
            def body(eng):
                if bar is not None and bar_val > 0:
                    eng.wait_ge(bar, bar_val)
                watermark = {e: 0 for e in ENGINES}
                for oid in self.per_eng[eng_name]:
                    _, fn, deps, self_edge, _, _ = self.ops[oid]
                    for d in deps:
                        de, dv = self.ops[d][0], self.ops[d][5]
                        if watermark[de] < dv:
                            eng.wait_ge(self.sems[de], dv)
                            watermark[de] = dv
                    inst = fn(eng)
                    inst.then_inc(self.sems[eng_name], self.ops[oid][4])
                if bar is not None:
                    eng.wait_ge(self.sems[eng_name], finals[eng_name])
                    eng.sem_inc(bar, 1)
            return body

        n_part = 0
        with self.nc.Block(self.tag) as blk:
            for e in ENGINES:
                if self.per_eng[e]:
                    getattr(blk, e)(engine_body(e))
                    n_part += 1
        return bar_val + n_part


def declare_dram(nc, names, host_inputs, outputs):
    aps = {}
    for n in sorted(set(names)):
        if n in WEIGHTS_F32:
            shape, dt = WEIGHTS_F32[n], F32
        elif n.startswith("wb_") and n[3:] in WEIGHTS_F32:
            shape, dt = WEIGHTS_F32[n[3:]], BF16
        elif n in SCRATCH:
            shape, dt = SCRATCH[n]
        elif n in ("x0", "out"):
            shape, dt = (B_LOC, T, D), F32
        elif n == "aux_ident":
            shape, dt = (128, 128), BF16
        elif n == "aux_ones":
            shape, dt = (1, 128), BF16
        elif n == "aux_mask":
            shape, dt = (128, 128), F32
        elif n == "aux_convb":
            shape, dt = (128, 32), F32
        elif n == "aux_convw":
            shape, dt = (128, 128), F32
        elif n == "aux_dtb":
            shape, dt = (128, 32), F32
        elif n == "aux_dvec":
            shape, dt = (128, 32), F32
        else:
            raise KeyError(n)
        kind = ("ExternalInput" if n in host_inputs else
                "ExternalOutput" if n in outputs else "Internal")
        aps[n] = nc.dram_tensor(n, list(shape), dt, kind=kind).ap()
    return aps


def cast_jobs(ap, names, FW=8192):
    jobs = []
    for n in names:
        shape = WEIGHTS_F32[n]
        src_, dst_ = ap[n], ap["wb_" + n]
        N = 1
        for d in shape:
            N *= d
        pat = " ".join(chr(97 + i) for i in range(len(shape)))
        s1 = src_.rearrange(f"{pat} -> ({pat})")
        d1 = dst_.rearrange(f"{pat} -> ({pat})")
        if N % 128:
            jobs.append((s1.rearrange("(o a) -> o a", o=1),
                         d1.rearrange("(o a) -> o a", o=1), 1, N))
            continue
        rem = N
        off = 0
        while rem:
            f = min(FW, rem // 128)
            take = 128 * f
            jobs.append((s1[off:off+take].rearrange("(p f) -> p f", p=128),
                         d1[off:off+take].rearrange("(p f) -> p f", p=128),
                         128, f))
            off += take
            rem -= take
    return jobs


def emit_cast_job(sch, bufs, prev_store, i, job):
    s2, d2, rr, cc = job
    nb = len(bufs)
    bi = i % nb
    deps = [prev_store[bi]] if prev_store[bi] is not None else []
    lo = sch.add("gpsimd",
                 (lambda g, s2=s2, rr=rr, cc=cc, bi=bi:
                  g.dma_start(out=bufs[bi][0:rr, 0:cc], in_=s2)),
                 deps=deps, dma=True)
    st = sch.add("sync",
                 (lambda s, d2=d2, rr=rr, cc=cc, bi=bi:
                  s.dma_start(out=d2, in_=bufs[bi][0:rr, 0:cc])),
                 deps=[lo], dma=True)
    prev_store[bi] = st
    return st


def ph_cast(nc, ap, names, tag="cast", bar=None, bar_val=0):
    jobs = cast_jobs(ap, names)
    if not jobs:
        return bar_val
    maxC = max(c for (_, _, _, c) in jobs)
    es = ExitStack()
    with es:
        bufs = [es.enter_context(nc.sbuf_tensor(f"{tag}_b{i}", [128, maxC], BF16))
                for i in range(4)]
        sch = Sched(nc, es, tag)
        prev_store = [None] * 4
        for i, job in enumerate(jobs):
            emit_cast_job(sch, bufs, prev_store, i, job)
        return sch.emit(bar, bar_val)


def ph_prep_mamba(nc, ap, bar=None, bar_val=0):
    es = ExitStack()
    with es:
        al = es.enter_context(nc.sbuf_tensor("pm_al", [128, DS * (DI // 128)], F32))
        an = es.enter_context(nc.sbuf_tensor("pm_an", [128, DS * (DI // 128)], F32))
        x1t = es.enter_context(nc.sbuf_tensor("pm_x1", [128, DI], BF16))
        x2t = es.enter_context(nc.sbuf_tensor("pm_x2", [96, DI], BF16))
        sch = Sched(nc, es, "prepm")
        nt = DI // 128
        lo = sch.add("sync", lambda s: s.dma_start(
            out=al[:].rearrange("p (n s) -> p n s", n=nt),
            in_=ap["L0_attn_A_log"].rearrange("(n p) s -> p n s", p=128)), dma=True)
        ex = sch.add("scalar", lambda sc: sc.activation(out=an[:], in_=al[:], func=AF.Exp),
                     deps=[lo])
        ng = sch.add("vector", lambda v: v.tensor_scalar_mul(an[:], an[:], -1.0), deps=[ex])
        sch.add("sync", lambda s: s.dma_start(
            out=ap["s_Aneg"].rearrange("(n p) s -> p n s", p=128),
            in_=an[:].rearrange("p (n s) -> p n s", n=nt)), deps=[ng], dma=True)
        c1 = sch.add("gpsimd", lambda g: g.dma_start(
            out=x1t[:], in_=ap["L0_attn_x_proj_w"][0:128, :]), dma=True)
        c2 = sch.add("gpsimd", lambda g: g.dma_start(
            out=x2t[0:32, :], in_=ap["L0_attn_x_proj_w"][128:160, :]), dma=True)
        z1a = sch.add("vector", lambda v: v.memset(x2t[32:64, :], 0.0))
        z1 = sch.add("vector", lambda v: v.memset(x2t[64:96, :], 0.0), deps=[z1a])
        sch.add("sync", lambda s: s.dma_start(out=ap["wb_xproj_pad"][0:128, :], in_=x1t[:]),
                deps=[c1], dma=True)
        sch.add("sync", lambda s: s.dma_start(out=ap["wb_xproj_pad"][128:224, :],
                                              in_=x2t[:]), deps=[c2, z1], dma=True)
        sch.add("sync", lambda s: s.dma_start(out=ap["wb_xproj_pad"][224:256, :],
                                              in_=x2t[32:64, :]), deps=[z1], dma=True)
        return sch.emit(bar, bar_val)


def ph_prep_gate(nc, ap, bar=None, bar_val=0):
    es = ExitStack()
    with es:
        gwb = es.enter_context(nc.sbuf_tensor("pg_gw", [E1, D], BF16))
        idn = es.enter_context(nc.sbuf_tensor("pg_id", [128, 128], BF16))
        gwt = es.enter_context(nc.sbuf_tensor("pg_gwt", [128, E1 * (D // 128)], BF16))
        pst = es.enter_context(nc.psum_tensor("pg_ps", [128, E1], BF16))
        sch = Sched(nc, es, "prepg")
        lg = sch.add("gpsimd", lambda g: g.dma_start(out=gwb[:], in_=ap["L1_ffn_gate_w"]),
                     dma=True)
        li = sch.add("sync", lambda s: s.dma_start(out=idn[:], in_=ap["aux_ident"]),
                     dma=True)
        prev_ev = None
        for c in range(D // 128):
            deps = [lg, li] + ([prev_ev] if prev_ev is not None else [])
            tr = sch.add("tensor", (lambda tn, c=c: tn.transpose(
                pst[:], in_=gwb[0:E1, c*128:(c+1)*128], identity=idn[0:E1, 0:E1])),
                deps=deps)
            ev = sch.add("vector", (lambda v, c=c: v.tensor_copy(
                out=gwt[:, c*E1:(c+1)*E1], in_=pst[:])), deps=[tr])
            prev_ev = ev
        sch.add("sync", lambda s: s.dma_start(
            out=ap["wb_gate_wT"].rearrange("(n p) e -> p n e", p=128),
            in_=gwt[:].rearrange("p (n e) -> p n e", e=E1)), deps=[prev_ev], dma=True)
        return sch.emit(bar, bar_val)


def ph_ln(nc, ap, src_name, dst_name, tag, bar=None, bar_val=0):
    es = ExitStack()
    with es:
        xt = [es.enter_context(nc.sbuf_tensor(f"{tag}_x{i}", [128, D], F32))
              for i in range(2)]
        xnb = [es.enter_context(nc.sbuf_tensor(f"{tag}_o{i}", [128, D], BF16))
               for i in range(2)]
        sq = es.enter_context(nc.sbuf_tensor(f"{tag}_sq", [128, D], F32))
        st1 = es.enter_context(nc.sbuf_tensor(f"{tag}_s1", [128, 8], F32))
        cst = es.enter_context(nc.sbuf_tensor(f"{tag}_cst", [128, 1], F32))
        sch = Sched(nc, es, tag)
        cs_op = sch.add("vector", lambda v: v.memset(cst[:], -0.5))
        src, dst = ap[src_name], ap[dst_name]
        prev_use = [None, None]
        prev_store = [None, None]
        for b in range(B_LOC):
            for it in range(T // 128):
                i = b * (T // 128) + it
                bi = i % 2
                deps = [prev_use[bi]] if prev_use[bi] is not None else []
                lo = sch.add("sync", (lambda s, b=b, it=it, bi=bi: s.dma_start(
                    out=xt[bi][:], in_=src[b, it*128:(it+1)*128, :])),
                    deps=deps, dma=True)
                rs = sch.add("vector", (lambda v, bi=bi: v.reduce_sum(
                    out=st1[:, 0:1], in_=xt[bi][:], axis=AX.X)), deps=[lo])
                ss = sch.add("scalar", (lambda sc, bi=bi: sc.activation(
                    out=sq[:], in_=xt[bi][:], func=AF.Square,
                    accum_out=st1[:, 1:2])), deps=[lo])
                nm = sch.add("vector", lambda v: v.tensor_scalar_mul(
                    st1[:, 2:3], st1[:, 0:1], -1.0 / D), deps=[rs])
                ms = sch.add("vector", lambda v: v.tensor_scalar_mul(
                    st1[:, 3:4], st1[:, 1:2], 1.0 / D), deps=[ss])
                m2 = sch.add("vector", lambda v: v.tensor_tensor(
                    out=st1[:, 4:5], in0=st1[:, 2:3], in1=st1[:, 2:3],
                    op=ALU.mult), deps=[nm])
                vr = sch.add("vector", lambda v: v.tensor_scalar(
                    out=st1[:, 5:6], in0=st1[:, 3:4], scalar1=st1[:, 4:5],
                    scalar2=EPS_LN, op0=ALU.subtract, op1=ALU.add),
                    deps=[ms, m2], self_edge=True)
                lv = sch.add("scalar", lambda sc: sc.activation(
                    out=st1[:, 6:7], in_=st1[:, 5:6], func=AF.Ln),
                    deps=[vr])
                rst = sch.add("scalar", lambda sc: sc.activation(
                    out=st1[:, 7:8], in_=st1[:, 6:7], func=AF.Exp,
                    scale=cst[:, 0:1]), deps=[lv, cs_op], self_edge=True)
                dep_s = [prev_store[bi]] if prev_store[bi] is not None else []
                ap_op = sch.add("vector", (lambda v, bi=bi: v.tensor_scalar(
                    out=xnb[bi][:], in0=xt[bi][:], scalar1=st1[:, 2:3],
                    scalar2=st1[:, 7:8], op0=ALU.add, op1=ALU.mult)),
                    deps=[rst, nm] + dep_s, self_edge=True)
                so = sch.add("sync", (lambda s, b=b, it=it, bi=bi: s.dma_start(
                    out=dst[b, it*128:(it+1)*128, :], in_=xnb[bi][:])),
                    deps=[ap_op], dma=True)
                prev_use[bi] = ap_op
                prev_store[bi] = so
        return sch.emit(bar, bar_val)


def ph_ffn0(nc, ap, bar=None, bar_val=0):
    NT = T // 128
    KC = D // 128
    es = ExitStack()
    with es:
        xnT = [es.enter_context(nc.sbuf_tensor(f"f0_xnT{c}", [128, T], BF16))
               for c in range(KC)]
        wb = [es.enter_context(nc.sbuf_tensor(f"f0_wb{c}", [128, D], BF16))
              for c in range(KC)]
        asb = es.enter_context(nc.sbuf_tensor("f0_as", [128, KC * R0], BF16))
        bsb = es.enter_context(nc.sbuf_tensor("f0_bs", [R0, D], BF16))
        wrb = es.enter_context(nc.sbuf_tensor("f0_wr", [128, KC * E0], BF16))
        arb = es.enter_context(nc.sbuf_tensor("f0_ar", [128, E0 * KC * R0], BF16))
        brb = es.enter_context(nc.sbuf_tensor("f0_br", [R0, E0 * D], BF16))
        bbr = es.enter_context(nc.sbuf_tensor("f0_bb", [1, D], BF16))
        brr = es.enter_context(nc.sbuf_tensor("f0_brr", [1, E0], BF16))
        ones = es.enter_context(nc.sbuf_tensor("f0_ones", [1, 128], BF16))
        idn = es.enter_context(nc.sbuf_tensor("f0_id", [128, 128], BF16))
        xtile = es.enter_context(nc.sbuf_tensor("f0_xt", [128, D], F32))
        otile = es.enter_context(nc.sbuf_tensor("f0_ot", [128, D], F32))
        s1b = es.enter_context(nc.sbuf_tensor("f0_s1", [128, R0], BF16))
        s1T = es.enter_context(nc.sbuf_tensor("f0_s1T", [R0, 128], BF16))
        gsb = es.enter_context(nc.sbuf_tensor("f0_g", [128, 16], F32))
        st = es.enter_context(nc.sbuf_tensor("f0_st", [128, 8], F32))
        g2 = es.enter_context(nc.sbuf_tensor("f0_g2", [128, E0], F32))
        keep = es.enter_context(nc.sbuf_tensor("f0_keep", [128, E0], F32))
        w6 = es.enter_context(nc.sbuf_tensor("f0_w6", [128, E0], F32))
        hwb = es.enter_context(nc.sbuf_tensor("f0_hw", [128, E0 * R0], BF16))
        hwT = es.enter_context(nc.sbuf_tensor("f0_hwT", [R0, E0 * 128], BF16))
        ps_main = es.enter_context(nc.psum_tensor("f0_pm", [128, D], F32))
        ps_s1 = es.enter_context(nc.psum_tensor("f0_ps1", [128, R0], F32))
        ps_rt = es.enter_context(nc.psum_tensor("f0_prt", [128, E0], F32))
        ps_h6 = es.enter_context(nc.psum_tensor("f0_ph6", [128, E0 * R0], F32))
        ps_tr = es.enter_context(nc.psum_tensor("f0_ptr", [R0, 128], BF16))
        sch = Sched(nc, es, "ffn0")

        wl = []
        for c in range(KC):
            wl.append(sch.add("sync", (lambda s, c=c: s.dma_start(
                out=wb[c][:], in_=ap["wb_L0_ffn_W_base"][c*128:(c+1)*128, :])), dma=True))
        wl.append(sch.add("sync", lambda s: s.dma_start(
            out=asb[:].rearrange("p (n r) -> p n r", r=R0),
            in_=ap["wb_L0_ffn_A_s"].rearrange("(n p) r -> p n r", p=128)), dma=True))
        wl.append(sch.add("sync", lambda s: s.dma_start(
            out=bsb[:], in_=ap["wb_L0_ffn_B_s"]), dma=True))
        wl.append(sch.add("sync", lambda s: s.dma_start(
            out=wrb[:].rearrange("p (n e) -> p n e", e=E0),
            in_=ap["wb_L0_ffn_W_router"].rearrange("(n p) e -> p n e", p=128)), dma=True))
        wl.append(sch.add("sync", lambda s: s.dma_start(
            out=arb[:].rearrange("p (e n r) -> p e n r", e=E0, r=R0),
            in_=ap["wb_L0_ffn_A_r"].rearrange("e (n p) r -> p e n r", p=128)), dma=True))
        wl.append(sch.add("sync", lambda s: s.dma_start(
            out=brb[:].rearrange("r (e c) -> r e c", e=E0),
            in_=ap["wb_L0_ffn_B_r"].rearrange("e r c -> r e c")), dma=True))
        wl.append(sch.add("sync", lambda s: s.dma_start(
            out=bbr[:], in_=ap["wb_L0_ffn_b_base"].rearrange("(o a) -> o a", o=1)), dma=True))
        wl.append(sch.add("sync", lambda s: s.dma_start(
            out=brr[:], in_=ap["wb_L0_ffn_b_router"].rearrange("(o a) -> o a", o=1)), dma=True))
        wl.append(sch.add("sync", lambda s: s.dma_start(out=ones[:], in_=ap["aux_ones"]),
                          dma=True))
        wl.append(sch.add("sync", lambda s: s.dma_start(out=idn[:], in_=ap["aux_ident"]),
                          dma=True))

        prev_evac = None
        xnT_load = [None] * KC
        last_xnT_user = None
        last_s1_read = None
        last_rt_read = None
        last_h6_read = [None] * E0
        last_tr_read = None
        for b in range(B_LOC):
            for c in range(KC):
                deps = [] if xnT_load[c] is None else [xnT_load[c]]
                if last_xnT_user is not None:
                    deps.append(last_xnT_user)
                xnT_load[c] = sch.add("scalar", (lambda s, b=b, c=c: s.dma_start_transpose(
                    xnT[c][:], in_=ap["s_xn0"][b, 0:T, c*128:(c+1)*128])),
                    deps=deps, dma=True)
            loads_b = list(xnT_load)
            for tt in range(NT):
                pm_dep = [prev_evac] if prev_evac is not None else []
                def f_base(tn, b=b, tt=tt):
                    for nchunk in range(4):
                        for c in range(KC):
                            tn.matmul(ps_main[:, nchunk*512:(nchunk+1)*512],
                                      lhsT=xnT[c][:, tt*128:(tt+1)*128],
                                      rhs=wb[c][:, nchunk*512:(nchunk+1)*512],
                                      start=(c == 0), stop=False)
                        last = tn.matmul(ps_main[:, nchunk*512:(nchunk+1)*512],
                                         lhsT=ones[:, 0:128],
                                         rhs=bbr[:, nchunk*512:(nchunk+1)*512],
                                         start=False, stop=False)
                    return last
                base = sch.add("tensor", f_base, deps=wl + loads_b + pm_dep)
                def f_s1(tn, tt=tt):
                    for c in range(KC):
                        last = tn.matmul(ps_s1[:], lhsT=xnT[c][:, tt*128:(tt+1)*128],
                                         rhs=asb[:, c*R0:(c+1)*R0],
                                         start=(c == 0), stop=(c == KC-1))
                    return last
                s1mm = sch.add("tensor", f_s1, deps=wl + loads_b +
                               ([last_s1_read] if last_s1_read else []))
                s1ev = sch.add("vector", lambda v: v.tensor_copy(out=s1b[:], in_=ps_s1[:]),
                               deps=[s1mm])
                last_s1_read = s1ev
                s1tr = sch.add("tensor", lambda tn: tn.transpose(
                    ps_tr[0:R0, :], in_=s1b[:], identity=idn[:]),
                    deps=[s1ev] + ([last_tr_read] if last_tr_read else []))
                s1tc = sch.add("vector", lambda v: v.tensor_copy(out=s1T[:], in_=ps_tr[0:R0, :]),
                               deps=[s1tr])
                def f_s2(tn):
                    for nchunk in range(4):
                        last = tn.matmul(ps_main[:, nchunk*512:(nchunk+1)*512],
                                         lhsT=s1T[0:R0, :],
                                         rhs=bsb[0:R0, nchunk*512:(nchunk+1)*512],
                                         start=False, stop=False)
                    return last
                s2mm = sch.add("tensor", f_s2, deps=[s1tc, base])
                def f_rt(tn, tt=tt):
                    for c in range(KC):
                        tn.matmul(ps_rt[:], lhsT=xnT[c][:, tt*128:(tt+1)*128],
                                  rhs=wrb[:, c*E0:(c+1)*E0], start=(c == 0), stop=False)
                    return tn.matmul(ps_rt[:], lhsT=ones[:, 0:128], rhs=brr[:],
                                     start=False, stop=True)
                rtmm = sch.add("tensor", f_rt, deps=wl + loads_b +
                               ([last_rt_read] if last_rt_read else []))
                mx = sch.add("vector", lambda v: v.reduce_max(
                    out=st[:, 0:1], in_=ps_rt[:], axis=AX.X, negate=True), deps=[rtmm])
                ex = sch.add("scalar", lambda sc: sc.activation(
                    out=gsb[:, 0:E0], in_=ps_rt[:], func=AF.Exp, bias=st[:, 0:1],
                    accum_out=st[:, 1:2]), deps=[mx])
                last_rt_read = ex
                rc = sch.add("vector", lambda v: v.reciprocal(
                    out=st[:, 2:3], in_=st[:, 1:2]), deps=[ex])
                gg = sch.add("vector", lambda v: v.tensor_scalar_mul(
                    gsb[:, 0:E0], gsb[:, 0:E0], st[:, 2:3]), deps=[rc, ex],
                    self_edge=True)
                m1 = sch.add("vector", lambda v: v.reduce_max(
                    out=st[:, 3:4], in_=gsb[:, 0:E0], axis=AX.X), deps=[gg])
                e1_ = sch.add("vector", lambda v: v.tensor_scalar(
                    out=g2[:], in0=gsb[:, 0:E0], scalar1=st[:, 3:4], scalar2=-1e30,
                    op0=is_ge_op(), op1=ALU.mult), deps=[m1], self_edge=True)
                g2s = sch.add("vector", lambda v: v.tensor_tensor(
                    out=g2[:], in0=gsb[:, 0:E0], in1=g2[:], op=ALU.add), deps=[e1_])
                m2 = sch.add("vector", lambda v: v.reduce_max(
                    out=st[:, 4:5], in_=g2[:], axis=AX.X), deps=[g2s])
                kp = sch.add("vector", lambda v: v.tensor_single_scalar(
                    out=keep[:], in_=gsb[:, 0:E0], scalar=st[:, 4:5], op=is_ge_op()),
                    deps=[m2], self_edge=True)
                w6op = sch.add("vector", lambda v: v.tensor_tensor(
                    out=w6[:], in0=gsb[:, 0:E0], in1=keep[:], op=ALU.mult), deps=[kp])
                def f_h6(tn, tt=tt):
                    for e in range(E0):
                        for c in range(KC):
                            last = tn.matmul(
                                ps_h6[:, e*R0:(e+1)*R0],
                                lhsT=xnT[c][:, tt*128:(tt+1)*128],
                                rhs=arb[:, (e*KC+c)*R0:(e*KC+c+1)*R0],
                                start=(c == 0), stop=(c == KC-1))
                    return last
                h6mm = sch.add("tensor", f_h6, deps=wl + loads_b +
                               [d for d in last_h6_read if d is not None])
                hw_ops = []
                for e in range(E0):
                    hw_ops.append(sch.add("vector", (lambda v, e=e: v.tensor_scalar_mul(
                        hwb[:, e*R0:(e+1)*R0], ps_h6[:, e*R0:(e+1)*R0],
                        w6[:, e:e+1])), deps=[h6mm, w6op], self_edge=True))
                last_h6_read = list(hw_ops)
                prev_trc = s1tc
                tr_ops = []
                for e in range(E0):
                    tre = sch.add("tensor", (lambda tn, e=e: tn.transpose(
                        ps_tr[0:R0, :], in_=hwb[:, e*R0:(e+1)*R0], identity=idn[:])),
                        deps=[hw_ops[e], prev_trc])
                    trc = sch.add("vector", (lambda v, e=e: v.tensor_copy(
                        out=hwT[:, e*128:(e+1)*128], in_=ps_tr[0:R0, :])), deps=[tre])
                    prev_trc = trc
                    tr_ops.append(trc)
                last_tr_read = prev_trc
                def f_rout(tn):
                    for e in range(E0):
                        for nchunk in range(4):
                            last = tn.matmul(
                                ps_main[:, nchunk*512:(nchunk+1)*512],
                                lhsT=hwT[0:R0, e*128:(e+1)*128],
                                rhs=brb[0:R0, e*D+nchunk*512:e*D+(nchunk+1)*512],
                                start=False, stop=(e == E0-1))
                    return last
                rout = sch.add("tensor", f_rout, deps=tr_ops + [s2mm])
                xl = sch.add("sync", (lambda s, b=b, tt=tt: s.dma_start(
                    out=xtile[:], in_=ap["x0"][b, tt*128:(tt+1)*128, :])),
                    deps=pm_dep, dma=True)
                ev = sch.add("vector", lambda v: v.tensor_tensor(
                    out=otile[:], in0=ps_main[:], in1=xtile[:], op=ALU.add),
                    deps=[rout, xl])
                sout = sch.add("sync", (lambda s, b=b, tt=tt: s.dma_start(
                    out=ap["s_x1acc"][b, tt*128:(tt+1)*128, :], in_=otile[:])),
                    deps=[ev], dma=True)
                prev_evac = sout
                last_xnT_user = h6mm
        return sch.emit(bar, bar_val)


def is_ge_op():
    return ALU.is_ge


def ph_mamba_in(nc, ap, bar=None, bar_val=0):
    KC = D // 128
    es = ExitStack()
    with es:
        xnT = [es.enter_context(nc.sbuf_tensor(f"mi_xnT{c}", [128, T], BF16))
               for c in range(KC)]
        wT = [[es.enter_context(nc.sbuf_tensor(f"mi_wT{s}_{c}", [128, 512], BF16))
               for c in range(KC)] for s in range(2)]
        cwsb = es.enter_context(nc.sbuf_tensor("mi_cw", [128, 32 * DCONV], F32))
        cbsb = es.enter_context(nc.sbuf_tensor("mi_cb", [128, 32], F32))
        xr = es.enter_context(nc.sbuf_tensor("mi_xr", [128, T], F32))
        cv = es.enter_context(nc.sbuf_tensor("mi_cv", [128, T], F32))
        xcf = es.enter_context(nc.sbuf_tensor("mi_xcf", [128, T], F32))
        xcb = es.enter_context(nc.sbuf_tensor("mi_xcb", [128, T], BF16))
        gzt = es.enter_context(nc.sbuf_tensor("mi_gz", [128, T], F32))
        xpj = es.enter_context(nc.sbuf_tensor("mi_xpj", [128, 256], BF16))
        xcbl = es.enter_context(nc.sbuf_tensor("mi_xcbl", [128, T], BF16))
        dtb = es.enter_context(nc.sbuf_tensor("mi_dtb", [128, T], BF16))
        bcb = es.enter_context(nc.sbuf_tensor("mi_bcb", [32, T], BF16))
        ps = [es.enter_context(nc.psum_tensor(f"mi_ps{i}", [128, T], F32))
              for i in range(2)]
        sch = Sched(nc, es, "mbin")

        cw_l = sch.add("sync", lambda s: s.dma_start(
            out=cwsb[:], in_=ap["aux_convw"]), dma=True)
        cb_l = sch.add("sync", lambda s: s.dma_start(
            out=cbsb[:], in_=ap["aux_convb"]), dma=True)

        xnT_load = [None] * KC
        last_xnT_user = None
        wT_load = [[None] * KC for _ in range(2)]
        last_wT_user = [None, None]
        last_ps_reader = [None, None]
        last_xr_reader = None
        last_cv_reader = None
        last_xcf_store = None
        last_xcb_store = None
        last_gz_store = None
        for b in range(B_LOC):
            for c in range(KC):
                deps = ([xnT_load[c]] if xnT_load[c] else []) + \
                       ([last_xnT_user] if last_xnT_user else [])
                xnT_load[c] = sch.add("scalar", (lambda s, b=b, c=c: s.dma_start_transpose(
                    xnT[c][:], in_=ap["s_xn0"][b, 0:T, c*128:(c+1)*128])),
                    deps=deps, dma=True)
            loads_b = list(xnT_load)
            for m in range(16):
                mset = m % 2
                for c in range(KC):
                    deps = ([wT_load[mset][c]] if wT_load[mset][c] else []) + \
                           ([last_wT_user[mset]] if last_wT_user[mset] else [])
                    wT_load[mset][c] = sch.add("scalar",
                        (lambda s, m=m, c=c, mset=mset: s.dma_start_transpose(
                            wT[mset][c][:],
                            in_=ap["wb_L0_attn_in_proj_w"][m*512:(m+1)*512,
                                                           c*128:(c+1)*128])),
                        deps=deps, dma=True)
                wloads = list(wT_load[mset])
                for et in range(4):
                    g = m * 4 + et
                    pi = g % 2
                    def f_mm(tn, et=et, mset=mset, pi=pi):
                        for nch in range(4):
                            for c in range(KC):
                                last = tn.matmul(
                                    ps[pi][:, nch*512:(nch+1)*512],
                                    lhsT=wT[mset][c][:, et*128:(et+1)*128],
                                    rhs=xnT[c][:, nch*512:(nch+1)*512],
                                    start=(c == 0), stop=(c == KC-1))
                        return last
                    mm = sch.add("tensor", f_mm, deps=wloads + loads_b +
                                 ([last_ps_reader[pi]] if last_ps_reader[pi] else []))
                    last_xnT_user = mm
                    last_wT_user[mset] = mm
                    if g < 32:
                        cp = sch.add("scalar", (lambda sc, pi=pi: sc.copy(
                            out=xr[:], in_=ps[pi][:])),
                            deps=[mm] + ([last_xr_reader] if last_xr_reader else []))
                        last_ps_reader[pi] = cp
                        c3 = sch.add("vector", (lambda v, g=g: v.tensor_scalar_mul(
                            cv[:], xr[:], cwsb[:, g*DCONV+3:g*DCONV+4])),
                            deps=[cp, cw_l] +
                                 ([last_cv_reader] if last_cv_reader else []))
                        c2 = sch.add("vector", (lambda v, g=g: v.scalar_tensor_tensor(
                            out=cv[:, 1:T], in0=xr[:, 0:T-1],
                            scalar=cwsb[:, g*DCONV+2:g*DCONV+3], in1=cv[:, 1:T],
                            op0=ALU.mult, op1=ALU.add)), deps=[c3])
                        c1 = sch.add("vector", (lambda v, g=g: v.scalar_tensor_tensor(
                            out=cv[:, 2:T], in0=xr[:, 0:T-2],
                            scalar=cwsb[:, g*DCONV+1:g*DCONV+2], in1=cv[:, 2:T],
                            op0=ALU.mult, op1=ALU.add)), deps=[c2])
                        c0 = sch.add("vector", (lambda v, g=g: v.scalar_tensor_tensor(
                            out=cv[:, 3:T], in0=xr[:, 0:T-3],
                            scalar=cwsb[:, g*DCONV+0:g*DCONV+1], in1=cv[:, 3:T],
                            op0=ALU.mult, op1=ALU.add)), deps=[c1])
                        last_xr_reader = c0
                        sf = sch.add("scalar", (lambda sc, g=g: sc.activation(
                            out=xcf[:], in_=cv[:], func=AF.Silu,
                            bias=cbsb[:, g:g+1])), deps=[c0, cb_l] +
                            ([last_xcf_store] if last_xcf_store else []))
                        sb_ = sch.add("scalar", (lambda sc, g=g: sc.activation(
                            out=xcb[:], in_=cv[:], func=AF.Silu,
                            bias=cbsb[:, g:g+1])), deps=[sf] +
                            ([last_xcb_store] if last_xcb_store else []))
                        last_cv_reader = sb_
                        last_xcf_store = sch.add("sync", (lambda s, b=b, g=g: s.dma_start(
                            out=ap["s_xcf"][b, g*128:(g+1)*128, :], in_=xcf[:])),
                            deps=[sf], dma=True)
                        last_xcb_store = sch.add("sync", (lambda s, b=b, g=g: s.dma_start(
                            out=ap["s_xcb"][b, g*128:(g+1)*128, :], in_=xcb[:])),
                            deps=[sb_], dma=True)
                    else:
                        gz_ = sch.add("scalar", (lambda sc, pi=pi: sc.activation(
                            out=gzt[:], in_=ps[pi][:], func=AF.Silu)),
                            deps=[mm] + ([last_gz_store] if last_gz_store else []))
                        last_ps_reader[pi] = gz_
                        last_gz_store = sch.add("sync", (lambda s, b=b, g=g: s.dma_start(
                            out=ap["s_gz"][b, (g-32)*128:(g-31)*128, :], in_=gzt[:])),
                            deps=[gz_], dma=True)
            ps_dt = ps[0]
            ps_bc = ps[1]
            mm_dt_ops = []
            xp_l = None
            xc_l = None
            for dc in range(32):
                xp_deps = (mm_dt_ops[-1:] if mm_dt_ops else
                           [d for d in last_ps_reader if d])
                xp_l = sch.add("scalar", (lambda s, dc=dc: s.dma_start_transpose(
                    xpj[:], in_=ap["wb_xproj_pad"][0:256, dc*128:(dc+1)*128])),
                    deps=xp_deps + ([xp_l] if xp_l else []), dma=True)
                xc_l = sch.add("sync", (lambda s, b=b, dc=dc: s.dma_start(
                    out=xcbl[:], in_=ap["s_xcb"][b, dc*128:(dc+1)*128, :])),
                    deps=(mm_dt_ops[-1:] if mm_dt_ops else
                          [last_xcb_store]) + ([xc_l] if xc_l else []), dma=True)
                def f_xd(tn, dc=dc):
                    for nch in range(4):
                        tn.matmul(ps_dt[:, nch*512:(nch+1)*512],
                                  lhsT=xpj[:, 0:128],
                                  rhs=xcbl[:, nch*512:(nch+1)*512],
                                  start=(dc == 0), stop=(dc == 31))
                        last = tn.matmul(ps_bc[0:32, nch*512:(nch+1)*512],
                                         lhsT=xpj[:, 128:160],
                                         rhs=xcbl[:, nch*512:(nch+1)*512],
                                         start=(dc == 0), stop=(dc == 31))
                    return last
                mm_op = sch.add("tensor", f_xd, deps=[xp_l, xc_l] +
                                ([d for d in last_ps_reader if d] if dc == 0 else []))
                mm_dt_ops.append(mm_op)
            ev1 = sch.add("vector", lambda v: v.tensor_copy(out=dtb[:], in_=ps_dt[:]),
                          deps=[mm_dt_ops[-1]])
            ev2 = sch.add("vector", lambda v: v.tensor_copy(out=bcb[:], in_=ps_bc[0:32, :]),
                          deps=[mm_dt_ops[-1]])
            s1 = sch.add("sync", (lambda s, b=b: s.dma_start(
                out=ap["s_dtlo"][b], in_=dtb[:])), deps=[ev1], dma=True)
            s2 = sch.add("sync", (lambda s, b=b: s.dma_start(
                out=ap["s_bc"][b], in_=bcb[:])), deps=[ev2], dma=True)
            last_ps_reader = [s1, s2]
        return sch.emit(bar, bar_val)


def ph_mamba_scan(nc, ap, bar=None, bar_val=0, cast_late=False):
    TC = 1024
    NTC = T // TC
    ND = DI // 128
    cjobs = cast_jobs(ap, CAST_LATE, FW=1024) if cast_late else []
    cpos = [0]
    es = ExitStack()
    with es:
        cbufs = [es.enter_context(nc.sbuf_tensor(f"msc_b{i}", [128, 1024], BF16))
                 for i in range(2)]
        cprev = [None, None]
        dtlo = es.enter_context(nc.sbuf_tensor("ms_dtlo", [128, TC], BF16))
        dtw = [es.enter_context(nc.sbuf_tensor(f"ms_dtw{j}", [128, 512], BF16))
               for j in range(ND // 4)]
        dtb = es.enter_context(nc.sbuf_tensor("ms_dtb", [128, ND], F32))
        dsb = es.enter_context(nc.sbuf_tensor("ms_d", [128, ND], F32))
        asb = es.enter_context(nc.sbuf_tensor("ms_a", [128, ND * DS], F32))
        Bball = es.enter_context(nc.sbuf_tensor("ms_Bb", [128, DS * TC], BF16))
        Cball = es.enter_context(nc.sbuf_tensor("ms_Cb", [128, DS * TC], BF16))
        sp1 = es.enter_context(nc.sbuf_tensor("ms_sp1", [128, TC], F32))
        sp2 = es.enter_context(nc.sbuf_tensor("ms_sp2", [128, TC], F32))
        delt = es.enter_context(nc.sbuf_tensor("ms_delt", [128, TC], F32))
        xct = [es.enter_context(nc.sbuf_tensor(f"ms_xc{i}", [128, TC], F32))
               for i in range(2)]
        gzt = es.enter_context(nc.sbuf_tensor("ms_gz", [128, TC], F32))
        wbf = es.enter_context(nc.sbuf_tensor("ms_wbf", [128, TC], BF16))
        daall = es.enter_context(nc.sbuf_tensor("ms_da", [128, DS * TC], BF16))
        uall = es.enter_context(nc.sbuf_tensor("ms_u", [128, DS * TC], BF16))
        hall = es.enter_context(nc.sbuf_tensor("ms_h", [128, DS * TC], BF16))
        ygb = es.enter_context(nc.sbuf_tensor("ms_yg", [128, TC], BF16))
        hstate = es.enter_context(nc.sbuf_tensor("ms_hst", [128, ND * DS], F32))
        ps_d = es.enter_context(nc.psum_tensor("ms_psd", [128, TC], F32))
        sch = Sched(nc, es, "mscan")

        def drip_cast(k=6):
            while cpos[0] < len(cjobs) and k > 0:
                emit_cast_job(sch, cbufs, cprev, cpos[0], cjobs[cpos[0]])
                cpos[0] += 1
                k -= 1

        base_l = []
        base_l.append(sch.add("sync", lambda s: s.dma_start(
            out=dtb[:], in_=ap["aux_dtb"]), dma=True))
        base_l.append(sch.add("sync", lambda s: s.dma_start(
            out=dsb[:], in_=ap["aux_dvec"]), dma=True))
        base_l.append(sch.add("sync", lambda s: s.dma_start(
            out=asb[:].rearrange("p (n s) -> p n s", s=DS),
            in_=ap["s_Aneg"].rearrange("(n p) s -> p n s", p=128)), dma=True))
        dtw_l = []
        for j in range(ND // 4):
            dtw_l.append(sch.add("scalar", (lambda s, j=j: s.dma_start_transpose(
                dtw[j][:], in_=ap["wb_L0_attn_dt_proj_w"][j*512:(j+1)*512, :])),
                dma=True))

        dtlo_load = None
        last_dtlo_user = None
        last_bc_users = []
        last_store = None
        last_da_reader = None
        last_u_reader = [None, None]
        last_h_reader = None
        last_uv = None
        last_wm = None
        last_xg_reader = [None, None]
        last_dl = None
        last_dav = None
        last_yg = None
        last_y1_reader = None
        state_cpy = {}
        for b in range(B_LOC):
            for tc in range(NTC):
                dl_deps = ([dtlo_load] if dtlo_load else []) + \
                          ([last_dtlo_user] if last_dtlo_user else [])
                dtlo_load = sch.add("sync", (lambda s, b=b, tc=tc: s.dma_start(
                    out=dtlo[:], in_=ap["s_dtlo"][b, :, tc*TC:(tc+1)*TC])),
                    deps=dl_deps, dma=True)
                new_bc = []
                for s_ in range(DS):
                    new_bc.append(sch.add("sync", (lambda s, b=b, tc=tc, s_=s_:
                        s.dma_start(out=Bball[:, s_*TC:(s_+1)*TC],
                            in_=ap["s_bc"][b, s_:s_+1, tc*TC:(tc+1)*TC]
                                .partition_broadcast(128))),
                        deps=last_bc_users, dma=True))
                    new_bc.append(sch.add("sync", (lambda s, b=b, tc=tc, s_=s_:
                        s.dma_start(out=Cball[:, s_*TC:(s_+1)*TC],
                            in_=ap["s_bc"][b, DS+s_:DS+s_+1, tc*TC:(tc+1)*TC]
                                .partition_broadcast(128))),
                        deps=last_bc_users, dma=True))
                bc_loads = new_bc
                last_bc_users = []
                prev_ps_reader = None
                xl = None
                gl = None
                for dt_ in range(ND):
                    mmdeps = [dtlo_load, dtw_l[dt_ // 4]]
                    if prev_ps_reader is not None:
                        mmdeps.append(prev_ps_reader)
                    def f_dmm(tn, dt_=dt_, tc=tc):
                        for nch in range(TC // 512):
                            last = tn.matmul(ps_d[:, nch*512:(nch+1)*512],
                                lhsT=dtw[dt_ // 4][:, (dt_ % 4)*128:(dt_ % 4+1)*128],
                                rhs=dtlo[:, nch*512:(nch+1)*512],
                                start=True, stop=True)
                        return last
                    mm = sch.add("tensor", f_dmm, deps=mmdeps)
                    last_dtlo_user = mm
                    r1 = sch.add("scalar", (lambda sc, dt_=dt_: sc.activation(
                        out=sp1[:], in_=ps_d[:], func=AF.Relu,
                        bias=dtb[:, dt_:dt_+1])),
                        deps=[mm, base_l[0]] + ([last_dl] if last_dl else []))
                    a1 = sch.add("scalar", (lambda sc, dt_=dt_: sc.activation(
                        out=sp2[:], in_=ps_d[:], func=AF.Abs,
                        bias=dtb[:, dt_:dt_+1])),
                        deps=[mm] + ([last_dl] if last_dl else []))
                    prev_ps_reader = a1
                    e1 = sch.add("scalar", lambda sc: sc.activation(
                        out=delt[:], in_=sp2[:], func=AF.Exp, scale=-1.0),
                        deps=[a1] + ([last_wm] if last_wm else []) +
                             ([last_yg] if last_yg else []))
                    l1 = sch.add("scalar", lambda sc: sc.activation(
                        out=sp2[:], in_=delt[:], func=AF.Ln, bias=1.0), deps=[e1])
                    dl = sch.add("vector", lambda v: v.tensor_tensor(
                        out=delt[:], in0=sp1[:], in1=sp2[:], op=ALU.add),
                        deps=[r1, l1] + ([last_dav] if last_dav else []))
                    last_dl = dl
                    xi = dt_ % 2
                    xl = sch.add("sync", (lambda s, b=b, dt_=dt_, tc=tc, xi=xi:
                        s.dma_start(
                        out=xct[xi][:], in_=ap["s_xcf"][b, dt_*128:(dt_+1)*128,
                                                        tc*TC:(tc+1)*TC])),
                        deps=([last_xg_reader[xi]] if last_xg_reader[xi] else []) +
                             ([xl] if xl else []), dma=True)
                    gl = sch.add("sync", (lambda s, b=b, dt_=dt_, tc=tc:
                        s.dma_start(
                        out=gzt[:], in_=ap["s_gz"][b, dt_*128:(dt_+1)*128,
                                                   tc*TC:(tc+1)*TC])),
                        deps=([last_yg] if last_yg else []) +
                             ([gl] if gl else []), dma=True)
                    wm = sch.add("vector", (lambda v, xi=xi: v.tensor_tensor(
                        out=wbf[:], in0=delt[:], in1=xct[xi][:], op=ALU.mult)),
                        deps=[dl, xl] + ([last_uv] if last_uv else []))
                    last_wm = wm
                    drip_cast()
                    da_ops = []
                    for s_ in range(DS):
                        dav = sch.add("scalar", (lambda sc, s_=s_, dt_=dt_:
                            sc.activation(out=daall[:, s_*TC:(s_+1)*TC], in_=delt[:],
                                          func=AF.Exp,
                                          scale=asb[:, dt_*DS+s_:dt_*DS+s_+1])),
                            deps=[dl, base_l[2]] +
                                 ([last_da_reader] if s_ == 0 and last_da_reader
                                  else []))
                        da_ops.append(dav)
                    last_dav = da_ops[-1]
                    HS2 = DS // 2
                    uva = sch.add("vector", lambda g: g.tensor_tensor(
                        out=uall[:, 0:HS2*TC].rearrange("p (s t) -> p s t", s=HS2),
                        in0=wbf[:].rearrange("p (o t) -> p o t", o=1)
                            .broadcast_to([128, HS2, TC]),
                        in1=Bball[:, 0:HS2*TC].rearrange("p (s t) -> p s t", s=HS2),
                        op=ALU.mult),
                        deps=[wm] + bc_loads[0:DS:2] +
                             ([last_u_reader[0]] if last_u_reader[0] else []))
                    uvb = sch.add("vector", lambda g: g.tensor_tensor(
                        out=uall[:, HS2*TC:DS*TC].rearrange("p (s t) -> p s t", s=HS2),
                        in0=wbf[:].rearrange("p (o t) -> p o t", o=1)
                            .broadcast_to([128, HS2, TC]),
                        in1=Bball[:, HS2*TC:DS*TC].rearrange("p (s t) -> p s t", s=HS2),
                        op=ALU.mult),
                        deps=[wm] + bc_loads[DS::2] +
                             ([last_u_reader[1]] if last_u_reader[1] else []))
                    last_uv = uvb
                    sc_ops = []
                    for s_ in range(DS):
                        scn_deps = [da_ops[s_], uva if s_ < DS // 2 else uvb]
                        if tc == 0:
                            init = 0.0
                        else:
                            init = hstate[:, dt_*DS+s_:dt_*DS+s_+1]
                            scn_deps.append(state_cpy[dt_])
                        if s_ == 0 and last_h_reader:
                            scn_deps.append(last_h_reader)
                        scn = sch.add("vector", (lambda v, s_=s_, init=init:
                            v.tensor_tensor_scan(out=hall[:, s_*TC:(s_+1)*TC],
                                                 data0=daall[:, s_*TC:(s_+1)*TC],
                                                 data1=uall[:, s_*TC:(s_+1)*TC],
                                                 initial=init,
                                                 op0=ALU.mult, op1=ALU.add)),
                            deps=scn_deps, self_edge=True)
                        sc_ops.append(scn)
                    last_da_reader = sc_ops[-1]
                    last_u_reader = [sc_ops[DS // 2 - 1], sc_ops[-1]]
                    cp_deps = list(sc_ops)
                    if tc < NTC - 1:
                        state_cpy[dt_] = sch.add("vector", (lambda v, dt_=dt_:
                            v.tensor_copy(out=hstate[:, dt_*DS:(dt_+1)*DS],
                                in_=hall[:].rearrange("p (s t) -> p s t", s=DS)
                                    [:, :, TC-1:TC])), deps=cp_deps)
                        cp_deps = [state_cpy[dt_]]
                    ym = sch.add("vector", lambda v: v.tensor_tensor(
                        out=hall[:], in0=hall[:], in1=Cball[:], op=ALU.mult),
                        deps=cp_deps + bc_loads[1::2])
                    t8 = sch.add("vector", lambda v: v.tensor_tensor(
                        out=hall[:, 0:8*TC], in0=hall[:, 0:8*TC],
                        in1=hall[:, 8*TC:16*TC], op=ALU.add), deps=[ym])
                    t4 = sch.add("vector", lambda v: v.tensor_tensor(
                        out=hall[:, 0:4*TC], in0=hall[:, 0:4*TC],
                        in1=hall[:, 4*TC:8*TC], op=ALU.add), deps=[t8])
                    t2 = sch.add("vector", lambda v: v.tensor_tensor(
                        out=hall[:, 0:2*TC], in0=hall[:, 0:2*TC],
                        in1=hall[:, 2*TC:4*TC], op=ALU.add), deps=[t4])
                    yr = sch.add("vector", lambda v: v.tensor_tensor(
                        out=hall[:, 0:TC], in0=hall[:, 0:TC], in1=hall[:, TC:2*TC],
                        op=ALU.add), deps=[t2])
                    last_h_reader = yr
                    y1 = sch.add("vector", (lambda v, dt_=dt_, xi=xi:
                        v.scalar_tensor_tensor(
                        out=delt[:], in0=xct[xi][:], scalar=dsb[:, dt_:dt_+1],
                        in1=hall[:, 0:TC],
                        op0=ALU.mult, op1=ALU.add)),
                        deps=[yr, base_l[1]] +
                             ([last_y1_reader] if last_y1_reader else []),
                        self_edge=True)
                    yg = sch.add("vector", lambda v: v.tensor_tensor(
                        out=ygb[:], in0=delt[:], in1=gzt[:], op=ALU.mult),
                        deps=[y1, gl] + ([last_store] if last_store else []))
                    last_yg = yg
                    last_y1_reader = yg
                    last_xg_reader[xi] = yg
                    last_store = sch.add("sync", (lambda s, b=b, dt_=dt_, tc=tc:
                        s.dma_start(out=ap["s_yg"][b, dt_*128:(dt_+1)*128,
                                                   tc*TC:(tc+1)*TC], in_=ygb[:])),
                        deps=[yg], dma=True)
                last_bc_users = [last_uv, last_h_reader]
        drip_cast(10**9)
        return sch.emit(bar, bar_val)


def ph_mamba_out(nc, ap, bar=None, bar_val=0):
    NDC = DI // 128
    es = ExitStack()
    with es:
        opT = [es.enter_context(nc.sbuf_tensor(f"mo_w{c}", [128, D], BF16))
               for c in range(NDC)]
        ygt_all = es.enter_context(nc.sbuf_tensor("mo_yg", [128, NDC * 128], BF16))
        xtile = es.enter_context(nc.sbuf_tensor("mo_xt", [128, D], F32))
        otile = es.enter_context(nc.sbuf_tensor("mo_ot", [128, D], F32))
        pm = es.enter_context(nc.psum_tensor("mo_pm", [128, D], F32))
        sch = Sched(nc, es, "mout")
        wl = []
        for c in range(NDC):
            for nch in range(4):
                wl.append(sch.add("scalar", (lambda s, c=c, nch=nch:
                    s.dma_start_transpose(opT[c][:, nch*512:(nch+1)*512],
                        in_=ap["wb_L0_attn_out_proj_w"][nch*512:(nch+1)*512,
                                                        c*128:(c+1)*128])), dma=True))
        prev_mm = None
        prev_ev = None
        prev_st = None
        for b in range(B_LOC):
            for tt in range(T // 128):
                yls = [sch.add("sync", (lambda s, b=b, tt=tt: s.dma_start(
                    out=ygt_all[:].rearrange("p (n f) -> p n f", f=128),
                    in_=ap["s_yg"][b].rearrange("(n p) t -> p n t", p=128)
                        [:, :, tt*128:(tt+1)*128])),
                    deps=([prev_mm] if prev_mm else []), dma=True)]
                def f_mm(tn):
                    for nch in range(4):
                        for c in range(NDC):
                            last = tn.matmul(pm[:, nch*512:(nch+1)*512],
                                             lhsT=ygt_all[:, c*128:(c+1)*128],
                                             rhs=opT[c][:, nch*512:(nch+1)*512],
                                             start=(c == 0), stop=(c == NDC-1))
                    return last
                mm = sch.add("tensor", f_mm, deps=yls + wl +
                             ([prev_ev] if prev_ev else []))
                prev_mm = mm
                xl = sch.add("sync", (lambda s, b=b, tt=tt: s.dma_start(
                    out=xtile[:], in_=ap["s_x1acc"][b, tt*128:(tt+1)*128, :])),
                    deps=([prev_ev] if prev_ev else []), dma=True)
                ev = sch.add("vector", lambda v: v.tensor_tensor(
                    out=otile[:], in0=pm[:], in1=xtile[:], op=ALU.add),
                    deps=[mm, xl] + ([prev_st] if prev_st else []))
                prev_ev = ev
                prev_st = sch.add("sync", (lambda s, b=b, tt=tt: s.dma_start(
                    out=ap["s_x1"][b, tt*128:(tt+1)*128, :], in_=otile[:])),
                    deps=[ev], dma=True)
        return sch.emit(bar, bar_val)


def ph_attn_pre(nc, ap, bar=None, bar_val=0):
    es = ExitStack()
    with es:
        wuq = [es.enter_context(nc.sbuf_tensor(f"apr_wuq{c}", [128, LQ], BF16))
               for c in range(16)]
        wuk = [es.enter_context(nc.sbuf_tensor(f"apr_wuk{c}", [128, LKV], BF16))
               for c in range(16)]
        wdq = [es.enter_context(nc.sbuf_tensor(f"apr_wdq{c}", [128, D], BF16))
               for c in range(4)]
        wuv = [es.enter_context(nc.sbuf_tensor(f"apr_wuv{c}", [128, LKV], BF16))
               for c in range(16)]
        woT = [es.enter_context(nc.sbuf_tensor(f"apr_woT{c}", [128, D], BF16))
               for c in range(16)]
        t1 = [es.enter_context(nc.sbuf_tensor(f"apr_t1{c}", [128, LKV], BF16))
              for c in range(4)]
        stg = es.enter_context(nc.sbuf_tensor("apr_stg", [128, D], BF16))
        pm = es.enter_context(nc.psum_tensor("apr_pm", [128, D], F32))
        sch = Sched(nc, es, "apre")
        wl = []
        for c in range(16):
            wl.append(sch.add("sync", (lambda s, c=c: s.dma_start(
                out=wuq[c][:], in_=ap["wb_L1_attn_Wuq"][c*128:(c+1)*128, :])), dma=True))
            wl.append(sch.add("sync", (lambda s, c=c: s.dma_start(
                out=wuk[c][:], in_=ap["wb_L1_attn_Wuk"][c*128:(c+1)*128, :])), dma=True))
            wl.append(sch.add("gpsimd", (lambda s, c=c: s.dma_start(
                out=wuv[c][:], in_=ap["wb_L1_attn_Wuv"][c*128:(c+1)*128, :])), dma=True))
            for nch in range(4):
                wl.append(sch.add("scalar", (lambda s, c=c, nch=nch:
                    s.dma_start_transpose(woT[c][:, nch*512:(nch+1)*512],
                        in_=ap["wb_L1_attn_Wo"][nch*512:(nch+1)*512,
                                                c*128:(c+1)*128])), dma=True))
        for c in range(4):
            wl.append(sch.add("sync", (lambda s, c=c: s.dma_start(
                out=wdq[c][:], in_=ap["wb_L1_attn_Wdq"][c*128:(c+1)*128, :])), dma=True))
        prev_ev = None
        for lt in range(4):
            def f_t1(tn, lt=lt):
                for c in range(16):
                    last = tn.matmul(pm[:, 0:LKV], lhsT=wuq[c][:, lt*128:(lt+1)*128],
                                     rhs=wuk[c][:], start=(c == 0), stop=(c == 15))
                return last
            mm = sch.add("tensor", f_t1, deps=wl + ([prev_ev] if prev_ev else []))
            prev_ev = sch.add("vector", (lambda v, lt=lt: v.tensor_copy(
                out=t1[lt][:], in_=pm[:, 0:LKV])), deps=[mm])
        for mt in range(16):
            def f_kf(tn, mt=mt):
                for lc in range(4):
                    last = tn.matmul(pm[:, 0:LKV], lhsT=wdq[lc][:, mt*128:(mt+1)*128],
                                     rhs=t1[lc][:], start=(lc == 0), stop=(lc == 3))
                return last
            mm = sch.add("tensor", f_kf, deps=[prev_ev])
            ev = sch.add("vector", lambda v: v.tensor_copy(
                out=stg[:, 0:LKV], in_=pm[:, 0:LKV]), deps=[mm])
            prev_ev = sch.add("sync", (lambda s, mt=mt: s.dma_start(
                out=ap["s_kfull"][mt*128:(mt+1)*128, :], in_=stg[:, 0:LKV])),
                deps=[ev], dma=True)
        for lt in range(4):
            def f_vf(tn, lt=lt):
                for nch in range(4):
                    for c in range(16):
                        last = tn.matmul(pm[:, nch*512:(nch+1)*512],
                                         lhsT=wuv[c][:, lt*128:(lt+1)*128],
                                         rhs=woT[c][:, nch*512:(nch+1)*512],
                                         start=(c == 0), stop=(c == 15))
                return last
            mm = sch.add("tensor", f_vf, deps=wl + [prev_ev])
            ev = sch.add("vector", lambda v: v.tensor_copy(
                out=stg[:], in_=pm[:]), deps=[mm])
            prev_ev = sch.add("sync", (lambda s, lt=lt: s.dma_start(
                out=ap["s_vfull"][lt*128:(lt+1)*128, :], in_=stg[:])),
                deps=[ev], dma=True)
        return sch.emit(bar, bar_val)


def ph_attn_proj(nc, ap, bar=None, bar_val=0):
    es = ExitStack()
    with es:
        xnT = [es.enter_context(nc.sbuf_tensor(f"apj_xnT{c}", [128, T], BF16))
               for c in range(16)]
        wdkvT = [es.enter_context(nc.sbuf_tensor(f"apj_wkt{c}", [128, LKV], BF16))
                 for c in range(16)]
        wdqT = [es.enter_context(nc.sbuf_tensor(f"apj_wqt{c}", [128, LQ], BF16))
                for c in range(16)]
        stg = es.enter_context(nc.sbuf_tensor("apj_stg", [128, T], BF16))
        pm = es.enter_context(nc.psum_tensor("apj_pm", [128, T], F32))
        sch = Sched(nc, es, "aproj")
        wl = []
        for c in range(16):
            wl.append(sch.add("scalar", (lambda s, c=c: s.dma_start_transpose(
                wdkvT[c][:], in_=ap["wb_L1_attn_Wdkv"][0:LKV, c*128:(c+1)*128])),
                dma=True))
            wl.append(sch.add("scalar", (lambda s, c=c: s.dma_start_transpose(
                wdqT[c][:], in_=ap["wb_L1_attn_Wdq"][0:LQ, c*128:(c+1)*128])),
                dma=True))
        xnT_load = [None] * 16
        last_xnT_user = None
        prev_ev = None
        prev_ev2 = None
        for b in range(B_LOC):
            for c in range(16):
                deps = ([xnT_load[c]] if xnT_load[c] else []) + \
                       ([last_xnT_user] if last_xnT_user else [])
                xnT_load[c] = sch.add("scalar", (lambda s, b=b, c=c:
                    s.dma_start_transpose(xnT[c][:],
                        in_=ap["s_xn1"][b, 0:T, c*128:(c+1)*128])), deps=deps, dma=True)
            loads_b = list(xnT_load)
            for which, wT, dst in (("ckvf", wdkvT, "s_ckvf"), ("qd", wdqT, "s_qd")):
                for lt in range(4):
                    def f_mm(tn, wT=wT, lt=lt):
                        for nch in range(4):
                            for c in range(16):
                                last = tn.matmul(pm[:, nch*512:(nch+1)*512],
                                                 lhsT=wT[c][:, lt*128:(lt+1)*128],
                                                 rhs=xnT[c][:, nch*512:(nch+1)*512],
                                                 start=(c == 0), stop=(c == 15))
                        return last
                    mm = sch.add("tensor", f_mm, deps=wl + loads_b +
                                 ([prev_ev] if prev_ev else []))
                    last_xnT_user = mm
                    ev = sch.add("vector", lambda v: v.tensor_copy(
                        out=stg[:], in_=pm[:]), deps=[mm])
                    prev_ev = sch.add("sync", (lambda s, b=b, lt=lt, dst=dst:
                        s.dma_start(out=ap[dst][b, lt*128:(lt+1)*128, :],
                                    in_=stg[:])), deps=[ev], dma=True)
        return sch.emit(bar, bar_val)


def ph_attn_core(nc, ap, bar=None, bar_val=0):
    NQT = T // 128
    es = ExitStack()
    with es:
        kfull = [es.enter_context(nc.sbuf_tensor(f"ac_kf{c}", [128, LKV], BF16))
                 for c in range(16)]
        vfull = [es.enter_context(nc.sbuf_tensor(f"ac_vf{c}", [128, D], BF16))
                 for c in range(4)]
        wuqT = [es.enter_context(nc.sbuf_tensor(f"ac_wuq{c}", [128, D], BF16))
                for c in range(4)]
        ckvf = [es.enter_context(nc.sbuf_tensor(f"ac_cf{c}", [128, T], BF16))
                for c in range(4)]
        qd = [es.enter_context(nc.sbuf_tensor(f"ac_qd{c}", [128, T], BF16))
              for c in range(4)]
        qh = es.enter_context(nc.sbuf_tensor("ac_qh", [128, T], BF16))
        qlat = [es.enter_context(nc.sbuf_tensor(f"ac_ql{c}", [128, T], BF16))
                for c in range(4)]
        wh = [es.enter_context(nc.sbuf_tensor(f"ac_wh{c}", [128, HS], BF16))
              for c in range(16)]
        pbf = es.enter_context(nc.sbuf_tensor("ac_p", [128, T], BF16))
        pT = [es.enter_context(nc.sbuf_tensor(f"ac_pT{c}", [128, 512], BF16))
              for c in range(16)]
        ystg = [es.enter_context(nc.sbuf_tensor(f"ac_y{i}", [128, HS], F32))
                for i in range(2)]
        rinv = es.enter_context(nc.sbuf_tensor("ac_ri", [128, NQT], F32))
        mask = es.enter_context(nc.sbuf_tensor("ac_mask", [128, 128], F32))
        cst = es.enter_context(nc.sbuf_tensor("ac_cst", [128, 1], F32))
        st = es.enter_context(nc.sbuf_tensor("ac_st", [128, 4], F32))
        ps_big = es.enter_context(nc.psum_tensor("ac_pb", [128, T], F32))
        ps_wh = es.enter_context(nc.psum_tensor("ac_pw", [128, HS], F32))
        ps_y = [es.enter_context(nc.psum_tensor(f"ac_py{i}", [128, HS], F32))
                for i in range(2)]
        sch = Sched(nc, es, "acore")
        wl = []
        for c in range(16):
            wl.append(sch.add("sync", (lambda s, c=c: s.dma_start(
                out=kfull[c][:], in_=ap["s_kfull"][c*128:(c+1)*128, :])), dma=True))
        for c in range(4):
            wl.append(sch.add("sync", (lambda s, c=c: s.dma_start(
                out=vfull[c][:], in_=ap["s_vfull"][c*128:(c+1)*128, :])), dma=True))
            for nch in range(4):
                wl.append(sch.add("scalar", (lambda s, c=c, nch=nch:
                    s.dma_start_transpose(wuqT[c][:, nch*512:(nch+1)*512],
                        in_=ap["wb_L1_attn_Wuq"][nch*512:(nch+1)*512,
                                                 c*128:(c+1)*128])), dma=True))
        wl.append(sch.add("sync", lambda s: s.dma_start(out=mask[:], in_=ap["aux_mask"]),
                          dma=True))
        csto = sch.add("vector", lambda v: v.memset(cst[:], RSQ_HS))
        wl.append(csto)

        state = {
            "bl": [None] * 8, "last_bl_user": None, "last_pbig": None,
            "last_py": [None, None], "last_pwh": None, "last_pbf_user": None,
            "last_wh_user": None, "last_pT_user": [None] * 16,
            "prev_pT_loads": [],
        }
        for b in range(B_LOC):
            deps_b = ([state["last_bl_user"]] if state["last_bl_user"] else [])
            nbl = []
            bl = state["bl"]
            for c in range(4):
                nbl.append(sch.add("sync", (lambda s, b=b, c=c: s.dma_start(
                    out=ckvf[c][:], in_=ap["s_ckvf"][b, c*128:(c+1)*128, :])),
                    deps=deps_b + (bl[c:c+1] if bl[c] else []), dma=True))
            for c in range(4):
                nbl.append(sch.add("gpsimd", (lambda s, b=b, c=c: s.dma_start(
                    out=qd[c][:], in_=ap["s_qd"][b, c*128:(c+1)*128, :])),
                    deps=deps_b + (bl[4+c:5+c] if bl[4+c] else []), dma=True))
            state["bl"] = bl = nbl
            for h in range(NH):
                def f_qh(tn, h=h):
                    for lc in range(4):
                        for nch in range(4):
                            last = tn.matmul(ps_big[:, nch*512:(nch+1)*512],
                                             lhsT=wuqT[lc][:, h*128:(h+1)*128],
                                             rhs=qd[lc][:, nch*512:(nch+1)*512],
                                             start=(lc == 0), stop=(lc == 3))
                    return last
                qhmm = sch.add("tensor", f_qh, deps=wl + bl +
                               ([state["last_pbig"]] if state["last_pbig"] else []))
                qhev = sch.add("scalar", lambda sc: sc.copy(out=qh[:], in_=ps_big[:]),
                               deps=[qhmm])
                state["last_pbig"] = qhev
                ql_evs = []
                for lt in range(4):
                    def f_ql(tn, h=h, lt=lt):
                        for nch in range(4):
                            last = tn.matmul(ps_big[:, nch*512:(nch+1)*512],
                                             lhsT=kfull[h][:, lt*128:(lt+1)*128],
                                             rhs=qh[:, nch*512:(nch+1)*512],
                                             start=True, stop=True)
                        return last
                    qlmm = sch.add("tensor", f_ql, deps=[qhev, state["last_pbig"]])
                    qlev = sch.add("scalar", (lambda sc, lt=lt: sc.activation(
                        out=qlat[lt][:], in_=ps_big[:], func=AF.Identity,
                        scale=cst[:, 0:1])), deps=[qlmm, csto])
                    ql_evs.append(qlev)
                    state["last_pbig"] = qlev
                wh_evs = []
                for kt in range(16):
                    def f_wh(tn, h=h, kt=kt):
                        for lc in range(4):
                            last = tn.matmul(ps_wh[:],
                                             lhsT=ckvf[lc][:, kt*128:(kt+1)*128],
                                             rhs=vfull[lc][:, h*128:(h+1)*128],
                                             start=(lc == 0), stop=(lc == 3))
                        return last
                    whmm = sch.add("tensor", f_wh, deps=wl + bl +
                                   ([state["last_pwh"]] if state["last_pwh"] else []))
                    whev = sch.add("vector", (lambda v, kt=kt: v.tensor_copy(
                        out=wh[kt][:], in_=ps_wh[:])), deps=[whmm] +
                        ([state["last_wh_user"]] if state["last_wh_user"] and
                         kt == 0 else []))
                    state["last_pwh"] = whev
                    wh_evs.append(whev)
                p_stores = {}
                rcs = {}

                def emit_scores(qt):
                    kr = (qt + 1) * 128
                    def f_sc(tn, qt=qt, kr=kr):
                        nch = 0
                        while nch * 512 < kr:
                            w = min(512, kr - nch * 512)
                            for lc in range(4):
                                last = tn.matmul(
                                    ps_big[:, nch*512:nch*512+w],
                                    lhsT=qlat[lc][:, qt*128:(qt+1)*128],
                                    rhs=ckvf[lc][:, nch*512:nch*512+w],
                                    start=(lc == 0), stop=(lc == 3))
                            nch += 1
                        return last
                    smm = sch.add("tensor", f_sc, deps=ql_evs +
                                  ([state["last_pbig"]] if state["last_pbig"]
                                   else []))
                    msk = sch.add("vector", (lambda v, kr=kr: v.tensor_tensor(
                        out=ps_big[:, kr-128:kr], in0=ps_big[:, kr-128:kr],
                        in1=mask[:], op=ALU.add)), deps=[smm])
                    mx = sch.add("vector", (lambda v, kr=kr: v.reduce_max(
                        out=st[:, 0:1], in_=ps_big[:, 0:kr], axis=AX.X,
                        negate=True)), deps=[msk])
                    exp = sch.add("scalar", (lambda sc, kr=kr: sc.activation(
                        out=pbf[:, 0:kr], in_=ps_big[:, 0:kr], func=AF.Exp,
                        bias=st[:, 0:1], accum_out=st[:, 1:2])),
                        deps=[mx] + ([state["last_pbf_user"]]
                                     if state["last_pbf_user"] else []))
                    state["last_pbig"] = exp
                    rcs[qt] = sch.add("vector", (lambda v, qt=qt: v.reciprocal(
                        out=rinv[:, qt:qt+1], in_=st[:, 1:2])), deps=[exp])
                    pst = sch.add("sync", (lambda s, qt=qt, kr=kr: s.dma_start(
                        out=ap["s_p"][qt*128:(qt+1)*128, 0:kr], in_=pbf[:, 0:kr])),
                        deps=[exp] + state["prev_pT_loads"], dma=True)
                    state["last_pbf_user"] = pst
                    p_stores[qt] = pst

                new_pT_loads = []

                def emit_pT_loads(qg):
                    gkc = (qg + 1) * 4
                    pT_loads = {}
                    for kc in range(gkc):
                        need = [p_stores[qt] for qt in range(qg*4, qg*4+4)
                                if (qt + 1) * 128 > kc * 128]
                        pT_loads[kc] = sch.add("scalar", (lambda s, qg=qg, kc=kc:
                            s.dma_start_transpose(pT[kc][:],
                                in_=ap["s_p"][qg*512:(qg+1)*512,
                                              kc*128:(kc+1)*128])),
                            deps=need + ([state["last_pT_user"][kc]]
                                         if state["last_pT_user"][kc] else []),
                            dma=True)
                        new_pT_loads.append(pT_loads[kc])
                    return pT_loads

                def emit_y_tile(qg, qi, pT_loads):
                    qt = qg * 4 + qi
                    nkc = qt + 1
                    yi = qt % 2
                    def f_y(tn, qi=qi, nkc=nkc, yi=yi):
                        for kc in range(nkc):
                            last = tn.matmul(ps_y[yi][:],
                                             lhsT=pT[kc][:, qi*128:(qi+1)*128],
                                             rhs=wh[kc][:],
                                             start=(kc == 0), stop=(kc == nkc-1))
                        return last
                    ymm = sch.add("tensor", f_y,
                                  deps=[pT_loads[kc] for kc in range(nkc)] +
                                       wh_evs[:nkc] +
                                       ([state["last_py"][yi]]
                                        if state["last_py"][yi] else []))
                    for kc in range(nkc):
                        state["last_pT_user"][kc] = ymm
                    state["last_wh_user"] = ymm
                    yev = sch.add("vector", (lambda v, qt=qt, yi=yi:
                        v.tensor_scalar_mul(ystg[yi][:], ps_y[yi][:],
                                            rinv[:, qt:qt+1])),
                        deps=[ymm, rcs[qt]] +
                             ([state["last_py"][yi]]
                              if state["last_py"][yi] else []),
                        self_edge=True)
                    yst = sch.add("gpsimd", (lambda s, b=b, h=h, qt=qt, yi=yi:
                        s.dma_start(
                            out=ap["s_atn"][b, qt*128:(qt+1)*128,
                                            h*128:(h+1)*128],
                            in_=ystg[yi][:])), deps=[yev], dma=True)
                    state["last_py"][yi] = yst
                    state["last_bl_user"] = ymm

                for qt in range(4):
                    emit_scores(qt)
                for qg in range(4):
                    pT_loads = emit_pT_loads(qg)
                    nxt = list(range((qg+1)*4, min((qg+2)*4, NQT)))
                    for qi in range(4):
                        if qi < len(nxt):
                            emit_scores(nxt[qi])
                        emit_y_tile(qg, qi, pT_loads)
                state["prev_pT_loads"] = new_pT_loads
        return sch.emit(bar, bar_val)


def ph_ffn1(nc, ap, bar=None, bar_val=0):
    TQ = 512
    es = ExitStack()
    with es:
        xnT = [es.enter_context(nc.sbuf_tensor(f"f1_xnT{c}", [128, TQ], BF16))
               for c in range(16)]
        acc = [es.enter_context(nc.sbuf_tensor(f"f1_acc{t}", [128, D], F32))
               for t in range(4)]
        tsb = [es.enter_context(nc.sbuf_tensor(f"f1_ts{i}", [128, TQ], BF16))
               for i in range(11)]
        gwT = es.enter_context(nc.sbuf_tensor("f1_gw", [128, 16 * E1], BF16))
        w1g = [es.enter_context(nc.sbuf_tensor(f"f1_w1g{i}", [128, 2048], BF16))
               for i in range(2)]
        w1u = [es.enter_context(nc.sbuf_tensor(f"f1_w1u{i}", [128, 2048], BF16))
               for i in range(2)]
        w2r = [es.enter_context(nc.sbuf_tensor(f"f1_w2r{i}", [128, D], BF16))
               for i in range(11)]
        sg = [es.enter_context(nc.sbuf_tensor(f"f1_sg{i}", [128, TQ], F32))
              for i in range(2)]
        cw = es.enter_context(nc.sbuf_tensor("f1_cw", [128, 4 * E1], F32))
        gsb = es.enter_context(nc.sbuf_tensor("f1_g", [128, E1], F32))
        g2 = es.enter_context(nc.sbuf_tensor("f1_g2", [128, E1], F32))
        keep = es.enter_context(nc.sbuf_tensor("f1_keep", [128, E1], F32))
        st = es.enter_context(nc.sbuf_tensor("f1_st", [128, 8], F32))
        xtile = es.enter_context(nc.sbuf_tensor("f1_xt", [128, D], F32))
        atile = es.enter_context(nc.sbuf_tensor("f1_at", [128, D], F32))
        otile = es.enter_context(nc.sbuf_tensor("f1_ot", [128, D], F32))
        ps_a = [es.enter_context(nc.psum_tensor(f"f1_pa{i}", [128, TQ], F32))
                for i in range(2)]
        ps_b = [es.enter_context(nc.psum_tensor(f"f1_pb{i}", [128, TQ], F32))
                for i in range(2)]
        ps_y = [es.enter_context(nc.psum_tensor(f"f1_py{i}", [128, 512], F32))
                for i in range(2)]
        ps_g = es.enter_context(nc.psum_tensor("f1_pg", [128, E1], F32))
        sch = Sched(nc, es, "ffn1")
        gwl = sch.add("sync", lambda s: s.dma_start(
            out=gwT[:].rearrange("p (n e) -> p n e", e=E1),
            in_=ap["wb_gate_wT"].rearrange("(n p) e -> p n e", p=128)), dma=True)

        tb_done = []
        prev_pair = [None, None]
        for srcn, dstn, RT, CT in (("wb_L1_ffn_sw1", "wt_sw1", D, I1),
                                   ("wb_L1_ffn_sw2", "wt_sw2", D, I1),
                                   ("wb_L1_ffn_sw3", "wt_sw3", I1, D)):
            for rc in range(RT // 128):
                bi = rc % 2
                buf = w1g[bi]
                lo = sch.add("scalar", (lambda s, srcn=srcn, rc=rc, CT=CT, buf=buf:
                    s.dma_start_transpose(buf[:, 0:CT],
                        in_=ap[srcn][0:CT, rc*128:(rc+1)*128])),
                    deps=([prev_pair[bi]] if prev_pair[bi] else []), dma=True)
                st_ = sch.add("sync", (lambda s, dstn=dstn, rc=rc, CT=CT, buf=buf:
                    s.dma_start(out=ap[dstn][rc*128:(rc+1)*128, 0:CT],
                                in_=buf[:, 0:CT])), deps=[lo], dma=True)
                prev_pair[bi] = st_
                tb_done.append(st_)
        xnT_load = [None] * 16
        last_xnT_user = None
        last_out_store = [None] * 4
        last_psg_read = None
        last_gsb_reader = None
        last_t_reader = [None] * 11
        last_sw_load = [None, None]
        last_sw_user = None
        last_w1_user = [None, None]
        last_w1_load = [None, None]
        last_w2_user = [None] * 11
        last_w2_load = [None] * 11
        last_swt_user = [None, None]
        last_swt_load = [None, None]
        last_psab_read = [None, None]
        last_psy_read = [None, None]
        last_acc_reader = [None] * 4
        for b in range(B_LOC):
            for qtr in range(4):
                t0 = qtr * TQ
                for c in range(16):
                    deps = ([xnT_load[c]] if xnT_load[c] else []) + \
                           ([last_xnT_user] if last_xnT_user else [])
                    xnT_load[c] = sch.add("scalar", (lambda s, b=b, t0=t0, c=c:
                        s.dma_start_transpose(xnT[c][:],
                            in_=ap["s_xn1"][b, t0:t0+TQ, c*128:(c+1)*128])),
                        deps=deps, dma=True)
                loads_b = list(xnT_load)
                w6ops = []
                for tt in range(4):
                    def f_g(tn, tt=tt):
                        for c in range(16):
                            last = tn.matmul(ps_g[:], lhsT=xnT[c][:, tt*128:(tt+1)*128],
                                             rhs=gwT[:, c*E1:(c+1)*E1],
                                             start=(c == 0), stop=(c == 15))
                        return last
                    gmm = sch.add("tensor", f_g, deps=loads_b + [gwl] +
                                  ([last_psg_read] if last_psg_read else []))
                    mx = sch.add("vector", lambda v: v.reduce_max(
                        out=st[:, 0:1], in_=ps_g[:], axis=AX.X, negate=True),
                        deps=[gmm])
                    ex = sch.add("scalar", lambda sc: sc.activation(
                        out=gsb[:], in_=ps_g[:], func=AF.Exp, bias=st[:, 0:1],
                        accum_out=st[:, 1:2]),
                        deps=[mx] + ([last_gsb_reader] if last_gsb_reader else []))
                    last_psg_read = ex
                    rc = sch.add("vector", lambda v: v.reciprocal(
                        out=st[:, 2:3], in_=st[:, 1:2]), deps=[ex])
                    gg = sch.add("vector", lambda v: v.tensor_scalar_mul(
                        gsb[:], gsb[:], st[:, 2:3]), deps=[rc, ex], self_edge=True)
                    m1 = sch.add("vector", lambda v: v.reduce_max(
                        out=st[:, 3:4], in_=gsb[:], axis=AX.X), deps=[gg])
                    e1_ = sch.add("vector", lambda v: v.tensor_scalar(
                        out=g2[:], in0=gsb[:], scalar1=st[:, 3:4], scalar2=-1e30,
                        op0=ALU.is_ge, op1=ALU.mult), deps=[m1], self_edge=True)
                    g2s = sch.add("vector", lambda v: v.tensor_tensor(
                        out=g2[:], in0=gsb[:], in1=g2[:], op=ALU.add), deps=[e1_])
                    m2 = sch.add("vector", lambda v: v.reduce_max(
                        out=st[:, 4:5], in_=g2[:], axis=AX.X), deps=[g2s])
                    kp = sch.add("vector", lambda v: v.tensor_single_scalar(
                        out=keep[:], in_=gsb[:], scalar=st[:, 4:5], op=ALU.is_ge),
                        deps=[m2], self_edge=True)
                    sm = sch.add("vector", lambda v: v.tensor_scalar(
                        out=st[:, 5:6], in0=st[:, 3:4], scalar1=st[:, 4:5],
                        scalar2=1e-20, op0=ALU.add, op1=ALU.add),
                        deps=[m1, m2], self_edge=True)
                    rn = sch.add("vector", lambda v: v.reciprocal(
                        out=st[:, 6:7], in_=st[:, 5:6]), deps=[sm], self_edge=True)
                    gk = sch.add("vector", lambda v: v.tensor_tensor(
                        out=keep[:], in0=gsb[:], in1=keep[:], op=ALU.mult), deps=[kp])
                    w6 = sch.add("vector", (lambda v, tt=tt: v.tensor_scalar_mul(
                        cw[:, tt*E1:(tt+1)*E1], keep[:], st[:, 6:7])),
                        deps=[gk, rn], self_edge=True)
                    last_gsb_reader = w6
                    w6ops.append(w6)

                for phase in range(9):
                    e = phase - 1
                    t_ops = [None] * 11
                    if True:
                        for it in range(11):
                            bi = it % 2
                            ldeps = ([last_w1_load[bi]] if last_w1_load[bi] else []) + \
                                    ([last_w1_user[bi]] if last_w1_user[bi] else [])
                            if phase == 0:
                                ldeps = ldeps + tb_done
                                gsrc = ap["wt_sw1"].rearrange("(n p) f -> p n f", p=128)
                                usrc = ap["wt_sw2"].rearrange("(n p) f -> p n f", p=128)
                                gsl = gsrc[:, :, it*128:(it+1)*128]
                                usl = usrc[:, :, it*128:(it+1)*128]
                            else:
                                gsl = ap["wb_L1_ffn_w1"][e, :, it*128:(it+1)*128] \
                                    .rearrange("(n p) f -> p n f", p=128)
                                usl = ap["wb_L1_ffn_w1"][e, :, I1+it*128:I1+(it+1)*128] \
                                    .rearrange("(n p) f -> p n f", p=128)
                            lg = sch.add("sync", (lambda s, gsl=gsl, bi=bi:
                                s.dma_start(
                                    out=w1g[bi][:].rearrange("p (n f) -> p n f", f=128),
                                    in_=gsl)), deps=ldeps, dma=True)
                            lu = sch.add("gpsimd", (lambda s, usl=usl, bi=bi:
                                s.dma_start(
                                    out=w1u[bi][:].rearrange("p (n f) -> p n f", f=128),
                                    in_=usl)), deps=ldeps, dma=True)
                            last_w1_load[bi] = lu
                            pi = it % 2
                            def f_gu(tn, bi=bi, pi=pi):
                                for c in range(16):
                                    tn.matmul(ps_a[pi][:],
                                              lhsT=w1g[bi][:, c*128:(c+1)*128],
                                              rhs=xnT[c][:], start=(c == 0), stop=False)
                                    last = tn.matmul(ps_b[pi][:],
                                                     lhsT=w1u[bi][:, c*128:(c+1)*128],
                                                     rhs=xnT[c][:], start=(c == 0),
                                                     stop=False)
                                return last
                            gu = sch.add("tensor", f_gu, deps=[lg, lu] + loads_b +
                                         ([last_psab_read[pi]] if last_psab_read[pi]
                                          else []))
                            last_w1_user[bi] = gu
                            sl_ = sch.add("scalar", (lambda sc, pi=pi:
                                sc.activation(out=sg[pi][:], in_=ps_a[pi][:],
                                              func=AF.Silu)),
                                deps=[gu] + ([last_psab_read[pi]] if
                                             last_psab_read[pi] else []))
                            tmul = sch.add("vector", (lambda v, it=it, pi=pi:
                                v.tensor_tensor(out=tsb[it][:], in0=sg[pi][:],
                                                in1=ps_b[pi][:], op=ALU.mult)),
                                deps=[sl_, gu] +
                                ([last_t_reader[it]] if last_t_reader[it] else []))
                            last_psab_read[pi] = tmul
                            t_ops[it] = tmul
                        for it in range(11):
                            wdeps = ([last_w2_load[it]] if last_w2_load[it] else []) + \
                                    ([last_w2_user[it]] if last_w2_user[it] else [])
                            if phase == 0:
                                wsl = ap["wt_sw3"][it*128:(it+1)*128, :]
                                wdeps = wdeps + tb_done
                            else:
                                wsl = ap["wb_L1_ffn_w2"][e, it*128:(it+1)*128, :]
                            last_w2_load[it] = sch.add("gpsimd", (lambda s, wsl=wsl,
                                                                  it=it:
                                s.dma_start(out=w2r[it][:], in_=wsl)),
                                deps=wdeps, dma=True)
                    for tt in range(4):
                        for fch in range(4):
                            yi = (tt * 4 + fch) % 2
                            def f_ye(tn, tt=tt, fch=fch, yi=yi):
                                for it in range(11):
                                    last = tn.matmul(ps_y[yi][:],
                                        lhsT=tsb[it][:, tt*128:(tt+1)*128],
                                        rhs=w2r[it][:, fch*512:(fch+1)*512],
                                        start=(it == 0), stop=(it == 10))
                                return last
                            mmop = sch.add("tensor", f_ye,
                                deps=[t for t in t_ops if t is not None] +
                                     [last_w2_load[it] for it in range(11)] +
                                     ([last_psy_read[yi]] if last_psy_read[yi]
                                      else []))
                            for it in range(11):
                                last_t_reader[it] = mmop
                                last_w2_user[it] = mmop
                            if phase == 0:
                                ev = sch.add("vector", (lambda v, tt=tt, fch=fch,
                                                        yi=yi:
                                    v.tensor_copy(out=acc[tt][:, fch*512:(fch+1)*512],
                                                  in_=ps_y[yi][:])),
                                    deps=[mmop] +
                                    ([last_acc_reader[tt]] if last_acc_reader[tt]
                                     else []))
                            else:
                                ev = sch.add("vector", (lambda v, tt=tt, fch=fch, e=e,
                                                        yi=yi:
                                    v.scalar_tensor_tensor(
                                        out=acc[tt][:, fch*512:(fch+1)*512],
                                        in0=ps_y[yi][:], scalar=cw[:, tt*E1+e:tt*E1+e+1],
                                        in1=acc[tt][:, fch*512:(fch+1)*512],
                                        op0=ALU.mult, op1=ALU.add)),
                                    deps=[mmop, w6ops[tt]], self_edge=True)
                            last_psy_read[yi] = ev
                prev_a1 = None
                prev_a2 = None
                prev_so = None
                for tt in range(4):
                    xl = sch.add("sync", (lambda s, b=b, t0=t0, tt=tt: s.dma_start(
                        out=xtile[:], in_=ap["s_x1"][b, t0+tt*128:t0+(tt+1)*128, :])),
                        deps=([last_acc_reader[tt]] if last_acc_reader[tt] else []) +
                             ([prev_a1] if prev_a1 else []),
                        dma=True)
                    al = sch.add("sync", (lambda s, b=b, t0=t0, tt=tt: s.dma_start(
                        out=atile[:], in_=ap["s_atn"][b, t0+tt*128:t0+(tt+1)*128, :])),
                        deps=([last_acc_reader[tt]] if last_acc_reader[tt] else []) +
                             ([prev_a2] if prev_a2 else []),
                        dma=True)
                    a1 = sch.add("vector", (lambda v, tt=tt: v.tensor_tensor(
                        out=otile[:], in0=acc[tt][:], in1=xtile[:], op=ALU.add)),
                        deps=[xl] + [p for p in last_psy_read if p] +
                             ([last_out_store[tt]] if last_out_store[tt] else []) +
                             ([prev_so] if prev_so else []))
                    a2 = sch.add("vector", lambda v: v.tensor_tensor(
                        out=otile[:], in0=otile[:], in1=atile[:], op=ALU.add),
                        deps=[a1, al])
                    so = sch.add("sync", (lambda s, b=b, t0=t0, tt=tt: s.dma_start(
                        out=ap["out"][b, t0+tt*128:t0+(tt+1)*128, :], in_=otile[:])),
                        deps=[a2], dma=True)
                    prev_a1, prev_a2, prev_so = a1, a2, so
                    last_out_store[tt] = so
                    last_acc_reader[tt] = a1
                last_xnT_user = gu
        return sch.emit(bar, bar_val)


def build_program(phase_fns, tensors, host_inputs, outputs):
    nc = bass.Bass("TRN2", target_bir_lowering=False)
    ap = declare_dram(nc, tensors, host_inputs, outputs)
    es = ExitStack()
    with es:
        bar = es.enter_context(nc.semaphore("phbar"))
        v = 0
        for fn in phase_fns:
            v = fn(nc, ap, bar, v)
    return nc


def aux_arrays():
    return {
        "aux_ident": np.eye(128, dtype=BF),
        "aux_ones": np.ones((1, 128), dtype=BF),
        "aux_mask": np.triu(np.full((128, 128), NEG, dtype=np.float32), 1),
    }


def aux_vec_arrays(inputs):
    cb = np.ascontiguousarray(
        inputs["L0_attn_conv_b"].astype(np.float32).reshape(32, 128).T)
    cw = np.ascontiguousarray(
        inputs["L0_attn_conv_w"].astype(np.float32).reshape(32, 128, 4)
        .transpose(1, 0, 2).reshape(128, 128))
    dtb = np.ascontiguousarray(
        inputs["L0_attn_dt_proj_b"].astype(np.float32).reshape(32, 128).T)
    dv = np.ascontiguousarray(
        inputs["L0_attn_D"].astype(np.float32).reshape(32, 128).T)
    return {"aux_convb": cb, "aux_convw": cw, "aux_dtb": dtb, "aux_dvec": dv}


def _c(names):
    return names + ["wb_" + n for n in names]


LAUNCHES = [
    ("ln0",
     [lambda nc, ap, b, v: ph_ln(nc, ap, "x0", "s_xn0", "ln0", bar=b, bar_val=v)],
     ["x0", "s_xn0"], ["x0"], ["s_xn0"]),
    ("ffn0",
     [lambda nc, ap, b, v: ph_cast(nc, ap, CAST_LISTS["ffn0"], "c0", bar=b, bar_val=v),
      lambda nc, ap, b, v: ph_ffn0(nc, ap, bar=b, bar_val=v)],
     _c(CAST_LISTS["ffn0"]) + ["x0", "s_xn0", "aux_ones", "aux_ident", "s_x1acc"],
     CAST_LISTS["ffn0"] + ["x0", "s_xn0", "aux_ones", "aux_ident"],
     ["s_x1acc"]),
    ("mamba_in",
     [lambda nc, ap, b, v: ph_cast(nc, ap, CAST_LISTS["mamba_in"], "c1", bar=b, bar_val=v),
      lambda nc, ap, b, v: ph_prep_mamba(nc, ap, bar=b, bar_val=v),
      lambda nc, ap, b, v: ph_mamba_in(nc, ap, bar=b, bar_val=v)],
     _c(CAST_LISTS["mamba_in"]) +
     ["L0_attn_x_proj_w", "L0_attn_A_log", "aux_convw", "aux_convb",
      "wb_xproj_pad", "s_Aneg", "s_xn0", "s_gz", "s_xcf", "s_xcb", "s_dtlo", "s_bc"],
     CAST_LISTS["mamba_in"] +
     ["L0_attn_x_proj_w", "L0_attn_A_log", "aux_convw", "aux_convb",
      "s_xn0"],
     ["s_gz", "s_xcf", "s_xcb", "s_dtlo", "s_bc", "s_Aneg"]),
    ("mamba_scan",
     [lambda nc, ap, b, v: ph_cast(nc, ap, CAST_LISTS["mamba_scan"], "c2", bar=b, bar_val=v),
      lambda nc, ap, b, v: ph_mamba_scan(nc, ap, bar=b, bar_val=v)],
     _c(CAST_LISTS["mamba_scan"]) +
     ["aux_dtb", "aux_dvec", "s_Aneg", "s_dtlo", "s_bc", "s_xcf",
      "s_gz", "s_yg"],
     CAST_LISTS["mamba_scan"] +
     ["aux_dtb", "aux_dvec", "s_Aneg", "s_dtlo", "s_bc", "s_xcf", "s_gz"],
     ["s_yg"]),
    ("mamba_out",
     [lambda nc, ap, b, v: ph_cast(nc, ap, CAST_LISTS["mamba_out"], "c3", bar=b, bar_val=v),
      lambda nc, ap, b, v: ph_mamba_out(nc, ap, bar=b, bar_val=v)],
     _c(CAST_LISTS["mamba_out"]) + ["s_yg", "s_x1acc", "s_x1"],
     CAST_LISTS["mamba_out"] + ["s_yg", "s_x1acc"],
     ["s_x1"]),
    ("ln1",
     [lambda nc, ap, b, v: ph_ln(nc, ap, "s_x1", "s_xn1", "ln1", bar=b, bar_val=v)],
     ["s_x1", "s_xn1"], ["s_x1"], ["s_xn1"]),
    ("attn_pre",
     [lambda nc, ap, b, v: ph_cast(nc, ap,
          ["L1_attn_Wuq", "L1_attn_Wuk", "L1_attn_Wdq", "L1_attn_Wuv",
           "L1_attn_Wo"], "c4", bar=b, bar_val=v),
      lambda nc, ap, b, v: ph_attn_pre(nc, ap, bar=b, bar_val=v)],
     _c(["L1_attn_Wuq", "L1_attn_Wuk", "L1_attn_Wdq", "L1_attn_Wuv",
         "L1_attn_Wo"]) + ["s_kfull", "s_vfull"],
     ["L1_attn_Wuq", "L1_attn_Wuk", "L1_attn_Wdq", "L1_attn_Wuv", "L1_attn_Wo"],
     ["s_kfull", "s_vfull"]),
    ("attn_proj",
     [lambda nc, ap, b, v: ph_cast(nc, ap, ["L1_attn_Wdkv", "L1_attn_Wdq"],
                                   "c5", bar=b, bar_val=v),
      lambda nc, ap, b, v: ph_attn_proj(nc, ap, bar=b, bar_val=v)],
     _c(["L1_attn_Wdkv", "L1_attn_Wdq"]) +
     ["s_xn1", "s_ckvf", "s_qd"],
     ["L1_attn_Wdkv", "L1_attn_Wdq", "s_xn1"],
     ["s_ckvf", "s_qd"]),
    ("attn_core",
     [lambda nc, ap, b, v: ph_cast(nc, ap, ["L1_attn_Wuq"], "c6", bar=b, bar_val=v),
      lambda nc, ap, b, v: ph_attn_core(nc, ap, bar=b, bar_val=v)],
     _c(["L1_attn_Wuq"]) +
     ["s_kfull", "s_vfull", "s_ckvf", "s_qd", "aux_mask", "s_p", "s_atn"],
     ["L1_attn_Wuq", "s_kfull", "s_vfull", "s_ckvf", "s_qd", "aux_mask"],
     ["s_atn"]),
    ("ffn1",
     [lambda nc, ap, b, v: ph_cast(nc, ap, CAST_LISTS["ffn1"], "c7", bar=b, bar_val=v),
      lambda nc, ap, b, v: ph_prep_gate(nc, ap, bar=b, bar_val=v),
      lambda nc, ap, b, v: ph_ffn1(nc, ap, bar=b, bar_val=v)],
     _c(CAST_LISTS["ffn1"]) +
     ["L1_ffn_gate_w", "wb_gate_wT", "aux_ident", "s_xn1", "s_x1", "s_atn", "out",
      "wt_sw1", "wt_sw2", "wt_sw3"],
     CAST_LISTS["ffn1"] +
     ["L1_ffn_gate_w", "aux_ident", "s_xn1", "s_x1", "s_atn"],
     ["out"]),
]


def run_launches(state, launches, core_ids=None):
    if core_ids is None:
        core_ids = list(range(8))
    for name, fns, tensors, reads, writes in launches:
        nc = build_program(fns, tensors, set(reads), set(writes))
        in_maps = [{n: state[c][n] for n in reads} for c in range(len(core_ids))]
        res = run_bass_kernel_spmd(nc, in_maps, core_ids=core_ids)
        for c in range(len(core_ids)):
            for w in writes:
                state[c][w] = np.asarray(res.results[c][w])
    return state


CAST_EARLY = ["L0_ffn_W_base", "L0_ffn_b_base", "L0_ffn_A_s", "L0_ffn_B_s",
              "L0_ffn_A_r", "L0_ffn_B_r", "L0_ffn_W_router", "L0_ffn_b_router",
              "L0_attn_in_proj_w", "L0_attn_dt_proj_w"]
CAST_LATE = ["L0_attn_out_proj_w",
             "L1_attn_Wdq", "L1_attn_Wuq", "L1_attn_Wdkv", "L1_attn_Wuk",
             "L1_attn_Wuv", "L1_attn_Wo", "L1_ffn_w1", "L1_ffn_w2",
             "L1_ffn_sw1", "L1_ffn_sw2", "L1_ffn_sw3"]
ALL_CAST = CAST_EARLY + CAST_LATE

FUSED_PHASES = [
    lambda nc, ap, b, v: ph_cast(nc, ap, CAST_EARLY, "cA", bar=b, bar_val=v),
    lambda nc, ap, b, v: ph_prep_mamba(nc, ap, bar=b, bar_val=v),
    lambda nc, ap, b, v: ph_prep_gate(nc, ap, bar=b, bar_val=v),
    lambda nc, ap, b, v: ph_ln(nc, ap, "x0", "s_xn0", "ln0", bar=b, bar_val=v),
    lambda nc, ap, b, v: ph_ffn0(nc, ap, bar=b, bar_val=v),
    lambda nc, ap, b, v: ph_mamba_in(nc, ap, bar=b, bar_val=v),
    lambda nc, ap, b, v: ph_mamba_scan(nc, ap, bar=b, bar_val=v, cast_late=True),
    lambda nc, ap, b, v: ph_mamba_out(nc, ap, bar=b, bar_val=v),
    lambda nc, ap, b, v: ph_ln(nc, ap, "s_x1", "s_xn1", "ln1", bar=b, bar_val=v),
    lambda nc, ap, b, v: ph_attn_pre(nc, ap, bar=b, bar_val=v),
    lambda nc, ap, b, v: ph_attn_proj(nc, ap, bar=b, bar_val=v),
    lambda nc, ap, b, v: ph_attn_core(nc, ap, bar=b, bar_val=v),
    lambda nc, ap, b, v: ph_ffn1(nc, ap, bar=b, bar_val=v),
]

FUSED_READS = (list(WEIGHTS_F32) + ["x0", "aux_ident", "aux_ones", "aux_mask",
                                    "aux_convb", "aux_convw", "aux_dtb",
                                    "aux_dvec"])
FUSED_TENSORS = (FUSED_READS + ["out"] + list(SCRATCH) +
                 ["wb_" + n for n in ALL_CAST])


def make_state(inputs, n_cores=8):
    aux = aux_arrays()
    auxv = aux_vec_arrays(inputs)
    x = np.ascontiguousarray(inputs["L0_attn_hidden_states"])
    state = []
    for c in range(n_cores):
        d = dict(inputs)
        d.pop("L0_attn_hidden_states")
        d["x0"] = np.ascontiguousarray(x[c*B_LOC:(c+1)*B_LOC])
        d.update(aux)
        d.update(auxv)
        state.append(d)
    return state


def kernel(**inputs):
    n_cores = 8
    state = make_state(inputs, n_cores)
    nc = build_program(FUSED_PHASES, FUSED_TENSORS, set(FUSED_READS), {"out"})
    in_maps = [{n: state[c][n] for n in FUSED_READS} for c in range(n_cores)]
    res = run_bass_kernel_spmd(nc, in_maps, core_ids=list(range(n_cores)))
    return np.concatenate([np.asarray(res.results[c]["out"])
                           for c in range(n_cores)], axis=0)


def kernel_unfused(**inputs):
    n_cores = 8
    state = make_state(inputs, n_cores)
    run_launches(state, LAUNCHES)
    return np.concatenate([state[c]["out"] for c in range(n_cores)], axis=0)


# version 11 (final)
# speedup vs baseline: 1.1155x; 1.1155x over previous
import os
os.environ.setdefault("GAUGE_NTFF_IGNORE_DMA", "1")
import numpy as np
import ml_dtypes
import concourse.bass as bass
import concourse.mybir as mybir
from concourse.bass_utils import run_bass_kernel_spmd
from contextlib import ExitStack

F32 = mybir.dt.float32
BF16 = mybir.dt.bfloat16
AF = mybir.ActivationFunctionType
ALU = mybir.AluOpType
AX = mybir.AxisListType
BF = ml_dtypes.bfloat16

B_LOC = 2
T = 2048
D = 2048
DI = 4096
DS = 16
DTR = 128
DCONV = 4
E0 = 6
R0 = 16
E1 = 8
I1 = 1408
LQ = 512
LKV = 512
NH = 16
HS = 128
NEG = -3.0e38
EPS_LN = 1e-5
RSQ_HS = 0.08838834764831845

WEIGHTS_F32 = {
    "L0_attn_in_proj_w": (2*DI, D), "L0_attn_conv_w": (DI, DCONV),
    "L0_attn_conv_b": (DI,), "L0_attn_x_proj_w": (DTR+2*DS, DI),
    "L0_attn_dt_proj_w": (DI, DTR), "L0_attn_dt_proj_b": (DI,),
    "L0_attn_A_log": (DI, DS), "L0_attn_D": (DI,),
    "L0_attn_out_proj_w": (D, DI),
    "L0_ffn_W_base": (D, D), "L0_ffn_b_base": (D,),
    "L0_ffn_A_s": (D, R0), "L0_ffn_B_s": (R0, D),
    "L0_ffn_A_r": (E0, D, R0), "L0_ffn_B_r": (E0, R0, D),
    "L0_ffn_W_router": (D, E0), "L0_ffn_b_router": (E0,),
    "L1_attn_Wdq": (LQ, D), "L1_attn_Wuq": (D, LQ), "L1_attn_Wdkv": (LKV, D),
    "L1_attn_Wuk": (D, LKV), "L1_attn_Wuv": (D, LKV), "L1_attn_Wo": (D, D),
    "L1_ffn_gate_w": (E1, D), "L1_ffn_w1": (E1, D, 2*I1),
    "L1_ffn_w2": (E1, I1, D), "L1_ffn_sw1": (I1, D), "L1_ffn_sw2": (I1, D),
    "L1_ffn_sw3": (D, I1),
}

SCRATCH = {
    "s_xn0":  ((B_LOC, T, D), BF16),
    "s_x1acc": ((B_LOC, T, D), F32),
    "s_gz":   ((B_LOC, DI, T), F32),
    "s_xcf":  ((B_LOC, DI, T), F32),
    "s_xcb":  ((B_LOC, DI, T), BF16),
    "s_bc":   ((B_LOC, 2*DS, T), BF16),
    "s_dtlo": ((B_LOC, DTR, T), BF16),
    "s_yg":   ((B_LOC, DI, T), BF16),
    "s_x1":   ((B_LOC, T, D), F32),
    "s_xn1":  ((B_LOC, T, D), BF16),
    "s_atn":  ((B_LOC, T, D), F32),
    "s_Aneg": ((DI, DS), F32),
    "wb_xproj_pad": ((256, DI), BF16),
    "wb_gate_wT": ((D, E1), BF16),
    "s_kfull": ((D, LKV), BF16),
    "s_vfull": ((LKV, D), BF16),
    "s_ckvf": ((B_LOC, LKV, T), BF16),
    "s_ckvt": ((B_LOC, T, LKV), BF16),
    "s_qd":   ((B_LOC, LQ, T), BF16),
    "s_p":    ((T, T), BF16),
    "wt_sw1": ((D, I1), BF16),
    "wt_sw2": ((D, I1), BF16),
    "wt_sw3": ((I1, D), BF16),
}

CAST_LISTS = {
    "ln0": [],
    "ffn0": ["L0_ffn_W_base", "L0_ffn_b_base", "L0_ffn_A_s", "L0_ffn_B_s",
             "L0_ffn_A_r", "L0_ffn_B_r", "L0_ffn_W_router", "L0_ffn_b_router"],
    "mamba_in": ["L0_attn_in_proj_w"],
    "mamba_scan": ["L0_attn_dt_proj_w"],
    "mamba_out": ["L0_attn_out_proj_w"],
    "ln1": [],
    "attn": ["L1_attn_Wdq", "L1_attn_Wuq", "L1_attn_Wdkv", "L1_attn_Wuk",
             "L1_attn_Wuv", "L1_attn_Wo"],
    "ffn1": ["L1_ffn_w1", "L1_ffn_w2", "L1_ffn_sw1", "L1_ffn_sw2",
             "L1_ffn_sw3"],
}


def cdiv(a, b):
    return (a + b - 1) // b


ENGINES = ("sync", "gpsimd", "scalar", "vector", "tensor")


class Sched:
    def __init__(self, nc, es, tag):
        self.nc = nc
        self.tag = tag
        self.ops = []
        self.per_eng = {e: [] for e in ENGINES}
        self.sems = {e: es.enter_context(nc.semaphore(f"{tag}_{e}")) for e in ENGINES}

    def add(self, eng, fn, deps=(), dma=False, self_edge=False):
        oid = len(self.ops)
        self.ops.append([eng, fn, tuple(deps), self_edge, 16 if dma else 1, 0])
        self.per_eng[eng].append(oid)
        return oid

    def emit(self, bar=None, bar_val=0):
        for e in ENGINES:
            c = 0
            for oid in self.per_eng[e]:
                c += self.ops[oid][4]
                self.ops[oid][5] = c
        finals = {e: (self.ops[self.per_eng[e][-1]][5] if self.per_eng[e] else 0)
                  for e in ENGINES}

        def engine_body(eng_name):
            def body(eng):
                if bar is not None and bar_val > 0:
                    eng.wait_ge(bar, bar_val)
                watermark = {e: 0 for e in ENGINES}
                for oid in self.per_eng[eng_name]:
                    _, fn, deps, self_edge, _, _ = self.ops[oid]
                    for d in deps:
                        de, dv = self.ops[d][0], self.ops[d][5]
                        if watermark[de] < dv:
                            eng.wait_ge(self.sems[de], dv)
                            watermark[de] = dv
                    inst = fn(eng)
                    inst.then_inc(self.sems[eng_name], self.ops[oid][4])
                if bar is not None:
                    eng.wait_ge(self.sems[eng_name], finals[eng_name])
                    eng.sem_inc(bar, 1)
            return body

        n_part = 0
        with self.nc.Block(self.tag) as blk:
            for e in ENGINES:
                if self.per_eng[e]:
                    getattr(blk, e)(engine_body(e))
                    n_part += 1
        return bar_val + n_part


def declare_dram(nc, names, host_inputs, outputs):
    aps = {}
    for n in sorted(set(names)):
        if n in WEIGHTS_F32:
            shape, dt = WEIGHTS_F32[n], F32
        elif n.startswith("wb_") and n[3:] in WEIGHTS_F32:
            shape, dt = WEIGHTS_F32[n[3:]], BF16
        elif n in SCRATCH:
            shape, dt = SCRATCH[n]
        elif n in ("x0", "out"):
            shape, dt = (B_LOC, T, D), F32
        elif n == "aux_ident":
            shape, dt = (128, 128), BF16
        elif n == "aux_ones":
            shape, dt = (1, 128), BF16
        elif n == "aux_mask":
            shape, dt = (128, 128), F32
        elif n == "aux_convb":
            shape, dt = (128, 32), F32
        elif n == "aux_convw":
            shape, dt = (128, 128), F32
        elif n == "aux_dtb":
            shape, dt = (128, 32), F32
        elif n == "aux_dvec":
            shape, dt = (128, 32), F32
        else:
            raise KeyError(n)
        kind = ("ExternalInput" if n in host_inputs else
                "ExternalOutput" if n in outputs else "Internal")
        aps[n] = nc.dram_tensor(n, list(shape), dt, kind=kind).ap()
    return aps


def cast_jobs(ap, names, FW=8192):
    jobs = []
    for n in names:
        shape = WEIGHTS_F32[n]
        src_, dst_ = ap[n], ap["wb_" + n]
        N = 1
        for d in shape:
            N *= d
        pat = " ".join(chr(97 + i) for i in range(len(shape)))
        s1 = src_.rearrange(f"{pat} -> ({pat})")
        d1 = dst_.rearrange(f"{pat} -> ({pat})")
        if N % 128:
            jobs.append((s1.rearrange("(o a) -> o a", o=1),
                         d1.rearrange("(o a) -> o a", o=1), 1, N))
            continue
        rem = N
        off = 0
        while rem:
            f = min(FW, rem // 128)
            take = 128 * f
            jobs.append((s1[off:off+take].rearrange("(p f) -> p f", p=128),
                         d1[off:off+take].rearrange("(p f) -> p f", p=128),
                         128, f))
            off += take
            rem -= take
    return jobs


def emit_cast_job(sch, bufs, prev_store, i, job):
    s2, d2, rr, cc = job
    nb = len(bufs)
    bi = i % nb
    deps = [prev_store[bi]] if prev_store[bi] is not None else []
    lo = sch.add("gpsimd",
                 (lambda g, s2=s2, rr=rr, cc=cc, bi=bi:
                  g.dma_start(out=bufs[bi][0:rr, 0:cc], in_=s2)),
                 deps=deps, dma=True)
    st = sch.add("sync",
                 (lambda s, d2=d2, rr=rr, cc=cc, bi=bi:
                  s.dma_start(out=d2, in_=bufs[bi][0:rr, 0:cc])),
                 deps=[lo], dma=True)
    prev_store[bi] = st
    return st


def ph_cast(nc, ap, names, tag="cast", bar=None, bar_val=0):
    jobs = cast_jobs(ap, names)
    if not jobs:
        return bar_val
    maxC = max(c for (_, _, _, c) in jobs)
    es = ExitStack()
    with es:
        bufs = [es.enter_context(nc.sbuf_tensor(f"{tag}_b{i}", [128, maxC], BF16))
                for i in range(4)]
        sch = Sched(nc, es, tag)
        prev_store = [None] * 4
        for i, job in enumerate(jobs):
            emit_cast_job(sch, bufs, prev_store, i, job)
        return sch.emit(bar, bar_val)


def ph_prep_mamba(nc, ap, bar=None, bar_val=0):
    es = ExitStack()
    with es:
        al = es.enter_context(nc.sbuf_tensor("pm_al", [128, DS * (DI // 128)], F32))
        an = es.enter_context(nc.sbuf_tensor("pm_an", [128, DS * (DI // 128)], F32))
        x1t = es.enter_context(nc.sbuf_tensor("pm_x1", [128, DI], BF16))
        x2t = es.enter_context(nc.sbuf_tensor("pm_x2", [96, DI], BF16))
        sch = Sched(nc, es, "prepm")
        nt = DI // 128
        lo = sch.add("sync", lambda s: s.dma_start(
            out=al[:].rearrange("p (n s) -> p n s", n=nt),
            in_=ap["L0_attn_A_log"].rearrange("(n p) s -> p n s", p=128)), dma=True)
        ex = sch.add("scalar", lambda sc: sc.activation(out=an[:], in_=al[:], func=AF.Exp),
                     deps=[lo])
        ng = sch.add("vector", lambda v: v.tensor_scalar_mul(an[:], an[:], -1.0), deps=[ex])
        sch.add("sync", lambda s: s.dma_start(
            out=ap["s_Aneg"].rearrange("(n p) s -> p n s", p=128),
            in_=an[:].rearrange("p (n s) -> p n s", n=nt)), deps=[ng], dma=True)
        c1 = sch.add("gpsimd", lambda g: g.dma_start(
            out=x1t[:], in_=ap["L0_attn_x_proj_w"][0:128, :]), dma=True)
        c2 = sch.add("gpsimd", lambda g: g.dma_start(
            out=x2t[0:32, :], in_=ap["L0_attn_x_proj_w"][128:160, :]), dma=True)
        z1a = sch.add("vector", lambda v: v.memset(x2t[32:64, :], 0.0))
        z1 = sch.add("vector", lambda v: v.memset(x2t[64:96, :], 0.0), deps=[z1a])
        sch.add("sync", lambda s: s.dma_start(out=ap["wb_xproj_pad"][0:128, :], in_=x1t[:]),
                deps=[c1], dma=True)
        sch.add("sync", lambda s: s.dma_start(out=ap["wb_xproj_pad"][128:224, :],
                                              in_=x2t[:]), deps=[c2, z1], dma=True)
        sch.add("sync", lambda s: s.dma_start(out=ap["wb_xproj_pad"][224:256, :],
                                              in_=x2t[32:64, :]), deps=[z1], dma=True)
        return sch.emit(bar, bar_val)


def ph_prep_gate(nc, ap, bar=None, bar_val=0):
    es = ExitStack()
    with es:
        gwb = es.enter_context(nc.sbuf_tensor("pg_gw", [E1, D], BF16))
        idn = es.enter_context(nc.sbuf_tensor("pg_id", [128, 128], BF16))
        gwt = es.enter_context(nc.sbuf_tensor("pg_gwt", [128, E1 * (D // 128)], BF16))
        pst = es.enter_context(nc.psum_tensor("pg_ps", [128, E1], BF16))
        sch = Sched(nc, es, "prepg")
        lg = sch.add("gpsimd", lambda g: g.dma_start(out=gwb[:], in_=ap["L1_ffn_gate_w"]),
                     dma=True)
        li = sch.add("sync", lambda s: s.dma_start(out=idn[:], in_=ap["aux_ident"]),
                     dma=True)
        prev_ev = None
        for c in range(D // 128):
            deps = [lg, li] + ([prev_ev] if prev_ev is not None else [])
            tr = sch.add("tensor", (lambda tn, c=c: tn.transpose(
                pst[:], in_=gwb[0:E1, c*128:(c+1)*128], identity=idn[0:E1, 0:E1])),
                deps=deps)
            ev = sch.add("vector", (lambda v, c=c: v.tensor_copy(
                out=gwt[:, c*E1:(c+1)*E1], in_=pst[:])), deps=[tr])
            prev_ev = ev
        sch.add("sync", lambda s: s.dma_start(
            out=ap["wb_gate_wT"].rearrange("(n p) e -> p n e", p=128),
            in_=gwt[:].rearrange("p (n e) -> p n e", e=E1)), deps=[prev_ev], dma=True)
        return sch.emit(bar, bar_val)


def ph_ln(nc, ap, src_name, dst_name, tag, bar=None, bar_val=0):
    es = ExitStack()
    with es:
        xt = [es.enter_context(nc.sbuf_tensor(f"{tag}_x{i}", [128, D], F32))
              for i in range(2)]
        xnb = [es.enter_context(nc.sbuf_tensor(f"{tag}_o{i}", [128, D], BF16))
               for i in range(2)]
        sq = es.enter_context(nc.sbuf_tensor(f"{tag}_sq", [128, D], F32))
        st1 = es.enter_context(nc.sbuf_tensor(f"{tag}_s1", [128, 8], F32))
        cst = es.enter_context(nc.sbuf_tensor(f"{tag}_cst", [128, 1], F32))
        sch = Sched(nc, es, tag)
        cs_op = sch.add("vector", lambda v: v.memset(cst[:], -0.5))
        src, dst = ap[src_name], ap[dst_name]
        prev_use = [None, None]
        prev_store = [None, None]
        for b in range(B_LOC):
            for it in range(T // 128):
                i = b * (T // 128) + it
                bi = i % 2
                deps = [prev_use[bi]] if prev_use[bi] is not None else []
                lo = sch.add("sync", (lambda s, b=b, it=it, bi=bi: s.dma_start(
                    out=xt[bi][:], in_=src[b, it*128:(it+1)*128, :])),
                    deps=deps, dma=True)
                rs = sch.add("vector", (lambda v, bi=bi: v.reduce_sum(
                    out=st1[:, 0:1], in_=xt[bi][:], axis=AX.X)), deps=[lo])
                ss = sch.add("scalar", (lambda sc, bi=bi: sc.activation(
                    out=sq[:], in_=xt[bi][:], func=AF.Square,
                    accum_out=st1[:, 1:2])), deps=[lo])
                nm = sch.add("vector", lambda v: v.tensor_scalar_mul(
                    st1[:, 2:3], st1[:, 0:1], -1.0 / D), deps=[rs])
                ms = sch.add("vector", lambda v: v.tensor_scalar_mul(
                    st1[:, 3:4], st1[:, 1:2], 1.0 / D), deps=[ss])
                m2 = sch.add("vector", lambda v: v.tensor_tensor(
                    out=st1[:, 4:5], in0=st1[:, 2:3], in1=st1[:, 2:3],
                    op=ALU.mult), deps=[nm])
                vr = sch.add("vector", lambda v: v.tensor_scalar(
                    out=st1[:, 5:6], in0=st1[:, 3:4], scalar1=st1[:, 4:5],
                    scalar2=EPS_LN, op0=ALU.subtract, op1=ALU.add),
                    deps=[ms, m2], self_edge=True)
                lv = sch.add("scalar", lambda sc: sc.activation(
                    out=st1[:, 6:7], in_=st1[:, 5:6], func=AF.Ln),
                    deps=[vr])
                rst = sch.add("scalar", lambda sc: sc.activation(
                    out=st1[:, 7:8], in_=st1[:, 6:7], func=AF.Exp,
                    scale=cst[:, 0:1]), deps=[lv, cs_op], self_edge=True)
                dep_s = [prev_store[bi]] if prev_store[bi] is not None else []
                ap_op = sch.add("vector", (lambda v, bi=bi: v.tensor_scalar(
                    out=xnb[bi][:], in0=xt[bi][:], scalar1=st1[:, 2:3],
                    scalar2=st1[:, 7:8], op0=ALU.add, op1=ALU.mult)),
                    deps=[rst, nm] + dep_s, self_edge=True)
                so = sch.add("sync", (lambda s, b=b, it=it, bi=bi: s.dma_start(
                    out=dst[b, it*128:(it+1)*128, :], in_=xnb[bi][:])),
                    deps=[ap_op], dma=True)
                prev_use[bi] = ap_op
                prev_store[bi] = so
        return sch.emit(bar, bar_val)


def ph_ffn0(nc, ap, bar=None, bar_val=0):
    NT = T // 128
    KC = D // 128
    es = ExitStack()
    with es:
        xnT = [es.enter_context(nc.sbuf_tensor(f"f0_xnT{c}", [128, T], BF16))
               for c in range(KC)]
        wb = [es.enter_context(nc.sbuf_tensor(f"f0_wb{c}", [128, D], BF16))
              for c in range(KC)]
        asb = es.enter_context(nc.sbuf_tensor("f0_as", [128, KC * R0], BF16))
        bsb = es.enter_context(nc.sbuf_tensor("f0_bs", [R0, D], BF16))
        wrb = es.enter_context(nc.sbuf_tensor("f0_wr", [128, KC * E0], BF16))
        arb = es.enter_context(nc.sbuf_tensor("f0_ar", [128, E0 * KC * R0], BF16))
        brb = es.enter_context(nc.sbuf_tensor("f0_br", [R0, E0 * D], BF16))
        bbr = es.enter_context(nc.sbuf_tensor("f0_bb", [1, D], BF16))
        brr = es.enter_context(nc.sbuf_tensor("f0_brr", [1, E0], BF16))
        ones = es.enter_context(nc.sbuf_tensor("f0_ones", [1, 128], BF16))
        idn = es.enter_context(nc.sbuf_tensor("f0_id", [128, 128], BF16))
        xtile = es.enter_context(nc.sbuf_tensor("f0_xt", [128, D], F32))
        otile = es.enter_context(nc.sbuf_tensor("f0_ot", [128, D], F32))
        s1b = es.enter_context(nc.sbuf_tensor("f0_s1", [128, R0], BF16))
        s1T = es.enter_context(nc.sbuf_tensor("f0_s1T", [R0, 128], BF16))
        gsb = es.enter_context(nc.sbuf_tensor("f0_g", [128, 16], F32))
        st = es.enter_context(nc.sbuf_tensor("f0_st", [128, 8], F32))
        g2 = es.enter_context(nc.sbuf_tensor("f0_g2", [128, E0], F32))
        keep = es.enter_context(nc.sbuf_tensor("f0_keep", [128, E0], F32))
        w6 = es.enter_context(nc.sbuf_tensor("f0_w6", [128, E0], F32))
        hwb = es.enter_context(nc.sbuf_tensor("f0_hw", [128, E0 * R0], BF16))
        hwT = es.enter_context(nc.sbuf_tensor("f0_hwT", [R0, E0 * 128], BF16))
        ps_main = es.enter_context(nc.psum_tensor("f0_pm", [128, D], F32))
        ps_s1 = es.enter_context(nc.psum_tensor("f0_ps1", [128, R0], F32))
        ps_rt = es.enter_context(nc.psum_tensor("f0_prt", [128, E0], F32))
        ps_h6 = es.enter_context(nc.psum_tensor("f0_ph6", [128, E0 * R0], F32))
        ps_tr = es.enter_context(nc.psum_tensor("f0_ptr", [R0, 128], BF16))
        sch = Sched(nc, es, "ffn0")

        wl = []
        for c in range(KC):
            wl.append(sch.add("sync", (lambda s, c=c: s.dma_start(
                out=wb[c][:], in_=ap["wb_L0_ffn_W_base"][c*128:(c+1)*128, :])), dma=True))
        wl.append(sch.add("sync", lambda s: s.dma_start(
            out=asb[:].rearrange("p (n r) -> p n r", r=R0),
            in_=ap["wb_L0_ffn_A_s"].rearrange("(n p) r -> p n r", p=128)), dma=True))
        wl.append(sch.add("sync", lambda s: s.dma_start(
            out=bsb[:], in_=ap["wb_L0_ffn_B_s"]), dma=True))
        wl.append(sch.add("sync", lambda s: s.dma_start(
            out=wrb[:].rearrange("p (n e) -> p n e", e=E0),
            in_=ap["wb_L0_ffn_W_router"].rearrange("(n p) e -> p n e", p=128)), dma=True))
        wl.append(sch.add("sync", lambda s: s.dma_start(
            out=arb[:].rearrange("p (e n r) -> p e n r", e=E0, r=R0),
            in_=ap["wb_L0_ffn_A_r"].rearrange("e (n p) r -> p e n r", p=128)), dma=True))
        wl.append(sch.add("sync", lambda s: s.dma_start(
            out=brb[:].rearrange("r (e c) -> r e c", e=E0),
            in_=ap["wb_L0_ffn_B_r"].rearrange("e r c -> r e c")), dma=True))
        wl.append(sch.add("sync", lambda s: s.dma_start(
            out=bbr[:], in_=ap["wb_L0_ffn_b_base"].rearrange("(o a) -> o a", o=1)), dma=True))
        wl.append(sch.add("sync", lambda s: s.dma_start(
            out=brr[:], in_=ap["wb_L0_ffn_b_router"].rearrange("(o a) -> o a", o=1)), dma=True))
        wl.append(sch.add("sync", lambda s: s.dma_start(out=ones[:], in_=ap["aux_ones"]),
                          dma=True))
        wl.append(sch.add("sync", lambda s: s.dma_start(out=idn[:], in_=ap["aux_ident"]),
                          dma=True))

        prev_evac = None
        xnT_load = [None] * KC
        last_xnT_user = None
        last_s1_read = None
        last_rt_read = None
        last_h6_read = [None] * E0
        last_tr_read = None
        for b in range(B_LOC):
            for c in range(KC):
                deps = [] if xnT_load[c] is None else [xnT_load[c]]
                if last_xnT_user is not None:
                    deps.append(last_xnT_user)
                xnT_load[c] = sch.add("scalar", (lambda s, b=b, c=c: s.dma_start_transpose(
                    xnT[c][:], in_=ap["s_xn0"][b, 0:T, c*128:(c+1)*128])),
                    deps=deps, dma=True)
            loads_b = list(xnT_load)
            for tt in range(NT):
                pm_dep = [prev_evac] if prev_evac is not None else []
                def f_base(tn, b=b, tt=tt):
                    for nchunk in range(4):
                        for c in range(KC):
                            tn.matmul(ps_main[:, nchunk*512:(nchunk+1)*512],
                                      lhsT=xnT[c][:, tt*128:(tt+1)*128],
                                      rhs=wb[c][:, nchunk*512:(nchunk+1)*512],
                                      start=(c == 0), stop=False)
                        last = tn.matmul(ps_main[:, nchunk*512:(nchunk+1)*512],
                                         lhsT=ones[:, 0:128],
                                         rhs=bbr[:, nchunk*512:(nchunk+1)*512],
                                         start=False, stop=False)
                    return last
                base = sch.add("tensor", f_base, deps=wl + loads_b + pm_dep)
                def f_s1(tn, tt=tt):
                    for c in range(KC):
                        last = tn.matmul(ps_s1[:], lhsT=xnT[c][:, tt*128:(tt+1)*128],
                                         rhs=asb[:, c*R0:(c+1)*R0],
                                         start=(c == 0), stop=(c == KC-1))
                    return last
                s1mm = sch.add("tensor", f_s1, deps=wl + loads_b +
                               ([last_s1_read] if last_s1_read else []))
                s1ev = sch.add("vector", lambda v: v.tensor_copy(out=s1b[:], in_=ps_s1[:]),
                               deps=[s1mm])
                last_s1_read = s1ev
                s1tr = sch.add("tensor", lambda tn: tn.transpose(
                    ps_tr[0:R0, :], in_=s1b[:], identity=idn[:]),
                    deps=[s1ev] + ([last_tr_read] if last_tr_read else []))
                s1tc = sch.add("vector", lambda v: v.tensor_copy(out=s1T[:], in_=ps_tr[0:R0, :]),
                               deps=[s1tr])
                def f_s2(tn):
                    for nchunk in range(4):
                        last = tn.matmul(ps_main[:, nchunk*512:(nchunk+1)*512],
                                         lhsT=s1T[0:R0, :],
                                         rhs=bsb[0:R0, nchunk*512:(nchunk+1)*512],
                                         start=False, stop=False)
                    return last
                s2mm = sch.add("tensor", f_s2, deps=[s1tc, base])
                def f_rt(tn, tt=tt):
                    for c in range(KC):
                        tn.matmul(ps_rt[:], lhsT=xnT[c][:, tt*128:(tt+1)*128],
                                  rhs=wrb[:, c*E0:(c+1)*E0], start=(c == 0), stop=False)
                    return tn.matmul(ps_rt[:], lhsT=ones[:, 0:128], rhs=brr[:],
                                     start=False, stop=True)
                rtmm = sch.add("tensor", f_rt, deps=wl + loads_b +
                               ([last_rt_read] if last_rt_read else []))
                mx = sch.add("vector", lambda v: v.reduce_max(
                    out=st[:, 0:1], in_=ps_rt[:], axis=AX.X, negate=True), deps=[rtmm])
                ex = sch.add("scalar", lambda sc: sc.activation(
                    out=gsb[:, 0:E0], in_=ps_rt[:], func=AF.Exp, bias=st[:, 0:1],
                    accum_out=st[:, 1:2]), deps=[mx])
                last_rt_read = ex
                rc = sch.add("vector", lambda v: v.reciprocal(
                    out=st[:, 2:3], in_=st[:, 1:2]), deps=[ex])
                gg = sch.add("vector", lambda v: v.tensor_scalar_mul(
                    gsb[:, 0:E0], gsb[:, 0:E0], st[:, 2:3]), deps=[rc, ex],
                    self_edge=True)
                m1 = sch.add("vector", lambda v: v.reduce_max(
                    out=st[:, 3:4], in_=gsb[:, 0:E0], axis=AX.X), deps=[gg])
                e1_ = sch.add("vector", lambda v: v.tensor_scalar(
                    out=g2[:], in0=gsb[:, 0:E0], scalar1=st[:, 3:4], scalar2=-1e30,
                    op0=is_ge_op(), op1=ALU.mult), deps=[m1], self_edge=True)
                g2s = sch.add("vector", lambda v: v.tensor_tensor(
                    out=g2[:], in0=gsb[:, 0:E0], in1=g2[:], op=ALU.add), deps=[e1_])
                m2 = sch.add("vector", lambda v: v.reduce_max(
                    out=st[:, 4:5], in_=g2[:], axis=AX.X), deps=[g2s])
                kp = sch.add("vector", lambda v: v.tensor_single_scalar(
                    out=keep[:], in_=gsb[:, 0:E0], scalar=st[:, 4:5], op=is_ge_op()),
                    deps=[m2], self_edge=True)
                w6op = sch.add("vector", lambda v: v.tensor_tensor(
                    out=w6[:], in0=gsb[:, 0:E0], in1=keep[:], op=ALU.mult), deps=[kp])
                def f_h6(tn, tt=tt):
                    for e in range(E0):
                        for c in range(KC):
                            last = tn.matmul(
                                ps_h6[:, e*R0:(e+1)*R0],
                                lhsT=xnT[c][:, tt*128:(tt+1)*128],
                                rhs=arb[:, (e*KC+c)*R0:(e*KC+c+1)*R0],
                                start=(c == 0), stop=(c == KC-1))
                    return last
                h6mm = sch.add("tensor", f_h6, deps=wl + loads_b +
                               [d for d in last_h6_read if d is not None])
                hw_ops = []
                for e in range(E0):
                    hw_ops.append(sch.add("vector", (lambda v, e=e: v.tensor_scalar_mul(
                        hwb[:, e*R0:(e+1)*R0], ps_h6[:, e*R0:(e+1)*R0],
                        w6[:, e:e+1])), deps=[h6mm, w6op], self_edge=True))
                last_h6_read = list(hw_ops)
                prev_trc = s1tc
                tr_ops = []
                for e in range(E0):
                    tre = sch.add("tensor", (lambda tn, e=e: tn.transpose(
                        ps_tr[0:R0, :], in_=hwb[:, e*R0:(e+1)*R0], identity=idn[:])),
                        deps=[hw_ops[e], prev_trc])
                    trc = sch.add("vector", (lambda v, e=e: v.tensor_copy(
                        out=hwT[:, e*128:(e+1)*128], in_=ps_tr[0:R0, :])), deps=[tre])
                    prev_trc = trc
                    tr_ops.append(trc)
                last_tr_read = prev_trc
                def f_rout(tn):
                    for e in range(E0):
                        for nchunk in range(4):
                            last = tn.matmul(
                                ps_main[:, nchunk*512:(nchunk+1)*512],
                                lhsT=hwT[0:R0, e*128:(e+1)*128],
                                rhs=brb[0:R0, e*D+nchunk*512:e*D+(nchunk+1)*512],
                                start=False, stop=(e == E0-1))
                    return last
                rout = sch.add("tensor", f_rout, deps=tr_ops + [s2mm])
                xl = sch.add("sync", (lambda s, b=b, tt=tt: s.dma_start(
                    out=xtile[:], in_=ap["x0"][b, tt*128:(tt+1)*128, :])),
                    deps=pm_dep, dma=True)
                ev = sch.add("vector", lambda v: v.tensor_tensor(
                    out=otile[:], in0=ps_main[:], in1=xtile[:], op=ALU.add),
                    deps=[rout, xl])
                sout = sch.add("sync", (lambda s, b=b, tt=tt: s.dma_start(
                    out=ap["s_x1acc"][b, tt*128:(tt+1)*128, :], in_=otile[:])),
                    deps=[ev], dma=True)
                prev_evac = sout
                last_xnT_user = h6mm
        return sch.emit(bar, bar_val)


def is_ge_op():
    return ALU.is_ge


def ph_mamba_in(nc, ap, bar=None, bar_val=0):
    KC = D // 128
    es = ExitStack()
    with es:
        xnT = [es.enter_context(nc.sbuf_tensor(f"mi_xnT{c}", [128, T], BF16))
               for c in range(KC)]
        wT = [[es.enter_context(nc.sbuf_tensor(f"mi_wT{s}_{c}", [128, 512], BF16))
               for c in range(KC)] for s in range(2)]
        cwsb = es.enter_context(nc.sbuf_tensor("mi_cw", [128, 32 * DCONV], F32))
        cbsb = es.enter_context(nc.sbuf_tensor("mi_cb", [128, 32], F32))
        xr = es.enter_context(nc.sbuf_tensor("mi_xr", [128, T], F32))
        cv = es.enter_context(nc.sbuf_tensor("mi_cv", [128, T], F32))
        xcf = es.enter_context(nc.sbuf_tensor("mi_xcf", [128, T], F32))
        xcb = es.enter_context(nc.sbuf_tensor("mi_xcb", [128, T], BF16))
        gzt = es.enter_context(nc.sbuf_tensor("mi_gz", [128, T], F32))
        xpj = [es.enter_context(nc.sbuf_tensor(f"mi_xpj{i}", [128, 256], BF16))
               for i in range(2)]
        xcbl = [es.enter_context(nc.sbuf_tensor(f"mi_xcbl{i}", [128, T], BF16))
                for i in range(2)]
        dtb = es.enter_context(nc.sbuf_tensor("mi_dtb", [128, T], BF16))
        bcb = es.enter_context(nc.sbuf_tensor("mi_bcb", [32, T], BF16))
        ps = [es.enter_context(nc.psum_tensor(f"mi_ps{i}", [128, T], F32))
              for i in range(2)]
        sch = Sched(nc, es, "mbin")

        cw_l = sch.add("sync", lambda s: s.dma_start(
            out=cwsb[:], in_=ap["aux_convw"]), dma=True)
        cb_l = sch.add("sync", lambda s: s.dma_start(
            out=cbsb[:], in_=ap["aux_convb"]), dma=True)

        xnT_load = [None] * KC
        last_xnT_user = None
        wT_load = [[None] * KC for _ in range(2)]
        last_wT_user = [None, None]
        last_ps_reader = [None, None]
        last_xr_reader = None
        last_cv_reader = None
        last_xcf_store = None
        last_xcb_store = None
        last_gz_store = None
        for b in range(B_LOC):
            for c in range(KC):
                deps = ([xnT_load[c]] if xnT_load[c] else []) + \
                       ([last_xnT_user] if last_xnT_user else [])
                xnT_load[c] = sch.add("scalar", (lambda s, b=b, c=c: s.dma_start_transpose(
                    xnT[c][:], in_=ap["s_xn0"][b, 0:T, c*128:(c+1)*128])),
                    deps=deps, dma=True)
            loads_b = list(xnT_load)
            for m in range(16):
                mset = m % 2
                for c in range(KC):
                    deps = ([wT_load[mset][c]] if wT_load[mset][c] else []) + \
                           ([last_wT_user[mset]] if last_wT_user[mset] else [])
                    wT_load[mset][c] = sch.add("scalar",
                        (lambda s, m=m, c=c, mset=mset: s.dma_start_transpose(
                            wT[mset][c][:],
                            in_=ap["wb_L0_attn_in_proj_w"][m*512:(m+1)*512,
                                                           c*128:(c+1)*128])),
                        deps=deps, dma=True)
                wloads = list(wT_load[mset])
                for et in range(4):
                    g = m * 4 + et
                    pi = g % 2
                    def f_mm(tn, et=et, mset=mset, pi=pi):
                        for nch in range(4):
                            for c in range(KC):
                                last = tn.matmul(
                                    ps[pi][:, nch*512:(nch+1)*512],
                                    lhsT=wT[mset][c][:, et*128:(et+1)*128],
                                    rhs=xnT[c][:, nch*512:(nch+1)*512],
                                    start=(c == 0), stop=(c == KC-1))
                        return last
                    mm = sch.add("tensor", f_mm, deps=wloads + loads_b +
                                 ([last_ps_reader[pi]] if last_ps_reader[pi] else []))
                    last_xnT_user = mm
                    last_wT_user[mset] = mm
                    if g < 32:
                        cp = sch.add("scalar", (lambda sc, pi=pi: sc.copy(
                            out=xr[:], in_=ps[pi][:])),
                            deps=[mm] + ([last_xr_reader] if last_xr_reader else []))
                        last_ps_reader[pi] = cp
                        c3 = sch.add("vector", (lambda v, g=g: v.tensor_scalar_mul(
                            cv[:], xr[:], cwsb[:, g*DCONV+3:g*DCONV+4])),
                            deps=[cp, cw_l] +
                                 ([last_cv_reader] if last_cv_reader else []))
                        c2 = sch.add("vector", (lambda v, g=g: v.scalar_tensor_tensor(
                            out=cv[:, 1:T], in0=xr[:, 0:T-1],
                            scalar=cwsb[:, g*DCONV+2:g*DCONV+3], in1=cv[:, 1:T],
                            op0=ALU.mult, op1=ALU.add)), deps=[c3])
                        c1 = sch.add("vector", (lambda v, g=g: v.scalar_tensor_tensor(
                            out=cv[:, 2:T], in0=xr[:, 0:T-2],
                            scalar=cwsb[:, g*DCONV+1:g*DCONV+2], in1=cv[:, 2:T],
                            op0=ALU.mult, op1=ALU.add)), deps=[c2])
                        c0 = sch.add("vector", (lambda v, g=g: v.scalar_tensor_tensor(
                            out=cv[:, 3:T], in0=xr[:, 0:T-3],
                            scalar=cwsb[:, g*DCONV+0:g*DCONV+1], in1=cv[:, 3:T],
                            op0=ALU.mult, op1=ALU.add)), deps=[c1])
                        last_xr_reader = c0
                        sf = sch.add("scalar", (lambda sc, g=g: sc.activation(
                            out=xcf[:], in_=cv[:], func=AF.Silu,
                            bias=cbsb[:, g:g+1])), deps=[c0, cb_l] +
                            ([last_xcf_store] if last_xcf_store else []))
                        sb_ = sch.add("scalar", (lambda sc, g=g: sc.activation(
                            out=xcb[:], in_=cv[:], func=AF.Silu,
                            bias=cbsb[:, g:g+1])), deps=[sf] +
                            ([last_xcb_store] if last_xcb_store else []))
                        last_cv_reader = sb_
                        last_xcf_store = sch.add("sync", (lambda s, b=b, g=g: s.dma_start(
                            out=ap["s_xcf"][b, g*128:(g+1)*128, :], in_=xcf[:])),
                            deps=[sf], dma=True)
                        last_xcb_store = sch.add("sync", (lambda s, b=b, g=g: s.dma_start(
                            out=ap["s_xcb"][b, g*128:(g+1)*128, :], in_=xcb[:])),
                            deps=[sb_], dma=True)
                    else:
                        gz_ = sch.add("scalar", (lambda sc, pi=pi: sc.activation(
                            out=gzt[:], in_=ps[pi][:], func=AF.Silu)),
                            deps=[mm] + ([last_gz_store] if last_gz_store else []))
                        last_ps_reader[pi] = gz_
                        last_gz_store = sch.add("sync", (lambda s, b=b, g=g: s.dma_start(
                            out=ap["s_gz"][b, (g-32)*128:(g-31)*128, :], in_=gzt[:])),
                            deps=[gz_], dma=True)
            ps_dt = ps[0]
            ps_bc = ps[1]
            mm_dt_ops = []
            xp_l = None
            xc_l = None
            for dc in range(32):
                di = dc % 2
                xp_deps = (mm_dt_ops[-2:-1] if len(mm_dt_ops) >= 2 else
                           [d for d in last_ps_reader if d])
                xp_l = sch.add("scalar", (lambda s, dc=dc, di=di: s.dma_start_transpose(
                    xpj[di][:], in_=ap["wb_xproj_pad"][0:256, dc*128:(dc+1)*128])),
                    deps=xp_deps + ([xp_l] if xp_l else []), dma=True)
                xc_l = sch.add("sync", (lambda s, b=b, dc=dc, di=di: s.dma_start(
                    out=xcbl[di][:], in_=ap["s_xcb"][b, dc*128:(dc+1)*128, :])),
                    deps=(mm_dt_ops[-2:-1] if len(mm_dt_ops) >= 2 else
                          [last_xcb_store]) + ([xc_l] if xc_l else []), dma=True)
                def f_xd(tn, dc=dc, di=di):
                    for nch in range(4):
                        tn.matmul(ps_dt[:, nch*512:(nch+1)*512],
                                  lhsT=xpj[di][:, 0:128],
                                  rhs=xcbl[di][:, nch*512:(nch+1)*512],
                                  start=(dc == 0), stop=(dc == 31))
                        last = tn.matmul(ps_bc[0:32, nch*512:(nch+1)*512],
                                         lhsT=xpj[di][:, 128:160],
                                         rhs=xcbl[di][:, nch*512:(nch+1)*512],
                                         start=(dc == 0), stop=(dc == 31))
                    return last
                mm_op = sch.add("tensor", f_xd, deps=[xp_l, xc_l] +
                                ([d for d in last_ps_reader if d] if dc == 0 else []))
                mm_dt_ops.append(mm_op)
            ev1 = sch.add("vector", lambda v: v.tensor_copy(out=dtb[:], in_=ps_dt[:]),
                          deps=[mm_dt_ops[-1]])
            ev2 = sch.add("vector", lambda v: v.tensor_copy(out=bcb[:], in_=ps_bc[0:32, :]),
                          deps=[mm_dt_ops[-1]])
            s1 = sch.add("sync", (lambda s, b=b: s.dma_start(
                out=ap["s_dtlo"][b], in_=dtb[:])), deps=[ev1], dma=True)
            s2 = sch.add("sync", (lambda s, b=b: s.dma_start(
                out=ap["s_bc"][b], in_=bcb[:])), deps=[ev2], dma=True)
            last_ps_reader = [s1, s2]
        return sch.emit(bar, bar_val)


def ph_mamba_scan(nc, ap, bar=None, bar_val=0, cast_late=False):
    TC = 1024
    NTC = T // TC
    ND = DI // 128
    cjobs = cast_jobs(ap, CAST_LATE, FW=1024) if cast_late else []
    cpos = [0]
    es = ExitStack()
    with es:
        cbufs = [es.enter_context(nc.sbuf_tensor(f"msc_b{i}", [128, 1024], BF16))
                 for i in range(2)]
        cprev = [None, None]
        dtlo = es.enter_context(nc.sbuf_tensor("ms_dtlo", [128, TC], BF16))
        dtw = [es.enter_context(nc.sbuf_tensor(f"ms_dtw{j}", [128, 512], BF16))
               for j in range(ND // 4)]
        dtb = es.enter_context(nc.sbuf_tensor("ms_dtb", [128, ND], F32))
        dsb = es.enter_context(nc.sbuf_tensor("ms_d", [128, ND], F32))
        asb = es.enter_context(nc.sbuf_tensor("ms_a", [128, ND * DS], F32))
        Bball = es.enter_context(nc.sbuf_tensor("ms_Bb", [128, DS * TC], BF16))
        Cball = es.enter_context(nc.sbuf_tensor("ms_Cb", [128, DS * TC], BF16))
        sp1 = es.enter_context(nc.sbuf_tensor("ms_sp1", [128, TC], F32))
        sp2 = es.enter_context(nc.sbuf_tensor("ms_sp2", [128, TC], F32))
        delt = es.enter_context(nc.sbuf_tensor("ms_delt", [128, TC], F32))
        xct = [es.enter_context(nc.sbuf_tensor(f"ms_xc{i}", [128, TC], F32))
               for i in range(2)]
        gzt = es.enter_context(nc.sbuf_tensor("ms_gz", [128, TC], F32))
        wbf = es.enter_context(nc.sbuf_tensor("ms_wbf", [128, TC], BF16))
        daall = es.enter_context(nc.sbuf_tensor("ms_da", [128, DS * TC], BF16))
        uall = es.enter_context(nc.sbuf_tensor("ms_u", [128, DS * TC], BF16))
        hall = es.enter_context(nc.sbuf_tensor("ms_h", [128, DS * TC], BF16))
        ygb = es.enter_context(nc.sbuf_tensor("ms_yg", [128, TC], BF16))
        hstate = es.enter_context(nc.sbuf_tensor("ms_hst", [128, ND * DS], F32))
        ps_d = es.enter_context(nc.psum_tensor("ms_psd", [128, TC], F32))
        sch = Sched(nc, es, "mscan")

        def drip_cast(k=6):
            while cpos[0] < len(cjobs) and k > 0:
                emit_cast_job(sch, cbufs, cprev, cpos[0], cjobs[cpos[0]])
                cpos[0] += 1
                k -= 1

        base_l = []
        base_l.append(sch.add("sync", lambda s: s.dma_start(
            out=dtb[:], in_=ap["aux_dtb"]), dma=True))
        base_l.append(sch.add("sync", lambda s: s.dma_start(
            out=dsb[:], in_=ap["aux_dvec"]), dma=True))
        base_l.append(sch.add("sync", lambda s: s.dma_start(
            out=asb[:].rearrange("p (n s) -> p n s", s=DS),
            in_=ap["s_Aneg"].rearrange("(n p) s -> p n s", p=128)), dma=True))
        dtw_l = []
        for j in range(ND // 4):
            dtw_l.append(sch.add("scalar", (lambda s, j=j: s.dma_start_transpose(
                dtw[j][:], in_=ap["wb_L0_attn_dt_proj_w"][j*512:(j+1)*512, :])),
                dma=True))

        dtlo_load = None
        last_dtlo_user = None
        last_bc_users = []
        last_store = None
        last_da_reader = None
        last_u_reader = [None, None]
        last_h_reader = None
        last_uv = None
        last_wm = None
        last_xg_reader = [None, None]
        last_dl = None
        last_dav = None
        last_yg = None
        last_y1_reader = None
        state_cpy = {}
        for b in range(B_LOC):
            for tc in range(NTC):
                dl_deps = ([dtlo_load] if dtlo_load else []) + \
                          ([last_dtlo_user] if last_dtlo_user else [])
                dtlo_load = sch.add("sync", (lambda s, b=b, tc=tc: s.dma_start(
                    out=dtlo[:], in_=ap["s_dtlo"][b, :, tc*TC:(tc+1)*TC])),
                    deps=dl_deps, dma=True)
                new_bc = []
                for s_ in range(DS):
                    new_bc.append(sch.add("sync", (lambda s, b=b, tc=tc, s_=s_:
                        s.dma_start(out=Bball[:, s_*TC:(s_+1)*TC],
                            in_=ap["s_bc"][b, s_:s_+1, tc*TC:(tc+1)*TC]
                                .partition_broadcast(128))),
                        deps=last_bc_users, dma=True))
                    new_bc.append(sch.add("sync", (lambda s, b=b, tc=tc, s_=s_:
                        s.dma_start(out=Cball[:, s_*TC:(s_+1)*TC],
                            in_=ap["s_bc"][b, DS+s_:DS+s_+1, tc*TC:(tc+1)*TC]
                                .partition_broadcast(128))),
                        deps=last_bc_users, dma=True))
                bc_loads = new_bc
                last_bc_users = []
                prev_ps_reader = None
                xl = None
                gl = None
                for dt_ in range(ND):
                    mmdeps = [dtlo_load, dtw_l[dt_ // 4]]
                    if prev_ps_reader is not None:
                        mmdeps.append(prev_ps_reader)
                    def f_dmm(tn, dt_=dt_, tc=tc):
                        for nch in range(TC // 512):
                            last = tn.matmul(ps_d[:, nch*512:(nch+1)*512],
                                lhsT=dtw[dt_ // 4][:, (dt_ % 4)*128:(dt_ % 4+1)*128],
                                rhs=dtlo[:, nch*512:(nch+1)*512],
                                start=True, stop=True)
                        return last
                    mm = sch.add("tensor", f_dmm, deps=mmdeps)
                    last_dtlo_user = mm
                    r1 = sch.add("scalar", (lambda sc, dt_=dt_: sc.activation(
                        out=sp1[:], in_=ps_d[:], func=AF.Relu,
                        bias=dtb[:, dt_:dt_+1])),
                        deps=[mm, base_l[0]] + ([last_dl] if last_dl else []))
                    a1 = sch.add("scalar", (lambda sc, dt_=dt_: sc.activation(
                        out=sp2[:], in_=ps_d[:], func=AF.Abs,
                        bias=dtb[:, dt_:dt_+1])),
                        deps=[mm] + ([last_dl] if last_dl else []))
                    prev_ps_reader = a1
                    e1 = sch.add("scalar", lambda sc: sc.activation(
                        out=delt[:], in_=sp2[:], func=AF.Exp, scale=-1.0),
                        deps=[a1] + ([last_wm] if last_wm else []) +
                             ([last_yg] if last_yg else []))
                    l1 = sch.add("scalar", lambda sc: sc.activation(
                        out=sp2[:], in_=delt[:], func=AF.Ln, bias=1.0), deps=[e1])
                    dl = sch.add("vector", lambda v: v.tensor_tensor(
                        out=delt[:], in0=sp1[:], in1=sp2[:], op=ALU.add),
                        deps=[r1, l1] + ([last_dav] if last_dav else []))
                    last_dl = dl
                    xi = dt_ % 2
                    xl = sch.add("sync", (lambda s, b=b, dt_=dt_, tc=tc, xi=xi:
                        s.dma_start(
                        out=xct[xi][:], in_=ap["s_xcf"][b, dt_*128:(dt_+1)*128,
                                                        tc*TC:(tc+1)*TC])),
                        deps=([last_xg_reader[xi]] if last_xg_reader[xi] else []) +
                             ([xl] if xl else []), dma=True)
                    gl = sch.add("sync", (lambda s, b=b, dt_=dt_, tc=tc:
                        s.dma_start(
                        out=gzt[:], in_=ap["s_gz"][b, dt_*128:(dt_+1)*128,
                                                   tc*TC:(tc+1)*TC])),
                        deps=([last_yg] if last_yg else []) +
                             ([gl] if gl else []), dma=True)
                    wm = sch.add("vector", (lambda v, xi=xi: v.tensor_tensor(
                        out=wbf[:], in0=delt[:], in1=xct[xi][:], op=ALU.mult)),
                        deps=[dl, xl] + ([last_uv] if last_uv else []))
                    last_wm = wm
                    drip_cast()
                    da_ops = []
                    for s_ in range(DS):
                        dav = sch.add("scalar", (lambda sc, s_=s_, dt_=dt_:
                            sc.activation(out=daall[:, s_*TC:(s_+1)*TC], in_=delt[:],
                                          func=AF.Exp,
                                          scale=asb[:, dt_*DS+s_:dt_*DS+s_+1])),
                            deps=[dl, base_l[2]] +
                                 ([last_da_reader] if s_ == 0 and last_da_reader
                                  else []))
                        da_ops.append(dav)
                    last_dav = da_ops[-1]
                    HS2 = DS // 2
                    uva = sch.add("vector", lambda g: g.tensor_tensor(
                        out=uall[:, 0:HS2*TC].rearrange("p (s t) -> p s t", s=HS2),
                        in0=wbf[:].rearrange("p (o t) -> p o t", o=1)
                            .broadcast_to([128, HS2, TC]),
                        in1=Bball[:, 0:HS2*TC].rearrange("p (s t) -> p s t", s=HS2),
                        op=ALU.mult),
                        deps=[wm] + bc_loads[0:DS:2] +
                             ([last_u_reader[0]] if last_u_reader[0] else []))
                    uvb = sch.add("vector", lambda g: g.tensor_tensor(
                        out=uall[:, HS2*TC:DS*TC].rearrange("p (s t) -> p s t", s=HS2),
                        in0=wbf[:].rearrange("p (o t) -> p o t", o=1)
                            .broadcast_to([128, HS2, TC]),
                        in1=Bball[:, HS2*TC:DS*TC].rearrange("p (s t) -> p s t", s=HS2),
                        op=ALU.mult),
                        deps=[wm] + bc_loads[DS::2] +
                             ([last_u_reader[1]] if last_u_reader[1] else []))
                    last_uv = uvb
                    sc_ops = []
                    for s_ in range(DS):
                        scn_deps = [da_ops[s_], uva if s_ < DS // 2 else uvb]
                        if tc == 0:
                            init = 0.0
                        else:
                            init = hstate[:, dt_*DS+s_:dt_*DS+s_+1]
                            scn_deps.append(state_cpy[dt_])
                        if s_ == 0 and last_h_reader:
                            scn_deps.append(last_h_reader)
                        scn = sch.add("vector", (lambda v, s_=s_, init=init:
                            v.tensor_tensor_scan(out=hall[:, s_*TC:(s_+1)*TC],
                                                 data0=daall[:, s_*TC:(s_+1)*TC],
                                                 data1=uall[:, s_*TC:(s_+1)*TC],
                                                 initial=init,
                                                 op0=ALU.mult, op1=ALU.add)),
                            deps=scn_deps, self_edge=True)
                        sc_ops.append(scn)
                    last_da_reader = sc_ops[-1]
                    last_u_reader = [sc_ops[DS // 2 - 1], sc_ops[-1]]
                    cp_deps = list(sc_ops)
                    if tc < NTC - 1:
                        state_cpy[dt_] = sch.add("vector", (lambda v, dt_=dt_:
                            v.tensor_copy(out=hstate[:, dt_*DS:(dt_+1)*DS],
                                in_=hall[:].rearrange("p (s t) -> p s t", s=DS)
                                    [:, :, TC-1:TC])), deps=cp_deps)
                        cp_deps = [state_cpy[dt_]]
                    ym = sch.add("vector", lambda v: v.tensor_tensor(
                        out=hall[:], in0=hall[:], in1=Cball[:], op=ALU.mult),
                        deps=cp_deps + bc_loads[1::2])
                    t8 = sch.add("vector", lambda v: v.tensor_tensor(
                        out=hall[:, 0:8*TC], in0=hall[:, 0:8*TC],
                        in1=hall[:, 8*TC:16*TC], op=ALU.add), deps=[ym])
                    t4 = sch.add("vector", lambda v: v.tensor_tensor(
                        out=hall[:, 0:4*TC], in0=hall[:, 0:4*TC],
                        in1=hall[:, 4*TC:8*TC], op=ALU.add), deps=[t8])
                    t2 = sch.add("vector", lambda v: v.tensor_tensor(
                        out=hall[:, 0:2*TC], in0=hall[:, 0:2*TC],
                        in1=hall[:, 2*TC:4*TC], op=ALU.add), deps=[t4])
                    yr = sch.add("vector", lambda v: v.tensor_tensor(
                        out=hall[:, 0:TC], in0=hall[:, 0:TC], in1=hall[:, TC:2*TC],
                        op=ALU.add), deps=[t2])
                    last_h_reader = yr
                    y1 = sch.add("vector", (lambda v, dt_=dt_, xi=xi:
                        v.scalar_tensor_tensor(
                        out=delt[:], in0=xct[xi][:], scalar=dsb[:, dt_:dt_+1],
                        in1=hall[:, 0:TC],
                        op0=ALU.mult, op1=ALU.add)),
                        deps=[yr, base_l[1]] +
                             ([last_y1_reader] if last_y1_reader else []),
                        self_edge=True)
                    yg = sch.add("vector", lambda v: v.tensor_tensor(
                        out=ygb[:], in0=delt[:], in1=gzt[:], op=ALU.mult),
                        deps=[y1, gl] + ([last_store] if last_store else []))
                    last_yg = yg
                    last_y1_reader = yg
                    last_xg_reader[xi] = yg
                    last_store = sch.add("sync", (lambda s, b=b, dt_=dt_, tc=tc:
                        s.dma_start(out=ap["s_yg"][b, dt_*128:(dt_+1)*128,
                                                   tc*TC:(tc+1)*TC], in_=ygb[:])),
                        deps=[yg], dma=True)
                last_bc_users = [last_uv, last_h_reader]
        drip_cast(10**9)
        return sch.emit(bar, bar_val)


def ph_mamba_out(nc, ap, bar=None, bar_val=0):
    NDC = DI // 128
    es = ExitStack()
    with es:
        opT = [es.enter_context(nc.sbuf_tensor(f"mo_w{c}", [128, D], BF16))
               for c in range(NDC)]
        ygt_all = es.enter_context(nc.sbuf_tensor("mo_yg", [128, NDC * 128], BF16))
        xtile = es.enter_context(nc.sbuf_tensor("mo_xt", [128, D], F32))
        otile = es.enter_context(nc.sbuf_tensor("mo_ot", [128, D], F32))
        pm = es.enter_context(nc.psum_tensor("mo_pm", [128, D], F32))
        sch = Sched(nc, es, "mout")
        wl = []
        for c in range(NDC):
            for nch in range(4):
                wl.append(sch.add("scalar", (lambda s, c=c, nch=nch:
                    s.dma_start_transpose(opT[c][:, nch*512:(nch+1)*512],
                        in_=ap["wb_L0_attn_out_proj_w"][nch*512:(nch+1)*512,
                                                        c*128:(c+1)*128])), dma=True))
        prev_mm = None
        prev_ev = None
        prev_st = None
        for b in range(B_LOC):
            for tt in range(T // 128):
                yls = [sch.add("sync", (lambda s, b=b, tt=tt: s.dma_start(
                    out=ygt_all[:].rearrange("p (n f) -> p n f", f=128),
                    in_=ap["s_yg"][b].rearrange("(n p) t -> p n t", p=128)
                        [:, :, tt*128:(tt+1)*128])),
                    deps=([prev_mm] if prev_mm else []), dma=True)]
                def f_mm(tn):
                    for nch in range(4):
                        for c in range(NDC):
                            last = tn.matmul(pm[:, nch*512:(nch+1)*512],
                                             lhsT=ygt_all[:, c*128:(c+1)*128],
                                             rhs=opT[c][:, nch*512:(nch+1)*512],
                                             start=(c == 0), stop=(c == NDC-1))
                    return last
                mm = sch.add("tensor", f_mm, deps=yls + wl +
                             ([prev_ev] if prev_ev else []))
                prev_mm = mm
                xl = sch.add("sync", (lambda s, b=b, tt=tt: s.dma_start(
                    out=xtile[:], in_=ap["s_x1acc"][b, tt*128:(tt+1)*128, :])),
                    deps=([prev_ev] if prev_ev else []), dma=True)
                ev = sch.add("vector", lambda v: v.tensor_tensor(
                    out=otile[:], in0=pm[:], in1=xtile[:], op=ALU.add),
                    deps=[mm, xl] + ([prev_st] if prev_st else []))
                prev_ev = ev
                prev_st = sch.add("sync", (lambda s, b=b, tt=tt: s.dma_start(
                    out=ap["s_x1"][b, tt*128:(tt+1)*128, :], in_=otile[:])),
                    deps=[ev], dma=True)
        return sch.emit(bar, bar_val)


def ph_attn_pre(nc, ap, bar=None, bar_val=0):
    es = ExitStack()
    with es:
        wuq = [es.enter_context(nc.sbuf_tensor(f"apr_wuq{c}", [128, LQ], BF16))
               for c in range(16)]
        wuk = [es.enter_context(nc.sbuf_tensor(f"apr_wuk{c}", [128, LKV], BF16))
               for c in range(16)]
        wdq = [es.enter_context(nc.sbuf_tensor(f"apr_wdq{c}", [128, D], BF16))
               for c in range(4)]
        wuv = [es.enter_context(nc.sbuf_tensor(f"apr_wuv{c}", [128, LKV], BF16))
               for c in range(16)]
        woT = [es.enter_context(nc.sbuf_tensor(f"apr_woT{c}", [128, D], BF16))
               for c in range(16)]
        t1 = [es.enter_context(nc.sbuf_tensor(f"apr_t1{c}", [128, LKV], BF16))
              for c in range(4)]
        stg = es.enter_context(nc.sbuf_tensor("apr_stg", [128, D], BF16))
        pm = es.enter_context(nc.psum_tensor("apr_pm", [128, D], F32))
        sch = Sched(nc, es, "apre")
        wl = []
        for c in range(16):
            wl.append(sch.add("sync", (lambda s, c=c: s.dma_start(
                out=wuq[c][:], in_=ap["wb_L1_attn_Wuq"][c*128:(c+1)*128, :])), dma=True))
            wl.append(sch.add("sync", (lambda s, c=c: s.dma_start(
                out=wuk[c][:], in_=ap["wb_L1_attn_Wuk"][c*128:(c+1)*128, :])), dma=True))
            wl.append(sch.add("gpsimd", (lambda s, c=c: s.dma_start(
                out=wuv[c][:], in_=ap["wb_L1_attn_Wuv"][c*128:(c+1)*128, :])), dma=True))
            for nch in range(4):
                wl.append(sch.add("scalar", (lambda s, c=c, nch=nch:
                    s.dma_start_transpose(woT[c][:, nch*512:(nch+1)*512],
                        in_=ap["wb_L1_attn_Wo"][nch*512:(nch+1)*512,
                                                c*128:(c+1)*128])), dma=True))
        for c in range(4):
            wl.append(sch.add("sync", (lambda s, c=c: s.dma_start(
                out=wdq[c][:], in_=ap["wb_L1_attn_Wdq"][c*128:(c+1)*128, :])), dma=True))
        prev_ev = None
        for lt in range(4):
            def f_t1(tn, lt=lt):
                for c in range(16):
                    last = tn.matmul(pm[:, 0:LKV], lhsT=wuq[c][:, lt*128:(lt+1)*128],
                                     rhs=wuk[c][:], start=(c == 0), stop=(c == 15))
                return last
            mm = sch.add("tensor", f_t1, deps=wl + ([prev_ev] if prev_ev else []))
            prev_ev = sch.add("vector", (lambda v, lt=lt: v.tensor_copy(
                out=t1[lt][:], in_=pm[:, 0:LKV])), deps=[mm])
        for mt in range(16):
            def f_kf(tn, mt=mt):
                for lc in range(4):
                    last = tn.matmul(pm[:, 0:LKV], lhsT=wdq[lc][:, mt*128:(mt+1)*128],
                                     rhs=t1[lc][:], start=(lc == 0), stop=(lc == 3))
                return last
            mm = sch.add("tensor", f_kf, deps=[prev_ev])
            ev = sch.add("vector", lambda v: v.tensor_copy(
                out=stg[:, 0:LKV], in_=pm[:, 0:LKV]), deps=[mm])
            prev_ev = sch.add("sync", (lambda s, mt=mt: s.dma_start(
                out=ap["s_kfull"][mt*128:(mt+1)*128, :], in_=stg[:, 0:LKV])),
                deps=[ev], dma=True)
        for lt in range(4):
            def f_vf(tn, lt=lt):
                for nch in range(4):
                    for c in range(16):
                        last = tn.matmul(pm[:, nch*512:(nch+1)*512],
                                         lhsT=wuv[c][:, lt*128:(lt+1)*128],
                                         rhs=woT[c][:, nch*512:(nch+1)*512],
                                         start=(c == 0), stop=(c == 15))
                return last
            mm = sch.add("tensor", f_vf, deps=wl + [prev_ev])
            ev = sch.add("vector", lambda v: v.tensor_copy(
                out=stg[:], in_=pm[:]), deps=[mm])
            prev_ev = sch.add("sync", (lambda s, lt=lt: s.dma_start(
                out=ap["s_vfull"][lt*128:(lt+1)*128, :], in_=stg[:])),
                deps=[ev], dma=True)
        return sch.emit(bar, bar_val)


def ph_attn_proj(nc, ap, bar=None, bar_val=0):
    es = ExitStack()
    with es:
        xnT = [es.enter_context(nc.sbuf_tensor(f"apj_xnT{c}", [128, T], BF16))
               for c in range(16)]
        wdkvT = [es.enter_context(nc.sbuf_tensor(f"apj_wkt{c}", [128, LKV], BF16))
                 for c in range(16)]
        wdqT = [es.enter_context(nc.sbuf_tensor(f"apj_wqt{c}", [128, LQ], BF16))
                for c in range(16)]
        stg = es.enter_context(nc.sbuf_tensor("apj_stg", [128, T], BF16))
        pm = es.enter_context(nc.psum_tensor("apj_pm", [128, T], F32))
        sch = Sched(nc, es, "aproj")
        wl = []
        for c in range(16):
            wl.append(sch.add("scalar", (lambda s, c=c: s.dma_start_transpose(
                wdkvT[c][:], in_=ap["wb_L1_attn_Wdkv"][0:LKV, c*128:(c+1)*128])),
                dma=True))
            wl.append(sch.add("scalar", (lambda s, c=c: s.dma_start_transpose(
                wdqT[c][:], in_=ap["wb_L1_attn_Wdq"][0:LQ, c*128:(c+1)*128])),
                dma=True))
        xnT_load = [None] * 16
        last_xnT_user = None
        prev_ev = None
        prev_ev2 = None
        for b in range(B_LOC):
            for c in range(16):
                deps = ([xnT_load[c]] if xnT_load[c] else []) + \
                       ([last_xnT_user] if last_xnT_user else [])
                xnT_load[c] = sch.add("scalar", (lambda s, b=b, c=c:
                    s.dma_start_transpose(xnT[c][:],
                        in_=ap["s_xn1"][b, 0:T, c*128:(c+1)*128])), deps=deps, dma=True)
            loads_b = list(xnT_load)
            for which, wT, dst in (("ckvf", wdkvT, "s_ckvf"), ("qd", wdqT, "s_qd")):
                for lt in range(4):
                    def f_mm(tn, wT=wT, lt=lt):
                        for nch in range(4):
                            for c in range(16):
                                last = tn.matmul(pm[:, nch*512:(nch+1)*512],
                                                 lhsT=wT[c][:, lt*128:(lt+1)*128],
                                                 rhs=xnT[c][:, nch*512:(nch+1)*512],
                                                 start=(c == 0), stop=(c == 15))
                        return last
                    mm = sch.add("tensor", f_mm, deps=wl + loads_b +
                                 ([prev_ev] if prev_ev else []))
                    last_xnT_user = mm
                    ev = sch.add("vector", lambda v: v.tensor_copy(
                        out=stg[:], in_=pm[:]), deps=[mm])
                    prev_ev = sch.add("sync", (lambda s, b=b, lt=lt, dst=dst:
                        s.dma_start(out=ap[dst][b, lt*128:(lt+1)*128, :],
                                    in_=stg[:])), deps=[ev], dma=True)
        return sch.emit(bar, bar_val)


def ph_attn_core(nc, ap, bar=None, bar_val=0):
    NQT = T // 128
    es = ExitStack()
    with es:
        kfull = [es.enter_context(nc.sbuf_tensor(f"ac_kf{c}", [128, LKV], BF16))
                 for c in range(16)]
        vfull = [es.enter_context(nc.sbuf_tensor(f"ac_vf{c}", [128, D], BF16))
                 for c in range(4)]
        wuqT = [es.enter_context(nc.sbuf_tensor(f"ac_wuq{c}", [128, D], BF16))
                for c in range(4)]
        ckvf = [es.enter_context(nc.sbuf_tensor(f"ac_cf{c}", [128, T], BF16))
                for c in range(4)]
        qd = [es.enter_context(nc.sbuf_tensor(f"ac_qd{c}", [128, T], BF16))
              for c in range(4)]
        qh = es.enter_context(nc.sbuf_tensor("ac_qh", [128, T], BF16))
        qlat = [es.enter_context(nc.sbuf_tensor(f"ac_ql{c}", [128, T], BF16))
                for c in range(4)]
        wh = [es.enter_context(nc.sbuf_tensor(f"ac_wh{c}", [128, HS], BF16))
              for c in range(16)]
        pbf = es.enter_context(nc.sbuf_tensor("ac_p", [128, T], BF16))
        pT = [es.enter_context(nc.sbuf_tensor(f"ac_pT{c}", [128, 512], BF16))
              for c in range(16)]
        ystg = [es.enter_context(nc.sbuf_tensor(f"ac_y{i}", [128, HS], F32))
                for i in range(2)]
        rinv = es.enter_context(nc.sbuf_tensor("ac_ri", [128, NQT], F32))
        mask = es.enter_context(nc.sbuf_tensor("ac_mask", [128, 128], F32))
        cst = es.enter_context(nc.sbuf_tensor("ac_cst", [128, 1], F32))
        st = es.enter_context(nc.sbuf_tensor("ac_st", [128, 4], F32))
        ps_big = es.enter_context(nc.psum_tensor("ac_pb", [128, T], F32))
        ps_wh = es.enter_context(nc.psum_tensor("ac_pw", [128, HS], F32))
        ps_y = [es.enter_context(nc.psum_tensor(f"ac_py{i}", [128, HS], F32))
                for i in range(2)]
        sch = Sched(nc, es, "acore")
        wl = []
        for c in range(16):
            wl.append(sch.add("sync", (lambda s, c=c: s.dma_start(
                out=kfull[c][:], in_=ap["s_kfull"][c*128:(c+1)*128, :])), dma=True))
        for c in range(4):
            wl.append(sch.add("sync", (lambda s, c=c: s.dma_start(
                out=vfull[c][:], in_=ap["s_vfull"][c*128:(c+1)*128, :])), dma=True))
            for nch in range(4):
                wl.append(sch.add("scalar", (lambda s, c=c, nch=nch:
                    s.dma_start_transpose(wuqT[c][:, nch*512:(nch+1)*512],
                        in_=ap["wb_L1_attn_Wuq"][nch*512:(nch+1)*512,
                                                 c*128:(c+1)*128])), dma=True))
        wl.append(sch.add("sync", lambda s: s.dma_start(out=mask[:], in_=ap["aux_mask"]),
                          dma=True))
        csto = sch.add("vector", lambda v: v.memset(cst[:], RSQ_HS))
        wl.append(csto)

        state = {
            "bl": [None] * 8, "last_bl_user": None, "last_pbig": None,
            "last_py": [None, None], "last_pwh": None, "last_pbf_user": None,
            "last_wh_user": None, "last_pT_user": [None] * 16,
            "prev_pT_loads": [],
        }
        for b in range(B_LOC):
            deps_b = ([state["last_bl_user"]] if state["last_bl_user"] else [])
            nbl = []
            bl = state["bl"]
            for c in range(4):
                nbl.append(sch.add("sync", (lambda s, b=b, c=c: s.dma_start(
                    out=ckvf[c][:], in_=ap["s_ckvf"][b, c*128:(c+1)*128, :])),
                    deps=deps_b + (bl[c:c+1] if bl[c] else []), dma=True))
            for c in range(4):
                nbl.append(sch.add("gpsimd", (lambda s, b=b, c=c: s.dma_start(
                    out=qd[c][:], in_=ap["s_qd"][b, c*128:(c+1)*128, :])),
                    deps=deps_b + (bl[4+c:5+c] if bl[4+c] else []), dma=True))
            state["bl"] = bl = nbl
            for h in range(NH):
                def f_qh(tn, h=h):
                    for lc in range(4):
                        for nch in range(4):
                            last = tn.matmul(ps_big[:, nch*512:(nch+1)*512],
                                             lhsT=wuqT[lc][:, h*128:(h+1)*128],
                                             rhs=qd[lc][:, nch*512:(nch+1)*512],
                                             start=(lc == 0), stop=(lc == 3))
                    return last
                qhmm = sch.add("tensor", f_qh, deps=wl + bl +
                               ([state["last_pbig"]] if state["last_pbig"] else []))
                qhev = sch.add("scalar", lambda sc: sc.copy(out=qh[:], in_=ps_big[:]),
                               deps=[qhmm])
                state["last_pbig"] = qhev
                ql_evs = []
                for lt in range(4):
                    def f_ql(tn, h=h, lt=lt):
                        for nch in range(4):
                            last = tn.matmul(ps_big[:, nch*512:(nch+1)*512],
                                             lhsT=kfull[h][:, lt*128:(lt+1)*128],
                                             rhs=qh[:, nch*512:(nch+1)*512],
                                             start=True, stop=True)
                        return last
                    qlmm = sch.add("tensor", f_ql, deps=[qhev, state["last_pbig"]])
                    qlev = sch.add("scalar", (lambda sc, lt=lt: sc.activation(
                        out=qlat[lt][:], in_=ps_big[:], func=AF.Identity,
                        scale=cst[:, 0:1])), deps=[qlmm, csto])
                    ql_evs.append(qlev)
                    state["last_pbig"] = qlev
                wh_evs = []
                for kt in range(16):
                    def f_wh(tn, h=h, kt=kt):
                        for lc in range(4):
                            last = tn.matmul(ps_wh[:],
                                             lhsT=ckvf[lc][:, kt*128:(kt+1)*128],
                                             rhs=vfull[lc][:, h*128:(h+1)*128],
                                             start=(lc == 0), stop=(lc == 3))
                        return last
                    whmm = sch.add("tensor", f_wh, deps=wl + bl +
                                   ([state["last_pwh"]] if state["last_pwh"] else []))
                    whev = sch.add("vector", (lambda v, kt=kt: v.tensor_copy(
                        out=wh[kt][:], in_=ps_wh[:])), deps=[whmm] +
                        ([state["last_wh_user"]] if state["last_wh_user"] and
                         kt == 0 else []))
                    state["last_pwh"] = whev
                    wh_evs.append(whev)
                p_stores = {}
                rcs = {}

                def emit_scores(qt):
                    kr = (qt + 1) * 128
                    def f_sc(tn, qt=qt, kr=kr):
                        nch = 0
                        while nch * 512 < kr:
                            w = min(512, kr - nch * 512)
                            for lc in range(4):
                                last = tn.matmul(
                                    ps_big[:, nch*512:nch*512+w],
                                    lhsT=qlat[lc][:, qt*128:(qt+1)*128],
                                    rhs=ckvf[lc][:, nch*512:nch*512+w],
                                    start=(lc == 0), stop=(lc == 3))
                            nch += 1
                        return last
                    smm = sch.add("tensor", f_sc, deps=ql_evs +
                                  ([state["last_pbig"]] if state["last_pbig"]
                                   else []))
                    msk = sch.add("vector", (lambda v, kr=kr: v.tensor_tensor(
                        out=ps_big[:, kr-128:kr], in0=ps_big[:, kr-128:kr],
                        in1=mask[:], op=ALU.add)), deps=[smm])
                    mx = sch.add("vector", (lambda v, kr=kr: v.reduce_max(
                        out=st[:, 0:1], in_=ps_big[:, 0:kr], axis=AX.X,
                        negate=True)), deps=[msk])
                    exp = sch.add("scalar", (lambda sc, kr=kr: sc.activation(
                        out=pbf[:, 0:kr], in_=ps_big[:, 0:kr], func=AF.Exp,
                        bias=st[:, 0:1], accum_out=st[:, 1:2])),
                        deps=[mx] + ([state["last_pbf_user"]]
                                     if state["last_pbf_user"] else []))
                    state["last_pbig"] = exp
                    rcs[qt] = sch.add("vector", (lambda v, qt=qt: v.reciprocal(
                        out=rinv[:, qt:qt+1], in_=st[:, 1:2])), deps=[exp])
                    pst = sch.add("sync", (lambda s, qt=qt, kr=kr: s.dma_start(
                        out=ap["s_p"][qt*128:(qt+1)*128, 0:kr], in_=pbf[:, 0:kr])),
                        deps=[exp] + state["prev_pT_loads"], dma=True)
                    state["last_pbf_user"] = pst
                    p_stores[qt] = pst

                new_pT_loads = []

                def emit_pT_loads(qg):
                    gkc = (qg + 1) * 4
                    pT_loads = {}
                    for kc in range(gkc):
                        need = [p_stores[qt] for qt in range(qg*4, qg*4+4)
                                if (qt + 1) * 128 > kc * 128]
                        pT_loads[kc] = sch.add("scalar", (lambda s, qg=qg, kc=kc:
                            s.dma_start_transpose(pT[kc][:],
                                in_=ap["s_p"][qg*512:(qg+1)*512,
                                              kc*128:(kc+1)*128])),
                            deps=need + ([state["last_pT_user"][kc]]
                                         if state["last_pT_user"][kc] else []),
                            dma=True)
                        new_pT_loads.append(pT_loads[kc])
                    return pT_loads

                def emit_y_tile(qg, qi, pT_loads):
                    qt = qg * 4 + qi
                    nkc = qt + 1
                    yi = qt % 2
                    def f_y(tn, qi=qi, nkc=nkc, yi=yi):
                        for kc in range(nkc):
                            last = tn.matmul(ps_y[yi][:],
                                             lhsT=pT[kc][:, qi*128:(qi+1)*128],
                                             rhs=wh[kc][:],
                                             start=(kc == 0), stop=(kc == nkc-1))
                        return last
                    ymm = sch.add("tensor", f_y,
                                  deps=[pT_loads[kc] for kc in range(nkc)] +
                                       wh_evs[:nkc] +
                                       ([state["last_py"][yi]]
                                        if state["last_py"][yi] else []))
                    for kc in range(nkc):
                        state["last_pT_user"][kc] = ymm
                    state["last_wh_user"] = ymm
                    yev = sch.add("vector", (lambda v, qt=qt, yi=yi:
                        v.tensor_scalar_mul(ystg[yi][:], ps_y[yi][:],
                                            rinv[:, qt:qt+1])),
                        deps=[ymm, rcs[qt]] +
                             ([state["last_py"][yi]]
                              if state["last_py"][yi] else []),
                        self_edge=True)
                    yst = sch.add("gpsimd", (lambda s, b=b, h=h, qt=qt, yi=yi:
                        s.dma_start(
                            out=ap["s_atn"][b, qt*128:(qt+1)*128,
                                            h*128:(h+1)*128],
                            in_=ystg[yi][:])), deps=[yev], dma=True)
                    state["last_py"][yi] = yst
                    state["last_bl_user"] = ymm

                for qt in range(4):
                    emit_scores(qt)
                for qg in range(4):
                    pT_loads = emit_pT_loads(qg)
                    nxt = list(range((qg+1)*4, min((qg+2)*4, NQT)))
                    for qi in range(4):
                        if qi < len(nxt):
                            emit_scores(nxt[qi])
                        emit_y_tile(qg, qi, pT_loads)
                state["prev_pT_loads"] = new_pT_loads
        return sch.emit(bar, bar_val)


def ph_ffn1(nc, ap, bar=None, bar_val=0):
    TQ = 512
    es = ExitStack()
    with es:
        xnT = [es.enter_context(nc.sbuf_tensor(f"f1_xnT{c}", [128, TQ], BF16))
               for c in range(16)]
        acc = [es.enter_context(nc.sbuf_tensor(f"f1_acc{t}", [128, D], F32))
               for t in range(4)]
        tsb = [es.enter_context(nc.sbuf_tensor(f"f1_ts{i}", [128, TQ], BF16))
               for i in range(11)]
        gwT = es.enter_context(nc.sbuf_tensor("f1_gw", [128, 16 * E1], BF16))
        w1g = [es.enter_context(nc.sbuf_tensor(f"f1_w1g{i}", [128, 2048], BF16))
               for i in range(2)]
        w1u = [es.enter_context(nc.sbuf_tensor(f"f1_w1u{i}", [128, 2048], BF16))
               for i in range(2)]
        w2r = [es.enter_context(nc.sbuf_tensor(f"f1_w2r{i}", [128, D], BF16))
               for i in range(11)]
        sg = [es.enter_context(nc.sbuf_tensor(f"f1_sg{i}", [128, TQ], F32))
              for i in range(2)]
        cw = es.enter_context(nc.sbuf_tensor("f1_cw", [128, 4 * E1], F32))
        gsb = es.enter_context(nc.sbuf_tensor("f1_g", [128, E1], F32))
        g2 = es.enter_context(nc.sbuf_tensor("f1_g2", [128, E1], F32))
        keep = es.enter_context(nc.sbuf_tensor("f1_keep", [128, E1], F32))
        st = es.enter_context(nc.sbuf_tensor("f1_st", [128, 8], F32))
        xtile = es.enter_context(nc.sbuf_tensor("f1_xt", [128, D], F32))
        atile = es.enter_context(nc.sbuf_tensor("f1_at", [128, D], F32))
        otile = es.enter_context(nc.sbuf_tensor("f1_ot", [128, D], F32))
        ps_a = [es.enter_context(nc.psum_tensor(f"f1_pa{i}", [128, TQ], F32))
                for i in range(2)]
        ps_b = [es.enter_context(nc.psum_tensor(f"f1_pb{i}", [128, TQ], F32))
                for i in range(2)]
        ps_y = [es.enter_context(nc.psum_tensor(f"f1_py{i}", [128, 512], F32))
                for i in range(2)]
        ps_g = es.enter_context(nc.psum_tensor("f1_pg", [128, E1], F32))
        sch = Sched(nc, es, "ffn1")
        gwl = sch.add("sync", lambda s: s.dma_start(
            out=gwT[:].rearrange("p (n e) -> p n e", e=E1),
            in_=ap["wb_gate_wT"].rearrange("(n p) e -> p n e", p=128)), dma=True)

        tb_done = []
        prev_pair = [None, None]
        for srcn, dstn, RT, CT in (("wb_L1_ffn_sw1", "wt_sw1", D, I1),
                                   ("wb_L1_ffn_sw2", "wt_sw2", D, I1),
                                   ("wb_L1_ffn_sw3", "wt_sw3", I1, D)):
            for rc in range(RT // 128):
                bi = rc % 2
                buf = w1g[bi]
                lo = sch.add("scalar", (lambda s, srcn=srcn, rc=rc, CT=CT, buf=buf:
                    s.dma_start_transpose(buf[:, 0:CT],
                        in_=ap[srcn][0:CT, rc*128:(rc+1)*128])),
                    deps=([prev_pair[bi]] if prev_pair[bi] else []), dma=True)
                st_ = sch.add("sync", (lambda s, dstn=dstn, rc=rc, CT=CT, buf=buf:
                    s.dma_start(out=ap[dstn][rc*128:(rc+1)*128, 0:CT],
                                in_=buf[:, 0:CT])), deps=[lo], dma=True)
                prev_pair[bi] = st_
                tb_done.append(st_)
        xnT_load = [None] * 16
        last_xnT_user = None
        last_out_store = [None] * 4
        last_psg_read = None
        last_gsb_reader = None
        last_t_reader = [None] * 11
        last_sw_load = [None, None]
        last_sw_user = None
        last_w1_user = [None, None]
        last_w1_load = [None, None]
        last_w2_user = [None] * 11
        last_w2_load = [None] * 11
        last_swt_user = [None, None]
        last_swt_load = [None, None]
        last_psab_read = [None, None]
        last_psy_read = [None, None]
        last_acc_reader = [None] * 4
        for b in range(B_LOC):
            for qtr in range(4):
                t0 = qtr * TQ
                for c in range(16):
                    deps = ([xnT_load[c]] if xnT_load[c] else []) + \
                           ([last_xnT_user] if last_xnT_user else [])
                    xnT_load[c] = sch.add("scalar", (lambda s, b=b, t0=t0, c=c:
                        s.dma_start_transpose(xnT[c][:],
                            in_=ap["s_xn1"][b, t0:t0+TQ, c*128:(c+1)*128])),
                        deps=deps, dma=True)
                loads_b = list(xnT_load)
                w6ops = []
                for tt in range(4):
                    def f_g(tn, tt=tt):
                        for c in range(16):
                            last = tn.matmul(ps_g[:], lhsT=xnT[c][:, tt*128:(tt+1)*128],
                                             rhs=gwT[:, c*E1:(c+1)*E1],
                                             start=(c == 0), stop=(c == 15))
                        return last
                    gmm = sch.add("tensor", f_g, deps=loads_b + [gwl] +
                                  ([last_psg_read] if last_psg_read else []))
                    mx = sch.add("vector", lambda v: v.reduce_max(
                        out=st[:, 0:1], in_=ps_g[:], axis=AX.X, negate=True),
                        deps=[gmm])
                    ex = sch.add("scalar", lambda sc: sc.activation(
                        out=gsb[:], in_=ps_g[:], func=AF.Exp, bias=st[:, 0:1],
                        accum_out=st[:, 1:2]),
                        deps=[mx] + ([last_gsb_reader] if last_gsb_reader else []))
                    last_psg_read = ex
                    rc = sch.add("vector", lambda v: v.reciprocal(
                        out=st[:, 2:3], in_=st[:, 1:2]), deps=[ex])
                    gg = sch.add("vector", lambda v: v.tensor_scalar_mul(
                        gsb[:], gsb[:], st[:, 2:3]), deps=[rc, ex], self_edge=True)
                    m1 = sch.add("vector", lambda v: v.reduce_max(
                        out=st[:, 3:4], in_=gsb[:], axis=AX.X), deps=[gg])
                    e1_ = sch.add("vector", lambda v: v.tensor_scalar(
                        out=g2[:], in0=gsb[:], scalar1=st[:, 3:4], scalar2=-1e30,
                        op0=ALU.is_ge, op1=ALU.mult), deps=[m1], self_edge=True)
                    g2s = sch.add("vector", lambda v: v.tensor_tensor(
                        out=g2[:], in0=gsb[:], in1=g2[:], op=ALU.add), deps=[e1_])
                    m2 = sch.add("vector", lambda v: v.reduce_max(
                        out=st[:, 4:5], in_=g2[:], axis=AX.X), deps=[g2s])
                    kp = sch.add("vector", lambda v: v.tensor_single_scalar(
                        out=keep[:], in_=gsb[:], scalar=st[:, 4:5], op=ALU.is_ge),
                        deps=[m2], self_edge=True)
                    sm = sch.add("vector", lambda v: v.tensor_scalar(
                        out=st[:, 5:6], in0=st[:, 3:4], scalar1=st[:, 4:5],
                        scalar2=1e-20, op0=ALU.add, op1=ALU.add),
                        deps=[m1, m2], self_edge=True)
                    rn = sch.add("vector", lambda v: v.reciprocal(
                        out=st[:, 6:7], in_=st[:, 5:6]), deps=[sm], self_edge=True)
                    gk = sch.add("vector", lambda v: v.tensor_tensor(
                        out=keep[:], in0=gsb[:], in1=keep[:], op=ALU.mult), deps=[kp])
                    w6 = sch.add("vector", (lambda v, tt=tt: v.tensor_scalar_mul(
                        cw[:, tt*E1:(tt+1)*E1], keep[:], st[:, 6:7])),
                        deps=[gk, rn], self_edge=True)
                    last_gsb_reader = w6
                    w6ops.append(w6)

                for phase in range(9):
                    e = phase - 1
                    t_ops = [None] * 11
                    if True:
                        for it in range(11):
                            bi = it % 2
                            ldeps = ([last_w1_load[bi]] if last_w1_load[bi] else []) + \
                                    ([last_w1_user[bi]] if last_w1_user[bi] else [])
                            if phase == 0:
                                ldeps = ldeps + tb_done
                                gsrc = ap["wt_sw1"].rearrange("(n p) f -> p n f", p=128)
                                usrc = ap["wt_sw2"].rearrange("(n p) f -> p n f", p=128)
                                gsl = gsrc[:, :, it*128:(it+1)*128]
                                usl = usrc[:, :, it*128:(it+1)*128]
                            else:
                                gsl = ap["wb_L1_ffn_w1"][e, :, it*128:(it+1)*128] \
                                    .rearrange("(n p) f -> p n f", p=128)
                                usl = ap["wb_L1_ffn_w1"][e, :, I1+it*128:I1+(it+1)*128] \
                                    .rearrange("(n p) f -> p n f", p=128)
                            lg = sch.add("sync", (lambda s, gsl=gsl, bi=bi:
                                s.dma_start(
                                    out=w1g[bi][:].rearrange("p (n f) -> p n f", f=128),
                                    in_=gsl)), deps=ldeps, dma=True)
                            lu = sch.add("gpsimd", (lambda s, usl=usl, bi=bi:
                                s.dma_start(
                                    out=w1u[bi][:].rearrange("p (n f) -> p n f", f=128),
                                    in_=usl)), deps=ldeps, dma=True)
                            last_w1_load[bi] = lu
                            pi = it % 2
                            def f_gu(tn, bi=bi, pi=pi):
                                for c in range(16):
                                    tn.matmul(ps_a[pi][:],
                                              lhsT=w1g[bi][:, c*128:(c+1)*128],
                                              rhs=xnT[c][:], start=(c == 0), stop=False)
                                    last = tn.matmul(ps_b[pi][:],
                                                     lhsT=w1u[bi][:, c*128:(c+1)*128],
                                                     rhs=xnT[c][:], start=(c == 0),
                                                     stop=False)
                                return last
                            gu = sch.add("tensor", f_gu, deps=[lg, lu] + loads_b +
                                         ([last_psab_read[pi]] if last_psab_read[pi]
                                          else []))
                            last_w1_user[bi] = gu
                            sl_ = sch.add("scalar", (lambda sc, pi=pi:
                                sc.activation(out=sg[pi][:], in_=ps_a[pi][:],
                                              func=AF.Silu)),
                                deps=[gu] + ([last_psab_read[pi]] if
                                             last_psab_read[pi] else []))
                            tmul = sch.add("vector", (lambda v, it=it, pi=pi:
                                v.tensor_tensor(out=tsb[it][:], in0=sg[pi][:],
                                                in1=ps_b[pi][:], op=ALU.mult)),
                                deps=[sl_, gu] +
                                ([last_t_reader[it]] if last_t_reader[it] else []))
                            last_psab_read[pi] = tmul
                            t_ops[it] = tmul
                        for it in range(11):
                            wdeps = ([last_w2_load[it]] if last_w2_load[it] else []) + \
                                    ([last_w2_user[it]] if last_w2_user[it] else [])
                            if phase == 0:
                                wsl = ap["wt_sw3"][it*128:(it+1)*128, :]
                                wdeps = wdeps + tb_done
                            else:
                                wsl = ap["wb_L1_ffn_w2"][e, it*128:(it+1)*128, :]
                            last_w2_load[it] = sch.add("gpsimd", (lambda s, wsl=wsl,
                                                                  it=it:
                                s.dma_start(out=w2r[it][:], in_=wsl)),
                                deps=wdeps, dma=True)
                    for tt in range(4):
                        for fch in range(4):
                            yi = (tt * 4 + fch) % 2
                            def f_ye(tn, tt=tt, fch=fch, yi=yi):
                                for it in range(11):
                                    last = tn.matmul(ps_y[yi][:],
                                        lhsT=tsb[it][:, tt*128:(tt+1)*128],
                                        rhs=w2r[it][:, fch*512:(fch+1)*512],
                                        start=(it == 0), stop=(it == 10))
                                return last
                            mmop = sch.add("tensor", f_ye,
                                deps=[t for t in t_ops if t is not None] +
                                     [last_w2_load[it] for it in range(11)] +
                                     ([last_psy_read[yi]] if last_psy_read[yi]
                                      else []))
                            for it in range(11):
                                last_t_reader[it] = mmop
                                last_w2_user[it] = mmop
                            if phase == 0:
                                ev = sch.add("vector", (lambda v, tt=tt, fch=fch,
                                                        yi=yi:
                                    v.tensor_copy(out=acc[tt][:, fch*512:(fch+1)*512],
                                                  in_=ps_y[yi][:])),
                                    deps=[mmop] +
                                    ([last_acc_reader[tt]] if last_acc_reader[tt]
                                     else []))
                            else:
                                ev = sch.add("vector", (lambda v, tt=tt, fch=fch, e=e,
                                                        yi=yi:
                                    v.scalar_tensor_tensor(
                                        out=acc[tt][:, fch*512:(fch+1)*512],
                                        in0=ps_y[yi][:], scalar=cw[:, tt*E1+e:tt*E1+e+1],
                                        in1=acc[tt][:, fch*512:(fch+1)*512],
                                        op0=ALU.mult, op1=ALU.add)),
                                    deps=[mmop, w6ops[tt]], self_edge=True)
                            last_psy_read[yi] = ev
                prev_a1 = None
                prev_a2 = None
                prev_so = None
                for tt in range(4):
                    xl = sch.add("sync", (lambda s, b=b, t0=t0, tt=tt: s.dma_start(
                        out=xtile[:], in_=ap["s_x1"][b, t0+tt*128:t0+(tt+1)*128, :])),
                        deps=([last_acc_reader[tt]] if last_acc_reader[tt] else []) +
                             ([prev_a1] if prev_a1 else []),
                        dma=True)
                    al = sch.add("sync", (lambda s, b=b, t0=t0, tt=tt: s.dma_start(
                        out=atile[:], in_=ap["s_atn"][b, t0+tt*128:t0+(tt+1)*128, :])),
                        deps=([last_acc_reader[tt]] if last_acc_reader[tt] else []) +
                             ([prev_a2] if prev_a2 else []),
                        dma=True)
                    a1 = sch.add("vector", (lambda v, tt=tt: v.tensor_tensor(
                        out=otile[:], in0=acc[tt][:], in1=xtile[:], op=ALU.add)),
                        deps=[xl] + [p for p in last_psy_read if p] +
                             ([last_out_store[tt]] if last_out_store[tt] else []) +
                             ([prev_so] if prev_so else []))
                    a2 = sch.add("vector", lambda v: v.tensor_tensor(
                        out=otile[:], in0=otile[:], in1=atile[:], op=ALU.add),
                        deps=[a1, al])
                    so = sch.add("sync", (lambda s, b=b, t0=t0, tt=tt: s.dma_start(
                        out=ap["out"][b, t0+tt*128:t0+(tt+1)*128, :], in_=otile[:])),
                        deps=[a2], dma=True)
                    prev_a1, prev_a2, prev_so = a1, a2, so
                    last_out_store[tt] = so
                    last_acc_reader[tt] = a1
                last_xnT_user = gu
        return sch.emit(bar, bar_val)


def build_program(phase_fns, tensors, host_inputs, outputs):
    nc = bass.Bass("TRN2", target_bir_lowering=False)
    ap = declare_dram(nc, tensors, host_inputs, outputs)
    es = ExitStack()
    with es:
        bar = es.enter_context(nc.semaphore("phbar"))
        v = 0
        for fn in phase_fns:
            v = fn(nc, ap, bar, v)
    return nc


def aux_arrays():
    return {
        "aux_ident": np.eye(128, dtype=BF),
        "aux_ones": np.ones((1, 128), dtype=BF),
        "aux_mask": np.triu(np.full((128, 128), NEG, dtype=np.float32), 1),
    }


def aux_vec_arrays(inputs):
    cb = np.ascontiguousarray(
        inputs["L0_attn_conv_b"].astype(np.float32).reshape(32, 128).T)
    cw = np.ascontiguousarray(
        inputs["L0_attn_conv_w"].astype(np.float32).reshape(32, 128, 4)
        .transpose(1, 0, 2).reshape(128, 128))
    dtb = np.ascontiguousarray(
        inputs["L0_attn_dt_proj_b"].astype(np.float32).reshape(32, 128).T)
    dv = np.ascontiguousarray(
        inputs["L0_attn_D"].astype(np.float32).reshape(32, 128).T)
    return {"aux_convb": cb, "aux_convw": cw, "aux_dtb": dtb, "aux_dvec": dv}


def _c(names):
    return names + ["wb_" + n for n in names]


LAUNCHES = [
    ("ln0",
     [lambda nc, ap, b, v: ph_ln(nc, ap, "x0", "s_xn0", "ln0", bar=b, bar_val=v)],
     ["x0", "s_xn0"], ["x0"], ["s_xn0"]),
    ("ffn0",
     [lambda nc, ap, b, v: ph_cast(nc, ap, CAST_LISTS["ffn0"], "c0", bar=b, bar_val=v),
      lambda nc, ap, b, v: ph_ffn0(nc, ap, bar=b, bar_val=v)],
     _c(CAST_LISTS["ffn0"]) + ["x0", "s_xn0", "aux_ones", "aux_ident", "s_x1acc"],
     CAST_LISTS["ffn0"] + ["x0", "s_xn0", "aux_ones", "aux_ident"],
     ["s_x1acc"]),
    ("mamba_in",
     [lambda nc, ap, b, v: ph_cast(nc, ap, CAST_LISTS["mamba_in"], "c1", bar=b, bar_val=v),
      lambda nc, ap, b, v: ph_prep_mamba(nc, ap, bar=b, bar_val=v),
      lambda nc, ap, b, v: ph_mamba_in(nc, ap, bar=b, bar_val=v)],
     _c(CAST_LISTS["mamba_in"]) +
     ["L0_attn_x_proj_w", "L0_attn_A_log", "aux_convw", "aux_convb",
      "wb_xproj_pad", "s_Aneg", "s_xn0", "s_gz", "s_xcf", "s_xcb", "s_dtlo", "s_bc"],
     CAST_LISTS["mamba_in"] +
     ["L0_attn_x_proj_w", "L0_attn_A_log", "aux_convw", "aux_convb",
      "s_xn0"],
     ["s_gz", "s_xcf", "s_xcb", "s_dtlo", "s_bc", "s_Aneg"]),
    ("mamba_scan",
     [lambda nc, ap, b, v: ph_cast(nc, ap, CAST_LISTS["mamba_scan"], "c2", bar=b, bar_val=v),
      lambda nc, ap, b, v: ph_mamba_scan(nc, ap, bar=b, bar_val=v)],
     _c(CAST_LISTS["mamba_scan"]) +
     ["aux_dtb", "aux_dvec", "s_Aneg", "s_dtlo", "s_bc", "s_xcf",
      "s_gz", "s_yg"],
     CAST_LISTS["mamba_scan"] +
     ["aux_dtb", "aux_dvec", "s_Aneg", "s_dtlo", "s_bc", "s_xcf", "s_gz"],
     ["s_yg"]),
    ("mamba_out",
     [lambda nc, ap, b, v: ph_cast(nc, ap, CAST_LISTS["mamba_out"], "c3", bar=b, bar_val=v),
      lambda nc, ap, b, v: ph_mamba_out(nc, ap, bar=b, bar_val=v)],
     _c(CAST_LISTS["mamba_out"]) + ["s_yg", "s_x1acc", "s_x1"],
     CAST_LISTS["mamba_out"] + ["s_yg", "s_x1acc"],
     ["s_x1"]),
    ("ln1",
     [lambda nc, ap, b, v: ph_ln(nc, ap, "s_x1", "s_xn1", "ln1", bar=b, bar_val=v)],
     ["s_x1", "s_xn1"], ["s_x1"], ["s_xn1"]),
    ("attn_pre",
     [lambda nc, ap, b, v: ph_cast(nc, ap,
          ["L1_attn_Wuq", "L1_attn_Wuk", "L1_attn_Wdq", "L1_attn_Wuv",
           "L1_attn_Wo"], "c4", bar=b, bar_val=v),
      lambda nc, ap, b, v: ph_attn_pre(nc, ap, bar=b, bar_val=v)],
     _c(["L1_attn_Wuq", "L1_attn_Wuk", "L1_attn_Wdq", "L1_attn_Wuv",
         "L1_attn_Wo"]) + ["s_kfull", "s_vfull"],
     ["L1_attn_Wuq", "L1_attn_Wuk", "L1_attn_Wdq", "L1_attn_Wuv", "L1_attn_Wo"],
     ["s_kfull", "s_vfull"]),
    ("attn_proj",
     [lambda nc, ap, b, v: ph_cast(nc, ap, ["L1_attn_Wdkv", "L1_attn_Wdq"],
                                   "c5", bar=b, bar_val=v),
      lambda nc, ap, b, v: ph_attn_proj(nc, ap, bar=b, bar_val=v)],
     _c(["L1_attn_Wdkv", "L1_attn_Wdq"]) +
     ["s_xn1", "s_ckvf", "s_qd"],
     ["L1_attn_Wdkv", "L1_attn_Wdq", "s_xn1"],
     ["s_ckvf", "s_qd"]),
    ("attn_core",
     [lambda nc, ap, b, v: ph_cast(nc, ap, ["L1_attn_Wuq"], "c6", bar=b, bar_val=v),
      lambda nc, ap, b, v: ph_attn_core(nc, ap, bar=b, bar_val=v)],
     _c(["L1_attn_Wuq"]) +
     ["s_kfull", "s_vfull", "s_ckvf", "s_qd", "aux_mask", "s_p", "s_atn"],
     ["L1_attn_Wuq", "s_kfull", "s_vfull", "s_ckvf", "s_qd", "aux_mask"],
     ["s_atn"]),
    ("ffn1",
     [lambda nc, ap, b, v: ph_cast(nc, ap, CAST_LISTS["ffn1"], "c7", bar=b, bar_val=v),
      lambda nc, ap, b, v: ph_prep_gate(nc, ap, bar=b, bar_val=v),
      lambda nc, ap, b, v: ph_ffn1(nc, ap, bar=b, bar_val=v)],
     _c(CAST_LISTS["ffn1"]) +
     ["L1_ffn_gate_w", "wb_gate_wT", "aux_ident", "s_xn1", "s_x1", "s_atn", "out",
      "wt_sw1", "wt_sw2", "wt_sw3"],
     CAST_LISTS["ffn1"] +
     ["L1_ffn_gate_w", "aux_ident", "s_xn1", "s_x1", "s_atn"],
     ["out"]),
]


def run_launches(state, launches, core_ids=None):
    if core_ids is None:
        core_ids = list(range(8))
    for name, fns, tensors, reads, writes in launches:
        nc = build_program(fns, tensors, set(reads), set(writes))
        in_maps = [{n: state[c][n] for n in reads} for c in range(len(core_ids))]
        res = run_bass_kernel_spmd(nc, in_maps, core_ids=core_ids)
        for c in range(len(core_ids)):
            for w in writes:
                state[c][w] = np.asarray(res.results[c][w])
    return state


CAST_EARLY = ["L0_ffn_W_base", "L0_ffn_b_base", "L0_ffn_A_s", "L0_ffn_B_s",
              "L0_ffn_A_r", "L0_ffn_B_r", "L0_ffn_W_router", "L0_ffn_b_router",
              "L0_attn_in_proj_w", "L0_attn_dt_proj_w"]
CAST_LATE = ["L0_attn_out_proj_w",
             "L1_attn_Wdq", "L1_attn_Wuq", "L1_attn_Wdkv", "L1_attn_Wuk",
             "L1_attn_Wuv", "L1_attn_Wo", "L1_ffn_w1", "L1_ffn_w2",
             "L1_ffn_sw1", "L1_ffn_sw2", "L1_ffn_sw3"]
ALL_CAST = CAST_EARLY + CAST_LATE

FUSED_PHASES = [
    lambda nc, ap, b, v: ph_cast(nc, ap, CAST_EARLY, "cA", bar=b, bar_val=v),
    lambda nc, ap, b, v: ph_prep_mamba(nc, ap, bar=b, bar_val=v),
    lambda nc, ap, b, v: ph_prep_gate(nc, ap, bar=b, bar_val=v),
    lambda nc, ap, b, v: ph_ln(nc, ap, "x0", "s_xn0", "ln0", bar=b, bar_val=v),
    lambda nc, ap, b, v: ph_ffn0(nc, ap, bar=b, bar_val=v),
    lambda nc, ap, b, v: ph_mamba_in(nc, ap, bar=b, bar_val=v),
    lambda nc, ap, b, v: ph_mamba_scan(nc, ap, bar=b, bar_val=v, cast_late=True),
    lambda nc, ap, b, v: ph_mamba_out(nc, ap, bar=b, bar_val=v),
    lambda nc, ap, b, v: ph_ln(nc, ap, "s_x1", "s_xn1", "ln1", bar=b, bar_val=v),
    lambda nc, ap, b, v: ph_attn_pre(nc, ap, bar=b, bar_val=v),
    lambda nc, ap, b, v: ph_attn_proj(nc, ap, bar=b, bar_val=v),
    lambda nc, ap, b, v: ph_attn_core(nc, ap, bar=b, bar_val=v),
    lambda nc, ap, b, v: ph_ffn1(nc, ap, bar=b, bar_val=v),
]

FUSED_READS = (list(WEIGHTS_F32) + ["x0", "aux_ident", "aux_ones", "aux_mask",
                                    "aux_convb", "aux_convw", "aux_dtb",
                                    "aux_dvec"])
FUSED_TENSORS = (FUSED_READS + ["out"] + list(SCRATCH) +
                 ["wb_" + n for n in ALL_CAST])


def make_state(inputs, n_cores=8):
    aux = aux_arrays()
    auxv = aux_vec_arrays(inputs)
    x = np.ascontiguousarray(inputs["L0_attn_hidden_states"])
    state = []
    for c in range(n_cores):
        d = dict(inputs)
        d.pop("L0_attn_hidden_states")
        d["x0"] = np.ascontiguousarray(x[c*B_LOC:(c+1)*B_LOC])
        d.update(aux)
        d.update(auxv)
        state.append(d)
    return state


def kernel(**inputs):
    n_cores = 8
    state = make_state(inputs, n_cores)
    nc = build_program(FUSED_PHASES, FUSED_TENSORS, set(FUSED_READS), {"out"})
    in_maps = [{n: state[c][n] for n in FUSED_READS} for c in range(n_cores)]
    res = run_bass_kernel_spmd(nc, in_maps, core_ids=list(range(n_cores)))
    return np.concatenate([np.asarray(res.results[c]["out"])
                           for c in range(n_cores)], axis=0)


def kernel_unfused(**inputs):
    n_cores = 8
    state = make_state(inputs, n_cores)
    run_launches(state, LAUNCHES)
    return np.concatenate([state[c]["out"] for c in range(n_cores)], axis=0)
